# version 03
# baseline (speedup 1.0000x reference)
import numpy as np
import concourse.bass as bass
import concourse.mybir as mybir
from concourse.bass_utils import run_bass_kernel_spmd

F32 = mybir.dt.float32
BF16 = mybir.dt.bfloat16
AF = mybir.ActivationFunctionType
ALU = mybir.AluOpType
AX = mybir.AxisListType

B, L, D = 4, 2048, 4096
TOK = 1024
DC = 32
HID, NHC = 11008, 86
OC = 32
HEADS, KVH = 32, 8
R = 16
FFN_SPLITS = [(0, 22), (22, 44), (44, 65), (65, 86)]
EPS_LN = 1e-5
EPS_ATTN = 1e-10


class SemTok:
    def __init__(self, nc, name):
        self.h = nc.alloc_semaphore(name)
        self.n = 0


class Prog:
    ENGS = ("T", "V", "S", "G", "Y")

    def __init__(self):
        self.ops = {e: [] for e in self.ENGS}
        self.last_wait = {e: {} for e in self.ENGS}

    def wait(self, eng, tok):
        if tok is None:
            return
        sem, val = tok
        lw = self.last_wait[eng]
        if lw.get(id(sem), -1) >= val:
            return
        lw[id(sem)] = val
        self.ops[eng].append(lambda E, s=sem.h, v=val: E.wait_ge(s, v))

    def waits(self, eng, toks):
        for t in toks:
            self.wait(eng, t)

    def op(self, eng, name, *args, inc=None, **kw):
        tok = None
        if inc is not None:
            inc.n += 1
            tok = (inc, inc.n)

        def f(E, name=name, args=args, kw=kw, tok=tok):
            ins = getattr(E, name)(*args, **kw)
            if tok is not None:
                ins.then_inc(tok[0].h, 1)
        self.ops[eng].append(f)
        return tok

    def dma(self, eng, sem, out, in_, **kw):
        sem.n += 16
        tok = (sem, sem.n)

        def f(E, out=out, in_=in_, kw=kw, tok=tok):
            E.dma_start(out=out, in_=in_, **kw).then_inc(tok[0].h, 16)
        self.ops[eng].append(f)
        return tok


DBG = {}

def build_nc(upto=99):
    dbg_sp = DBG.get("sp", 3)
    dbg_eps = DBG.get("eps", KVH)
    dbg_bias = DBG.get("bias", True)
    dbg_epi = DBG.get("epi", True)
    dbg_m = DBG.get("m", True)
    nc = bass.Bass("TRN2", target_bir_lowering=False, debug=False)

    x_my = nc.dram_tensor("x_my", [TOK, D], F32, kind="ExternalInput").ap()
    x_ot = nc.dram_tensor("x_ot", [TOK, D], F32, kind="ExternalInput").ap()
    xT = nc.dram_tensor("xT", [D, TOK], F32, kind="ExternalInput").ap()
    wkv_t = nc.dram_tensor("wkv_t", [KVH, 128, DC * 256], F32, kind="ExternalInput").ap()
    wq_t = nc.dram_tensor("wq_t", [HEADS, 128, DC * 128], F32, kind="ExternalInput").ap()
    wo_t = nc.dram_tensor("wo_t", [OC, 128, DC * 128], F32, kind="ExternalInput").ap()
    w13_t = nc.dram_tensor("w13_t", [NHC, 2, 128, DC * 128], F32, kind="ExternalInput").ap()
    w2_t = nc.dram_tensor("w2_t", [OC, 128, NHC * 128], F32, kind="ExternalInput").ap()
    b13_t = nc.dram_tensor("b13_t", [NHC, R, 256], F32, kind="ExternalInput").ap()
    b2_t = nc.dram_tensor("b2_t", [OC, R, 128], F32, kind="ExternalInput").ap()
    a1_t = nc.dram_tensor("a1_t", [128, DC * R], F32, kind="ExternalInput").ap()
    a3_t = nc.dram_tensor("a3_t", [128, DC * R], F32, kind="ExternalInput").ap()
    a2_t = nc.dram_tensor("a2_t", [128, NHC * R], F32, kind="ExternalInput").ap()
    bq_p = nc.dram_tensor("bq_p", [128, HEADS], F32, kind="ExternalInput").ap()
    bo_p = nc.dram_tensor("bo_p", [128, OC], F32, kind="ExternalInput").ap()
    bkv_p = nc.dram_tensor("bkv_p", [1, KVH * 256], F32, kind="ExternalInput").ap()
    out = nc.dram_tensor("out", [D, TOK], F32, kind="ExternalOutput").ap()

    xn_scr = nc.dram_tensor("xn_scr", [L, D], BF16, kind="Internal").ap()
    acc_scr = [nc.dram_tensor(f"acc{i}", [D, TOK], F32, kind="Internal").ap()
               for i in range(2)]

    xnT_my = nc.alloc_sbuf_tensor("xnT_my", [128, DC * TOK], BF16)
    buf_B = nc.alloc_sbuf_tensor("buf_B", [128, DC * TOK], BF16)
    wring = nc.alloc_sbuf_tensor("wring", [128, 16384], BF16)
    fstage = nc.alloc_sbuf_tensor("fstage", [128, 4096], F32)
    estage = nc.alloc_sbuf_tensor("estage", [128, 2048], F32)
    xnstage = nc.alloc_sbuf_tensor("xnstage", [128, 2048], BF16)
    qbf = nc.alloc_sbuf_tensor("qbf", [128, 1024], BF16)
    M_bf = nc.alloc_sbuf_tensor("M_bf", [128, KVH * 128], BF16)
    bkv_r = nc.alloc_sbuf_tensor("bkv_r", [1, 512], BF16)
    b13r = nc.alloc_sbuf_tensor("b13r", [R, 512], BF16)
    b2r = nc.alloc_sbuf_tensor("b2r", [R, 256], BF16)
    a1_sb = nc.alloc_sbuf_tensor("a1_sb", [128, DC * R], BF16)
    a3_sb = nc.alloc_sbuf_tensor("a3_sb", [128, DC * R], BF16)
    a2r = nc.alloc_sbuf_tensor("a2r", [128, 2 * 22 * R], BF16)
    xa1_bf = nc.alloc_sbuf_tensor("xa1_bf", [R, TOK], BF16)
    xa2s_bf = xa1_bf
    xa3_bf = nc.alloc_sbuf_tensor("xa3_bf", [R, TOK], BF16)
    xa2_acc = nc.alloc_sbuf_tensor("xa2_acc", [R, TOK], BF16)
    bq_sb = nc.alloc_sbuf_tensor("bq_sb", [128, HEADS], F32)
    bq1_sb = nc.alloc_sbuf_tensor("bq1_sb", [128, HEADS], F32)
    bo_sb = nc.alloc_sbuf_tensor("bo_sb", [128, OC], F32)
    stats = nc.alloc_sbuf_tensor("stats", [128, 16], F32)
    nrm = nc.alloc_sbuf_tensor("nrm", [128, 16], F32)
    ones_r = nc.alloc_sbuf_tensor("ones_r", [1, 128], BF16)
    ones_c = nc.alloc_sbuf_tensor("ones_c", [128, 2], BF16)

    assert nc.sbuf_bytes_remaining >= 0, nc.sbuf_bytes_remaining

    ps = [nc.alloc_psum_tensor(f"ps{i}", [128, 512], F32) for i in range(8)]

    P = Prog()
    sT = SemTok(nc, "sT")
    sV = SemTok(nc, "sV")
    sS = SemTok(nc, "sS")
    sIni = SemTok(nc, "sIni")
    sWs = [SemTok(nc, f"sW{i}") for i in range(4)]
    sFs = [SemTok(nc, f"sF{i}") for i in range(2)]
    sTr = SemTok(nc, "sTr")
    sScr = SemTok(nc, "sScr")
    sAc = [SemTok(nc, f"sAc{i}") for i in range(2)]
    sB13 = [SemTok(nc, f"sB13{i}") for i in range(2)]
    sB2 = [SemTok(nc, f"sB2{i}") for i in range(2)]
    sKb = [SemTok(nc, f"sKb{i}") for i in range(2)]
    sA2 = [SemTok(nc, f"sA2{i}") for i in range(2)]

    MM = dict(skip_group_check=True)
    bank_free = [[] for _ in range(8)]

    def tnow():
        return (sT, sT.n)

    def vnow():
        return (sV, sV.n)

    def snow():
        return (sS, sS.n)

    for dst, src in ((bq_sb[:], bq_p), (bo_sb[:], bo_p),
                     (a1_sb[:], a1_t), (a3_sb[:], a3_t)):
        P.dma("G", sIni, dst, src)
    tok_init = (sIni, sIni.n)
    P.op("V", "memset", ones_r[:], 1.0)
    P.op("V", "memset", ones_c[:], 1.0)
    P.wait("V", tok_init)
    tok_const = P.op("V", "tensor_scalar_add", bq1_sb[:], bq_sb[:], 1.0, inc=sV)

    xslot_free = [None, None]
    dump_free = None
    xn_dma_tok = None
    for t in range(16):
        src = x_my if t < 8 else x_ot
        rows = (t % 8) * 128
        htoks = [None, None]
        for half in range(2):
            P.wait("Y", xslot_free[half])
            htoks[half] = P.dma("Y", sFs[half],
                                fstage[:, half * 2048:(half + 1) * 2048],
                                src[rows:rows + 128, half * 2048:(half + 1) * 2048])
        for half in range(2):
            xh = fstage[:, half * 2048:(half + 1) * 2048]
            P.wait("V", htoks[half])
            P.op("V", "reduce_sum", stats[:, half:half + 1], xh, axis=AX.X)
            P.waits("S", [htoks[half], dump_free])
            tq = P.op("S", "activation", estage[:], xh, AF.Square, inc=sS)
            P.wait("V", tq)
            dump_free = P.op("V", "reduce_sum", stats[:, 2 + half:3 + half],
                             estage[:], axis=AX.X, inc=sV)
        P.op("V", "drain")
        P.op("V", "tensor_add", stats[:, 4:5], stats[:, 0:1], stats[:, 1:2])
        P.op("V", "tensor_add", stats[:, 5:6], stats[:, 2:3], stats[:, 3:4])
        P.op("V", "drain")
        P.op("V", "tensor_scalar_mul", stats[:, 6:7], stats[:, 4:5], 1.0 / D)
        P.op("V", "drain")
        P.op("V", "tensor_mul", stats[:, 7:8], stats[:, 6:7], stats[:, 6:7])
        P.op("V", "drain")
        P.op("V", "scalar_tensor_tensor", stats[:, 8:9], stats[:, 5:6], 1.0 / D,
             stats[:, 7:8], ALU.mult, ALU.subtract)
        P.op("V", "drain")
        P.op("V", "tensor_scalar_add", stats[:, 9:10], stats[:, 8:9], EPS_LN)
        P.op("V", "drain")
        tr = P.op("V", "reciprocal", stats[:, 10:11], stats[:, 9:10], inc=sV)
        P.wait("S", tr)
        trs = P.op("S", "activation", stats[:, 11:12], stats[:, 10:11], AF.Sqrt,
                   inc=sS)
        P.wait("V", trs)
        tmb = P.op("V", "tensor_scalar", stats[:, 12:13], stats[:, 6:7],
                   stats[:, 11:12], -1.0, ALU.mult, ALU.mult, inc=sV)
        P.op("S", "drain")
        P.wait("S", tmb)
        for half in range(2):
            xh = fstage[:, half * 2048:(half + 1) * 2048]
            P.wait("S", xn_dma_tok)
            tn = P.op("S", "activation", xnstage[:], xh, AF.Identity,
                      bias=stats[:, 12:13], scale=stats[:, 11:12], inc=sS)
            xslot_free[half] = tn
            P.wait("Y", tn)
            xn_dma_tok = P.dma("Y", sScr,
                               xn_scr[t * 128:(t + 1) * 128,
                                      half * 2048:(half + 1) * 2048], xnstage[:])

    def debug_finish(taps=(), dram_taps=()):
        P.wait("V", (sS, sS.n))
        P.wait("V", (sT, sT.n))
        P.op("V", "memset", fstage[:, 2048:3072], 0.0, inc=sV)
        P.wait("Y", (sV, sV.n))
        for oc_ in range(OC):
            P.dma("Y", sAc[0], out[oc_ * 128:(oc_ + 1) * 128, :],
                  fstage[:, 2048:3072])
        P.wait("Y", (sAc[0], sAc[0].n))
        for ap, row in taps:
            ncols = ap.shape[-1]
            pdim = ap.shape[0]
            tv_ = P.op("V", "tensor_copy", fstage[0:pdim, 2048:2048 + ncols], ap,
                       inc=sV)
            P.wait("Y", tv_)
            P.dma("Y", sAc[0], out[row:row + pdim, 0:ncols],
                  fstage[0:pdim, 2048:2048 + ncols])
            P.wait("Y", (sAc[0], sAc[0].n))
        for ap, row in dram_taps:
            ncols = ap.shape[-1]
            P.wait("G", (sAc[0], sAc[0].n))
            tg_ = P.dma("G", sIni, fstage[:, 2048:2048 + ncols], ap)
            P.wait("Y", tg_)
            P.dma("Y", sAc[0], out[row:row + 128, 0:ncols],
                  fstage[:, 2048:2048 + ncols])
            P.wait("Y", (sAc[0], sAc[0].n))
        emit(nc, P)
        return nc

    if upto <= 0:
        return debug_finish()

    P.wait("Y", (sScr, sScr.n))
    for dc in range(DC):
        P.dma("Y", sTr, xnT_my[:, dc * TOK:(dc + 1) * TOK],
              xn_scr[0:TOK, dc * 128:(dc + 1) * 128], transpose=True)
    tok_xnT = (sTr, sTr.n)

    ring_free = [None, None, None, None]
    ring_i = [0]

    def ring_load(src_ap):
        s = ring_i[0] % 4
        ring_i[0] += 1
        P.wait("G", ring_free[s])
        ncols = src_ap.shape[-1]
        dst = wring[:, s * 4096:s * 4096 + ncols]
        tok = P.dma("G", sWs[s], dst, src_ap)
        return s, dst, tok

    def ring_mark(s):
        ring_free[s] = tnow()

    SP_TILES = [list(range(8)), [8, 9, 10, 11], [12, 13, 14, 15]]
    bkv_free = [None, None]
    est_free = [[snow()], [snow()]]
    kvst_free = [[], []]
    it = 0
    tok_oq = None
    last_sp = dbg_sp - 1
    for sp in range(dbg_sp):
        if sp > 0:
            P.wait("Y", tnow())
            rows0 = TOK + (sp - 1) * 512
            for dc in range(DC):
                P.dma("Y", sTr, buf_B[:, dc * 512:(dc + 1) * 512],
                      xn_scr[rows0:rows0 + 512, dc * 128:(dc + 1) * 128],
                      transpose=True)
            tok_oq = (sTr, sTr.n)
        for ep in range(dbg_eps):
            sA, wA, tok_wA = ring_load(wkv_t[ep][:, 0:4096])
            sB, wB, tok_wB = ring_load(wkv_t[ep][:, 4096:8192])
            kpar = (sp * KVH + ep) % 2
            P.wait("G", bkv_free[kpar])
            tok_bkv = P.dma("G", sKb[kpar], bkv_r[0:1, kpar * 256:(kpar + 1) * 256],
                            bkv_p[0:1, ep * 256:(ep + 1) * 256])
            for t in SP_TILES[sp]:
                pg = it % 2
                kb = ps[pg]
                P.waits("T", [tok_wA, tok_wB, tok_bkv, tok_init] + bank_free[pg])
                if sp == 0:
                    P.wait("T", tok_xnT)
                    lhs, cw, toff = xnT_my, TOK, t * 128
                else:
                    P.wait("T", tok_oq)
                    lhs, cw, toff = buf_B, 512, (t - 8 - (sp - 1) * 4) * 128
                P.wait("T", tok_const)
                for dc in range(DC):
                    wsrc = wA if dc < 16 else wB
                    wcol = (dc % 16) * 256
                    fin_kv = (dc == 31) and not dbg_bias
                    tok_kv = P.op("T", "matmul", kb[:, 0:256],
                                  lhs[:, dc * cw + toff:dc * cw + toff + 128],
                                  wsrc[:, wcol:wcol + 256], start=(dc == 0),
                                  stop=fin_kv, inc=(sT if fin_kv else None), **MM)
                if dbg_bias:
                    tok_kv = P.op("T", "matmul", kb[:, 0:256], ones_r[0:1, :],
                                  bkv_r[0:1, kpar * 256:(kpar + 1) * 256],
                                  start=False, stop=True, inc=sT, **MM)
                if not dbg_epi:
                    bank_free[pg] = []
                    it += 1
                    tok_nm = tok_kv
                    continue
                eA = estage[:, pg * 512:pg * 512 + 128]
                eB = estage[:, pg * 512 + 128:pg * 512 + 256]
                eC = estage[:, pg * 512 + 256:pg * 512 + 384]
                kv = qbf[:, pg * 512:pg * 512 + 256]
                P.waits("V", [tok_kv] + est_free[pg] + kvst_free[pg])
                tm = P.op("V", "tensor_scalar_min", eA, kb[:, 0:128], 0.0, inc=sV)
                tvc = P.op("V", "tensor_copy", kv[:, 128:256], kb[:, 128:256],
                           inc=sV)
                P.wait("S", tm)
                te = P.op("S", "activation", eB, eA, AF.Exp, inc=sS)
                tu = P.op("V", "tensor_scalar_add", eC, kb[:, 0:128], 1.0, inc=sV)
                P.op("V", "drain")
                P.wait("V", te)
                tkb = P.op("V", "tensor_tensor", kv[:, 0:128], eC, eB, ALU.max,
                           inc=sV)
                est_free[pg] = [tkb, te]
                bank_free[pg] = [tu, tvc]
                if not dbg_m:
                    kvst_free[pg] = [tkb]
                    it += 1
                    tok_nm = tok_kv
                    continue
                P.waits("T", [tkb, tvc])
                mbank = ps[2 + ep // 4]
                mcol = (ep % 4) * 128
                stM = (sp == 0 and t == 0 and ep % 4 == 0)
                stN = (sp == 0 and t == 0 and ep == 0)
                spmm = (sp == last_sp and t == SP_TILES[sp][-1])
                P.op("T", "matmul", mbank[:, mcol:mcol + 128], kv[:, 0:128],
                     kv[:, 128:256], start=stM, stop=spmm, **MM)
                tok_nm = P.op("T", "matmul", ps[4][:, 4 * ep:4 * ep + 2],
                              kv[:, 0:128], ones_c[:], start=stN, stop=spmm,
                              inc=sT, **MM)
                kvst_free[pg] = [tok_nm]
                it += 1
            ring_mark(sA)
            ring_mark(sB)
            bkv_free[kpar] = tnow()
    tok_M_last = tok_nm
    P.wait("V", tok_M_last)
    P.op("V", "tensor_copy", M_bf[:, 0:512], ps[2][:, 0:512])
    P.op("V", "tensor_copy", M_bf[:, 512:1024], ps[3][:, 0:512])
    for e_ in range(KVH):
        P.op("V", "tensor_scalar_add", nrm[:, 8 + e_:9 + e_],
             ps[4][:, 4 * e_:4 * e_ + 1], EPS_ATTN)
    P.op("V", "drain")
    tok_M = P.op("V", "reciprocal", nrm[:, 0:8], nrm[:, 8:16], inc=sV)
    for i in (2, 3, 4):
        bank_free[i] = [tok_M]

    if upto <= 1:
        return debug_finish(taps=[
            (xnT_my[:, 0:1024], 0),
            (xnT_my[:, 5 * 1024:6 * 1024], 128),
            (M_bf[:, 0:1024], 256),
            (nrm[:, 0:16], 384),
            (qbf[:, 0:1024], 512),
            (stats[:, 0:16], 768),
        ], dram_taps=[
            (xn_scr[0:128, 0:1024], 640),
            (xn_scr[1920:2048, 0:1024], 896),
        ])

    attnT = buf_B
    est2_free = [[snow()], [snow()]]
    qbf_free = [kvst_free[0] + [snow()], kvst_free[1] + [snow()]]
    tok_attnT = None
    for h in range(HEADS):
        s, wdst, tok_w = ring_load(wq_t[h])
        b0, b1 = (h % 2) * 2, (h % 2) * 2 + 1
        P.waits("T", [tok_w, tok_xnT] + bank_free[b0] + bank_free[b1])
        tok_q = None
        for dc in range(DC):
            st, fin = dc == 0, dc == 31
            P.op("T", "matmul", ps[b0][:], wdst[:, dc * 128:(dc + 1) * 128],
                 xnT_my[:, dc * TOK:dc * TOK + 512], start=st, stop=fin, **MM)
            tk = P.op("T", "matmul", ps[b1][:], wdst[:, dc * 128:(dc + 1) * 128],
                      xnT_my[:, dc * TOK + 512:(dc + 1) * TOK], start=st,
                      stop=fin, inc=(sT if fin else None), **MM)
            if fin:
                tok_q = tk
        ring_mark(s)
        for half in range(2):
            qb = ps[b0 + half]
            pg = (h * 2 + half) % 2
            eA = estage[:, pg * 1024:pg * 1024 + 512]
            eB = estage[:, pg * 1024 + 512:(pg + 1) * 1024]
            qslot = qbf[:, pg * 512:(pg + 1) * 512]
            P.waits("V", [tok_q, tok_const] + est2_free[pg])
            tm = P.op("V", "tensor_scalar", eA, qb[:], bq_sb[:, h:h + 1], 0.0,
                      ALU.add, ALU.min, inc=sV)
            P.wait("S", tm)
            te = P.op("S", "activation", eB, eA, AF.Exp, inc=sS)
            P.wait("V", te)
            tu = P.op("V", "tensor_scalar_add", eA, qb[:], bq1_sb[:, h:h + 1],
                      inc=sV)
            P.op("V", "drain")
            P.waits("V", qbf_free[pg])
            tqb = P.op("V", "tensor_tensor", qslot, eA, eB, ALU.max, inc=sV)
            est2_free[pg] = [tqb, te]
            bank_free[b0 + half] = [tu]
            ab = 4 + (h % 2) * 2 + half
            kvh = h // 4
            P.waits("T", [tqb, tok_M] + bank_free[ab])
            tok_a = P.op("T", "matmul", ps[ab][:],
                         M_bf[:, kvh * 128:(kvh + 1) * 128], qslot,
                         start=True, stop=True, inc=sT, **MM)
            qbf_free[pg] = [tok_a]
            P.wait("V", tok_a)
            tok_attnT = P.op("V", "tensor_scalar_mul",
                             attnT[:, h * TOK + half * 512:
                                   h * TOK + (half + 1) * 512],
                             ps[ab][:], nrm[:, kvh:kvh + 1], inc=sV)
            bank_free[ab] = [tok_attnT]

    if upto <= 2:
        return debug_finish()

    xT_free = [[snow(), vnow()], [snow(), vnow()]]
    out_free = [[], []]
    for oc in range(OC):
        s, wdst, tok_w = ring_load(wo_t[oc])
        pgx = oc % 2
        P.waits("Y", xT_free[pgx])
        tok_xT = P.dma("Y", sFs[pgx], fstage[:, pgx * 1024:(pgx + 1) * 1024],
                       xT[oc * 128:(oc + 1) * 128, :])
        b0, b1 = (oc % 2) * 2, (oc % 2) * 2 + 1
        P.waits("T", [tok_w, tok_attnT] + bank_free[b0] + bank_free[b1])
        tok_wo = None
        for fc in range(DC):
            st, fin = fc == 0, fc == 31
            P.op("T", "matmul", ps[b0][:], wdst[:, fc * 128:(fc + 1) * 128],
                 attnT[:, fc * TOK:fc * TOK + 512], start=st, stop=fin, **MM)
            tk = P.op("T", "matmul", ps[b1][:], wdst[:, fc * 128:(fc + 1) * 128],
                      attnT[:, fc * TOK + 512:(fc + 1) * TOK], start=st,
                      stop=fin, inc=(sT if fin else None), **MM)
            if fin:
                tok_wo = tk
        ring_mark(s)
        P.waits("V", [tok_wo, tok_xT] + out_free[pgx])
        tv = None
        for half in range(2):
            tv = P.op("V", "scalar_tensor_tensor",
                      fstage[:, 2048 + pgx * 1024 + half * 512:
                             2048 + pgx * 1024 + (half + 1) * 512],
                      ps[b0 + half][:], bo_sb[:, oc:oc + 1],
                      fstage[:, pgx * 1024 + half * 512:
                             pgx * 1024 + (half + 1) * 512],
                      ALU.add, ALU.add, inc=sV)
        xT_free[pgx] = [tv]
        bank_free[b0] = [tv]
        bank_free[b1] = [tv]
        P.wait("Y", tv)
        wt = P.dma("Y", sAc[pgx], acc_scr[0][oc * 128:(oc + 1) * 128, :],
                   fstage[:, 2048 + pgx * 1024:2048 + (pgx + 1) * 1024])
        out_free[pgx] = [wt]
    tok_accs_done = [(sAc[0], sAc[0].n), (sAc[1], sAc[1].n)]

    if upto <= 3:
        return debug_finish()

    for (asb, xout) in ((a1_sb, xa1_bf), (a3_sb, xa3_bf)):
        P.waits("T", [tok_init] + bank_free[4] + bank_free[5])
        tok_xa = None
        for dc in range(DC):
            st, fin = dc == 0, dc == 31
            P.op("T", "matmul", ps[4][0:R, :], asb[:, dc * R:(dc + 1) * R],
                 xnT_my[:, dc * TOK:dc * TOK + 512], start=st, stop=fin, **MM)
            tk = P.op("T", "matmul", ps[5][0:R, :], asb[:, dc * R:(dc + 1) * R],
                      xnT_my[:, dc * TOK + 512:(dc + 1) * TOK], start=st,
                      stop=fin, inc=(sT if fin else None), **MM)
            if fin:
                tok_xa = tk
        P.wait("V", tok_xa)
        P.op("V", "tensor_scalar_mul", xout[:, 0:512], ps[4][0:R, :], 2.0)
        tva = P.op("V", "tensor_scalar_mul", xout[:, 512:1024], ps[5][0:R, :],
                   2.0, inc=sV)
        bank_free[4] = [tva]
        bank_free[5] = [tva]
    tok_xa13 = tva

    if upto <= 4:
        return debug_finish()

    g_sb = buf_B
    b13_free = [None, None]
    b2_free = [None, None]
    a2r_free = [None, None]
    sig_free = [[vnow()], [vnow()]]
    acc_rd_free = [xT_free[0], xT_free[1]]
    tok_g_banks = {}
    tok_xa2s = None
    for p, (h0, h1) in enumerate(FFN_SPLITS):
        n = h1 - h0
        pga = p % 2
        P.wait("G", a2r_free[pga])
        tok_a2 = P.dma("G", sA2[pga], a2r[:, pga * 22 * R:pga * 22 * R + n * R],
                       a2_t[:, h0 * R:h1 * R])
        for hc in range(h0, h1):
            s1, w1d, tok_w1 = ring_load(w13_t[hc, 0])
            s3, w3d, tok_w3 = ring_load(w13_t[hc, 1])
            pgb = hc % 2
            P.wait("G", b13_free[pgb])
            tok_b13 = P.dma("G", sB13[pgb], b13r[:, pgb * 256:(pgb + 1) * 256],
                            b13_t[hc])
            bb = [pgb * 4, pgb * 4 + 1, pgb * 4 + 2, pgb * 4 + 3]
            P.waits("T", [tok_w1, tok_w3, tok_b13, tok_xa13]
                    + bank_free[bb[0]] + bank_free[bb[1]]
                    + bank_free[bb[2]] + bank_free[bb[3]])
            for dc in range(DC):
                st = dc == 0
                P.op("T", "matmul", ps[bb[0]][:], w1d[:, dc * 128:(dc + 1) * 128],
                     xnT_my[:, dc * TOK:dc * TOK + 512], start=st, stop=False, **MM)
                P.op("T", "matmul", ps[bb[1]][:], w1d[:, dc * 128:(dc + 1) * 128],
                     xnT_my[:, dc * TOK + 512:(dc + 1) * TOK], start=st,
                     stop=False, **MM)
                P.op("T", "matmul", ps[bb[2]][:], w3d[:, dc * 128:(dc + 1) * 128],
                     xnT_my[:, dc * TOK:dc * TOK + 512], start=st, stop=False, **MM)
                P.op("T", "matmul", ps[bb[3]][:], w3d[:, dc * 128:(dc + 1) * 128],
                     xnT_my[:, dc * TOK + 512:(dc + 1) * TOK], start=st,
                     stop=False, **MM)
            P.op("T", "matmul", ps[bb[0]][:], b13r[0:R, pgb * 256:pgb * 256 + 128],
                 xa1_bf[:, 0:512], start=False, stop=True, **MM)
            P.op("T", "matmul", ps[bb[1]][:], b13r[0:R, pgb * 256:pgb * 256 + 128],
                 xa1_bf[:, 512:1024], start=False, stop=True, **MM)
            P.op("T", "matmul", ps[bb[2]][:],
                 b13r[0:R, pgb * 256 + 128:pgb * 256 + 256],
                 xa3_bf[:, 0:512], start=False, stop=True, **MM)
            tok_h = P.op("T", "matmul", ps[bb[3]][:],
                         b13r[0:R, pgb * 256 + 128:pgb * 256 + 256],
                         xa3_bf[:, 512:1024], start=False, stop=True, inc=sT, **MM)
            b13_free[pgb] = tok_h
            ring_mark(s1)
            ring_mark(s3)
            P.waits("S", [tok_h] + sig_free[pgb])
            P.op("S", "activation", estage[:, pgb * 1024:pgb * 1024 + 512],
                 ps[bb[0]][:], AF.Silu)
            tok_sig = P.op("S", "activation",
                           estage[:, pgb * 1024 + 512:(pgb + 1) * 1024],
                           ps[bb[1]][:], AF.Silu, inc=sS)
            P.wait("V", tok_sig)
            P.op("V", "tensor_mul", g_sb[:, (hc - h0) * TOK:(hc - h0) * TOK + 512],
                 estage[:, pgb * 1024:pgb * 1024 + 512], ps[bb[2]][:])
            tok_g = P.op("V", "tensor_mul",
                         g_sb[:, (hc - h0) * TOK + 512:(hc - h0 + 1) * TOK],
                         estage[:, pgb * 1024 + 512:(pgb + 1) * 1024],
                         ps[bb[3]][:], inc=sV)
            sig_free[pgb] = [tok_g]
            bank_free[bb[0]] = [tok_sig]
            bank_free[bb[1]] = [tok_sig]
            bank_free[bb[2]] = [tok_g]
            bank_free[bb[3]] = [tok_g]
            tok_g_banks[hc] = tok_g
        P.waits("T", [tok_a2] + bank_free[0] + bank_free[1])
        tok_xa2 = None
        for hc in range(h0, h1):
            P.wait("T", tok_g_banks[hc])
            st, fin = hc == h0, hc == h1 - 1
            acol = pga * 22 * R + (hc - h0) * R
            P.op("T", "matmul", ps[0][0:R, :], a2r[:, acol:acol + R],
                 g_sb[:, (hc - h0) * TOK:(hc - h0) * TOK + 512],
                 start=st, stop=fin, **MM)
            tk = P.op("T", "matmul", ps[1][0:R, :], a2r[:, acol:acol + R],
                      g_sb[:, (hc - h0) * TOK + 512:(hc - h0 + 1) * TOK],
                      start=st, stop=fin, inc=(sT if fin else None), **MM)
            if fin:
                tok_xa2 = tk
        a2r_free[pga] = tok_xa2
        P.wait("V", tok_xa2)
        if p == 0:
            P.op("V", "tensor_copy", xa2_acc[:, 0:512], ps[0][0:R, :])
            tx2 = P.op("V", "tensor_copy", xa2_acc[:, 512:1024], ps[1][0:R, :],
                       inc=sV)
        else:
            P.op("V", "tensor_add", xa2_acc[:, 0:512], xa2_acc[:, 0:512],
                 ps[0][0:R, :])
            tx2 = P.op("V", "tensor_add", xa2_acc[:, 512:1024],
                       xa2_acc[:, 512:1024], ps[1][0:R, :], inc=sV)
        bank_free[0] = [tx2]
        bank_free[1] = [tx2]
        if p == 3:
            P.op("V", "drain")
            P.op("V", "tensor_scalar_mul", xa2s_bf[:, 0:512], xa2_acc[:, 0:512],
                 2.0)
            tok_xa2s = P.op("V", "tensor_scalar_mul", xa2s_bf[:, 512:1024],
                            xa2_acc[:, 512:1024], 2.0, inc=sV)
        asrc = acc_scr[0] if p % 2 == 0 else acc_scr[1]
        adst = acc_scr[1] if p % 2 == 0 else acc_scr[0]
        P.waits("Y", tok_accs_done)
        acc_wr0 = acc_wr1 = None
        for oc in range(OC):
            s, wdst, tok_w = ring_load(w2_t[oc][:, h0 * 128:h1 * 128])
            pgx = oc % 2
            P.waits("Y", acc_rd_free[pgx])
            tok_ar = P.dma("Y", sFs[pgx], fstage[:, pgx * 1024:(pgx + 1) * 1024],
                           asrc[oc * 128:(oc + 1) * 128, :])
            b0, b1 = 2 + (oc % 2) * 2, 3 + (oc % 2) * 2
            P.waits("T", [tok_w] + bank_free[b0] + bank_free[b1])
            tok_w2 = None
            for hc in range(h0, h1):
                st = hc == h0
                fin = (hc == h1 - 1) and p != 3
                P.op("T", "matmul", ps[b0][:],
                     wdst[:, (hc - h0) * 128:(hc - h0 + 1) * 128],
                     g_sb[:, (hc - h0) * TOK:(hc - h0) * TOK + 512],
                     start=st, stop=fin, **MM)
                tk = P.op("T", "matmul", ps[b1][:],
                          wdst[:, (hc - h0) * 128:(hc - h0 + 1) * 128],
                          g_sb[:, (hc - h0) * TOK + 512:(hc - h0 + 1) * TOK],
                          start=st, stop=fin, inc=(sT if fin else None), **MM)
                if fin:
                    tok_w2 = tk
            if p == 3:
                pgc = oc % 2
                P.wait("G", b2_free[pgc])
                tok_b2 = P.dma("G", sB2[pgc], b2r[:, pgc * 128:(pgc + 1) * 128],
                               b2_t[oc])
                P.waits("T", [tok_b2, tok_xa2s])
                P.op("T", "matmul", ps[b0][:], b2r[0:R, pgc * 128:(pgc + 1) * 128],
                     xa2s_bf[:, 0:512], start=False, stop=True, **MM)
                tok_w2 = P.op("T", "matmul", ps[b1][:],
                              b2r[0:R, pgc * 128:(pgc + 1) * 128],
                              xa2s_bf[:, 512:1024], start=False, stop=True,
                              inc=sT, **MM)
                b2_free[pgc] = tok_w2
            ring_mark(s)
            P.waits("V", [tok_w2, tok_ar] + out_free[pgx])
            tv = None
            for half in range(2):
                tv = P.op("V", "tensor_add",
                          fstage[:, 2048 + pgx * 1024 + half * 512:
                                 2048 + pgx * 1024 + (half + 1) * 512],
                          ps[b0 + half][:],
                          fstage[:, pgx * 1024 + half * 512:
                                 pgx * 1024 + (half + 1) * 512], inc=sV)
            acc_rd_free[pgx] = [tv]
            bank_free[b0] = [tv]
            bank_free[b1] = [tv]
            P.wait("Y", tv)
            dst = out if p == 3 else adst
            wt = P.dma("Y", sAc[pgx], dst[oc * 128:(oc + 1) * 128, :],
                       fstage[:, 2048 + pgx * 1024:2048 + (pgx + 1) * 1024])
            out_free[pgx] = [wt]
        tok_accs_done = [(sAc[0], sAc[0].n), (sAc[1], sAc[1].n)]

    P.waits("Y", tok_accs_done)
    emit(nc, P)
    return nc


def emit(nc, P):
    with nc.Block() as blk:
        @blk.tensor
        def _(E):
            for f in P.ops["T"]:
                f(E)

        @blk.vector
        def _(E):
            for f in P.ops["V"]:
                f(E)

        @blk.scalar
        def _(E):
            for f in P.ops["S"]:
                f(E)

        @blk.gpsimd
        def _(E):
            for f in P.ops["G"]:
                f(E)

        @blk.sync
        def _(E):
            for f in P.ops["Y"]:
                f(E)


def _prep_tapes(Wq, bq, Wk, bk, Wv, bv, Wo, bo, w1, w3, w2,
                A1, B1, A3, B3, A2, B2):
    C = np.ascontiguousarray
    f = np.float32

    def img(W):
        o, d = W.shape
        ocs, dcs = o // 128, d // 128
        return C(W.reshape(ocs, 128, dcs, 128).transpose(0, 3, 2, 1)
                 .reshape(ocs, 128, dcs * 128)).astype(f)

    wq_t = img(Wq)
    wo_t = img(Wo)
    w13_t = C(np.stack([img(w1), img(w3)], axis=1))
    w2_t = img(w2)
    wk4 = Wk.reshape(KVH, 128, DC, 128).transpose(0, 3, 2, 1)
    wv4 = Wv.reshape(KVH, 128, DC, 128).transpose(0, 3, 2, 1)
    wkv_t = C(np.concatenate([wk4, wv4], axis=3).reshape(KVH, 128, DC * 256)).astype(f)
    b13_t = C(np.concatenate([B1.reshape(NHC, 128, R).transpose(0, 2, 1),
                              B3.reshape(NHC, 128, R).transpose(0, 2, 1)],
                             axis=2)).astype(f)
    b2_t = C(B2.reshape(OC, 128, R).transpose(0, 2, 1)).astype(f)
    a1_t = C(A1.reshape(R, DC, 128).transpose(2, 1, 0).reshape(128, DC * R)).astype(f)
    a3_t = C(A3.reshape(R, DC, 128).transpose(2, 1, 0).reshape(128, DC * R)).astype(f)
    a2_t = C(A2.reshape(R, NHC, 128).transpose(2, 1, 0).reshape(128, NHC * R)).astype(f)
    bq_p = C(bq.reshape(HEADS, 128).T).astype(f)
    bo_p = C(bo.reshape(OC, 128).T).astype(f)
    bkv_p = C(np.stack([bk.reshape(KVH, 128), bv.reshape(KVH, 128)], axis=1)
              .reshape(1, KVH * 256)).astype(f)
    return dict(wq_t=wq_t, wo_t=wo_t, w13_t=w13_t, w2_t=w2_t, wkv_t=wkv_t,
                b13_t=b13_t, b2_t=b2_t, a1_t=a1_t, a3_t=a3_t, a2_t=a2_t,
                bq_p=bq_p, bo_p=bo_p, bkv_p=bkv_p)


def kernel(L0_attn_hidden_states, L0_attn_Wq, L0_attn_bq, L0_attn_Wk, L0_attn_bk,
           L0_attn_Wv, L0_attn_bv, L0_attn_Wo, L0_attn_bo, L0_ffn_w1, L0_ffn_w3,
           L0_ffn_w2, L0_ffn_lora_A1, L0_ffn_lora_B1, L0_ffn_lora_A3,
           L0_ffn_lora_B3, L0_ffn_lora_A2, L0_ffn_lora_B2):
    x = np.ascontiguousarray(np.asarray(L0_attn_hidden_states, dtype=np.float32))
    a = lambda v: np.asarray(v, np.float32)
    tapes = _prep_tapes(a(L0_attn_Wq), a(L0_attn_bq), a(L0_attn_Wk), a(L0_attn_bk),
                        a(L0_attn_Wv), a(L0_attn_bv), a(L0_attn_Wo), a(L0_attn_bo),
                        a(L0_ffn_w1), a(L0_ffn_w3), a(L0_ffn_w2),
                        a(L0_ffn_lora_A1), a(L0_ffn_lora_B1), a(L0_ffn_lora_A3),
                        a(L0_ffn_lora_B3), a(L0_ffn_lora_A2), a(L0_ffn_lora_B2))

    nc = build_nc()
    in_maps = []
    for c in range(8):
        b, h = c // 2, c % 2
        xm = np.ascontiguousarray(x[b, h * TOK:(h + 1) * TOK, :])
        xo = np.ascontiguousarray(x[b, (1 - h) * TOK:(2 - h) * TOK, :])
        xt = np.ascontiguousarray(xm.T)
        in_maps.append(dict(x_my=xm, x_ot=xo, xT=xt, **tapes))

    res = run_bass_kernel_spmd(nc, in_maps, core_ids=list(range(8)))
    outp = np.empty((B, L, D), np.float32)
    for c in range(8):
        b, h = c // 2, c % 2
        outp[b, h * TOK:(h + 1) * TOK, :] = res.results[c]["out"].T
    return outp


# version 04
# speedup vs baseline: 1.0413x; 1.0413x over previous
import numpy as np
import concourse.bass as bass
import concourse.mybir as mybir
from concourse.bass_utils import run_bass_kernel_spmd

F32 = mybir.dt.float32
BF16 = mybir.dt.bfloat16
AF = mybir.ActivationFunctionType
ALU = mybir.AluOpType
AX = mybir.AxisListType

B, L, D = 4, 2048, 4096
TOK = 1024
DC = 32
HID, NHC = 11008, 86
OC = 32
HEADS, KVH = 32, 8
R = 16
FFN_SPLITS = [(0, 22), (22, 44), (44, 65), (65, 86)]
EPS_LN = 1e-5
EPS_ATTN = 1e-10


class SemTok:
    def __init__(self, nc, name):
        self.h = nc.alloc_semaphore(name)
        self.n = 0


class Prog:
    ENGS = ("T", "V", "S", "G", "Y")

    def __init__(self):
        self.ops = {e: [] for e in self.ENGS}
        self.last_wait = {e: {} for e in self.ENGS}

    def wait(self, eng, tok):
        if tok is None:
            return
        sem, val = tok
        lw = self.last_wait[eng]
        if lw.get(id(sem), -1) >= val:
            return
        lw[id(sem)] = val
        self.ops[eng].append(lambda E, s=sem.h, v=val: E.wait_ge(s, v))

    def waits(self, eng, toks):
        for t in toks:
            self.wait(eng, t)

    def op(self, eng, name, *args, inc=None, **kw):
        tok = None
        if inc is not None:
            inc.n += 1
            tok = (inc, inc.n)

        def f(E, name=name, args=args, kw=kw, tok=tok):
            ins = getattr(E, name)(*args, **kw)
            if tok is not None:
                ins.then_inc(tok[0].h, 1)
        self.ops[eng].append(f)
        return tok

    def dma(self, eng, sem, out, in_, **kw):
        sem.n += 16
        tok = (sem, sem.n)

        def f(E, out=out, in_=in_, kw=kw, tok=tok):
            E.dma_start(out=out, in_=in_, **kw).then_inc(tok[0].h, 16)
        self.ops[eng].append(f)
        return tok


DBG = {}

def build_nc(upto=99):
    dbg_sp = DBG.get("sp", 3)
    dbg_eps = DBG.get("eps", KVH)
    dbg_bias = DBG.get("bias", True)
    dbg_epi = DBG.get("epi", True)
    dbg_m = DBG.get("m", True)
    nc = bass.Bass("TRN2", target_bir_lowering=False, debug=False)

    x_my = nc.dram_tensor("x_my", [TOK, D], F32, kind="ExternalInput").ap()
    x_ot = nc.dram_tensor("x_ot", [TOK, D], F32, kind="ExternalInput").ap()
    xT = nc.dram_tensor("xT", [D, TOK], F32, kind="ExternalInput").ap()
    wkv_t = nc.dram_tensor("wkv_t", [KVH, 128, DC * 256], F32, kind="ExternalInput").ap()
    wq_t = nc.dram_tensor("wq_t", [HEADS, 128, DC * 128], F32, kind="ExternalInput").ap()
    wo_t = nc.dram_tensor("wo_t", [OC, 128, DC * 128], F32, kind="ExternalInput").ap()
    w13_t = nc.dram_tensor("w13_t", [NHC, 2, 128, DC * 128], F32, kind="ExternalInput").ap()
    w2_t = nc.dram_tensor("w2_t", [OC, 128, NHC * 128], F32, kind="ExternalInput").ap()
    b13_t = nc.dram_tensor("b13_t", [NHC, R, 256], F32, kind="ExternalInput").ap()
    b2_t = nc.dram_tensor("b2_t", [OC, R, 128], F32, kind="ExternalInput").ap()
    a1_t = nc.dram_tensor("a1_t", [128, DC * R], F32, kind="ExternalInput").ap()
    a3_t = nc.dram_tensor("a3_t", [128, DC * R], F32, kind="ExternalInput").ap()
    a2_t = nc.dram_tensor("a2_t", [128, NHC * R], F32, kind="ExternalInput").ap()
    bq_p = nc.dram_tensor("bq_p", [128, HEADS], F32, kind="ExternalInput").ap()
    bo_p = nc.dram_tensor("bo_p", [128, OC], F32, kind="ExternalInput").ap()
    bkv_p = nc.dram_tensor("bkv_p", [1, KVH * 256], F32, kind="ExternalInput").ap()
    id_p = nc.dram_tensor("id_p", [128, 128], F32, kind="ExternalInput").ap()
    out = nc.dram_tensor("out", [D, TOK], F32, kind="ExternalOutput").ap()

    xn_scr = nc.dram_tensor("xn_scr", [L, D], BF16, kind="Internal").ap()
    acc_scr = [nc.dram_tensor(f"acc{i}", [D, TOK], F32, kind="Internal").ap()
               for i in range(2)]

    xnT_my = nc.alloc_sbuf_tensor("xnT_my", [128, DC * TOK], BF16)
    buf_B = nc.alloc_sbuf_tensor("buf_B", [128, DC * TOK], BF16)
    wring = nc.alloc_sbuf_tensor("wring", [128, 16384], BF16)
    fstage = nc.alloc_sbuf_tensor("fstage", [128, 4096], F32)
    estage = nc.alloc_sbuf_tensor("estage", [128, 2048], F32)
    qbf = nc.alloc_sbuf_tensor("qbf", [128, 1024], BF16)
    M_bf = nc.alloc_sbuf_tensor("M_bf", [128, KVH * 128], BF16)
    bkv_r = nc.alloc_sbuf_tensor("bkv_r", [1, 512], BF16)
    b13r = nc.alloc_sbuf_tensor("b13r", [R, 512], BF16)
    b2r = nc.alloc_sbuf_tensor("b2r", [R, 256], BF16)
    a1_sb = nc.alloc_sbuf_tensor("a1_sb", [128, DC * R], BF16)
    a3_sb = nc.alloc_sbuf_tensor("a3_sb", [128, DC * R], BF16)
    a2r = nc.alloc_sbuf_tensor("a2r", [128, 2 * 22 * R], BF16)
    xa1_bf = nc.alloc_sbuf_tensor("xa1_bf", [R, TOK], BF16)
    xa2s_bf = xa1_bf
    xa3_bf = nc.alloc_sbuf_tensor("xa3_bf", [R, TOK], BF16)
    xa2_acc = nc.alloc_sbuf_tensor("xa2_acc", [R, TOK], BF16)
    bq_sb = nc.alloc_sbuf_tensor("bq_sb", [128, HEADS], F32)
    bq1_sb = nc.alloc_sbuf_tensor("bq1_sb", [128, HEADS], F32)
    bo_sb = nc.alloc_sbuf_tensor("bo_sb", [128, OC], F32)
    stats = nc.alloc_sbuf_tensor("stats", [128, 16], F32)
    nrm = nc.alloc_sbuf_tensor("nrm", [128, 16], F32)
    id_sb = nc.alloc_sbuf_tensor("id_sb", [128, 128], F32)
    ones_r = nc.alloc_sbuf_tensor("ones_r", [1, 128], BF16)
    ones_c = nc.alloc_sbuf_tensor("ones_c", [128, 2], BF16)

    assert nc.sbuf_bytes_remaining >= 0, nc.sbuf_bytes_remaining

    ps = [nc.alloc_psum_tensor(f"ps{i}", [128, 512], F32) for i in range(8)]

    P = Prog()
    sT = SemTok(nc, "sT")
    sV = SemTok(nc, "sV")
    sS = SemTok(nc, "sS")
    sIni = SemTok(nc, "sIni")
    sWs = [SemTok(nc, f"sW{i}") for i in range(4)]
    sFs = [SemTok(nc, f"sF{i}") for i in range(2)]
    sTr = SemTok(nc, "sTr")
    sScr = SemTok(nc, "sScr")
    sAc = [SemTok(nc, f"sAc{i}") for i in range(2)]
    sB13 = [SemTok(nc, f"sB13{i}") for i in range(2)]
    sB2 = [SemTok(nc, f"sB2{i}") for i in range(2)]
    sKb = [SemTok(nc, f"sKb{i}") for i in range(2)]
    sA2 = [SemTok(nc, f"sA2{i}") for i in range(2)]

    MM = dict(skip_group_check=True)
    bank_free = [[] for _ in range(8)]

    def tnow():
        return (sT, sT.n)

    def vnow():
        return (sV, sV.n)

    def snow():
        return (sS, sS.n)

    for dst, src in ((bq_sb[:], bq_p), (bo_sb[:], bo_p), (id_sb[:], id_p),
                     (a1_sb[:], a1_t), (a3_sb[:], a3_t)):
        P.dma("G", sIni, dst, src)
    tok_init = (sIni, sIni.n)
    P.op("V", "memset", ones_r[:], 1.0)
    P.op("V", "memset", ones_c[:], 1.0)
    P.wait("V", tok_init)
    tok_const = P.op("V", "tensor_scalar_add", bq1_sb[:], bq_sb[:], 1.0, inc=sV)

    xslot_free = [None, None]
    dump_free = None
    xn_gate = None
    tp_free = [[], []]
    tok_xnT = None
    xnT_v = xnT_my[:].rearrange("p (dc t) -> p dc t", t=TOK)
    for t in range(16):
        src = x_my if t < 8 else x_ot
        rows = (t % 8) * 128
        htoks = [None, None]
        for half in range(2):
            P.wait("Y", xslot_free[half])
            htoks[half] = P.dma("Y", sFs[half],
                                fstage[:, half * 2048:(half + 1) * 2048],
                                src[rows:rows + 128, half * 2048:(half + 1) * 2048])
        for half in range(2):
            xh = fstage[:, half * 2048:(half + 1) * 2048]
            P.wait("V", htoks[half])
            P.op("V", "reduce_sum", stats[:, half:half + 1], xh, axis=AX.X)
            P.waits("S", [htoks[half], dump_free, xn_gate])
            tq = P.op("S", "activation", estage[:], xh, AF.Square, inc=sS)
            P.wait("V", tq)
            dump_free = P.op("V", "reduce_sum", stats[:, 2 + half:3 + half],
                             estage[:], axis=AX.X, inc=sV)
        P.op("V", "drain")
        P.op("V", "tensor_add", stats[:, 4:5], stats[:, 0:1], stats[:, 1:2])
        P.op("V", "tensor_add", stats[:, 5:6], stats[:, 2:3], stats[:, 3:4])
        P.op("V", "drain")
        P.op("V", "tensor_scalar_mul", stats[:, 6:7], stats[:, 4:5], 1.0 / D)
        P.op("V", "drain")
        P.op("V", "tensor_mul", stats[:, 7:8], stats[:, 6:7], stats[:, 6:7])
        P.op("V", "drain")
        P.op("V", "scalar_tensor_tensor", stats[:, 8:9], stats[:, 5:6], 1.0 / D,
             stats[:, 7:8], ALU.mult, ALU.subtract)
        P.op("V", "drain")
        P.op("V", "tensor_scalar_add", stats[:, 9:10], stats[:, 8:9], EPS_LN)
        P.op("V", "drain")
        tr = P.op("V", "reciprocal", stats[:, 10:11], stats[:, 9:10], inc=sV)
        P.wait("S", tr)
        trs = P.op("S", "activation", stats[:, 11:12], stats[:, 10:11], AF.Sqrt,
                   inc=sS)
        P.wait("V", trs)
        tmb = P.op("V", "tensor_scalar", stats[:, 12:13], stats[:, 6:7],
                   stats[:, 11:12], -1.0, ALU.mult, ALU.mult, inc=sV)
        P.op("S", "drain")
        P.wait("S", tmb)
        for half in range(2):
            xh = fstage[:, half * 2048:(half + 1) * 2048]
            P.wait("S", xn_gate)
            tn = P.op("S", "activation", estage[:], xh, AF.Identity,
                      bias=stats[:, 12:13], scale=stats[:, 11:12], inc=sS)
            xslot_free[half] = tn
            if t < 8:
                for grp in range(4):
                    bk = ps[grp % 2]
                    P.waits("T", [tn, tok_init, tok_const] + tp_free[grp % 2])
                    for j in range(4):
                        cc = grp * 4 + j
                        ttok = P.op("T", "transpose", bk[:, j * 128:(j + 1) * 128],
                                    estage[:, cc * 128:(cc + 1) * 128], id_sb[:],
                                    inc=(sT if j == 3 else None))
                    dc0 = half * 16 + grp * 4
                    P.wait("V", ttok)
                    vtok = P.op("V", "tensor_copy",
                                xnT_v[:, dc0:dc0 + 4, t * 128:(t + 1) * 128],
                                bk[:, 0:512].rearrange("p (c x) -> p c x", x=128),
                                inc=sV)
                    tp_free[grp % 2] = [vtok]
                xn_gate = ttok
                tok_xnT = vtok
            else:
                P.wait("G", tn)
                xn_gate = P.dma("G", sScr,
                                xn_scr[t * 128:(t + 1) * 128,
                                       half * 2048:(half + 1) * 2048], estage[:])

    def debug_finish(taps=(), dram_taps=()):
        P.wait("V", (sS, sS.n))
        P.wait("V", (sT, sT.n))
        P.op("V", "memset", fstage[:, 2048:3072], 0.0, inc=sV)
        P.wait("Y", (sV, sV.n))
        for oc_ in range(OC):
            P.dma("Y", sAc[0], out[oc_ * 128:(oc_ + 1) * 128, :],
                  fstage[:, 2048:3072])
        P.wait("Y", (sAc[0], sAc[0].n))
        for ap, row in taps:
            ncols = ap.shape[-1]
            pdim = ap.shape[0]
            P.wait("V", (sAc[0], sAc[0].n))
            tv_ = P.op("V", "tensor_copy", fstage[0:pdim, 2048:2048 + ncols], ap,
                       inc=sV)
            P.wait("Y", tv_)
            P.dma("Y", sAc[0], out[row:row + pdim, 0:ncols],
                  fstage[0:pdim, 2048:2048 + ncols])
            P.wait("Y", (sAc[0], sAc[0].n))
        for ap, row in dram_taps:
            ncols = ap.shape[-1]
            P.wait("G", (sAc[0], sAc[0].n))
            tg_ = P.dma("G", sIni, fstage[:, 2048:2048 + ncols], ap)
            P.wait("Y", tg_)
            P.dma("Y", sAc[0], out[row:row + 128, 0:ncols],
                  fstage[:, 2048:2048 + ncols])
            P.wait("Y", (sAc[0], sAc[0].n))
        emit(nc, P)
        return nc

    if upto <= 0:
        return debug_finish()


    ring_free = [None, None, None, None]
    ring_i = [0]

    def ring_load(src_ap):
        s = ring_i[0] % 4
        ring_i[0] += 1
        P.wait("G", ring_free[s])
        ncols = src_ap.shape[-1]
        dst = wring[:, s * 4096:s * 4096 + ncols]
        tok = P.dma("G", sWs[s], dst, src_ap)
        return s, dst, tok

    def ring_mark(s):
        ring_free[s] = tnow()

    SP_TILES = [list(range(8)), [8, 9, 10, 11], [12, 13, 14, 15]]
    xnst_free = (sS, sS.n)
    tp_free = [[], []]
    bkv_free = [None, None]
    est_free = [[snow(), tnow()], [snow(), tnow()]]
    kvst_free = [[], []]
    it = 0
    tok_oq = None
    last_sp = dbg_sp - 1
    for sp in range(dbg_sp):
        if sp > 0:
            P.wait("G", (sScr, sScr.n))
            rows0 = TOK + (sp - 1) * 512
            bufB_v = buf_B[:].rearrange("p (dc t) -> p dc t", t=512)
            for tl in range(4):
                for half in range(2):
                    P.wait("G", xnst_free)
                    ld = P.dma("G", sTr, fstage[:, 0:2048],
                               xn_scr[rows0 + tl * 128:rows0 + (tl + 1) * 128,
                                      half * 2048:(half + 1) * 2048])
                    for grp in range(4):
                        bk = ps[5 + grp % 2]
                        P.waits("T", [ld] + tp_free[grp % 2])
                        for j in range(4):
                            cc = grp * 4 + j
                            ttok = P.op("T", "transpose",
                                        bk[:, j * 128:(j + 1) * 128],
                                        fstage[:, cc * 128:(cc + 1) * 128],
                                        id_sb[:], inc=(sT if j == 3 else None))
                        dc0 = half * 16 + grp * 4
                        P.wait("V", ttok)
                        vtok = P.op("V", "tensor_copy",
                                    bufB_v[:, dc0:dc0 + 4,
                                           tl * 128:(tl + 1) * 128],
                                    bk[:, 0:512].rearrange("p (c x) -> p c x",
                                                           x=128), inc=sV)
                        tp_free[grp % 2] = [vtok]
                    xnst_free = ttok
            tok_oq = vtok
        for ep in range(dbg_eps):
            sA, wA, tok_wA = ring_load(wkv_t[ep][:, 0:4096])
            sB, wB, tok_wB = ring_load(wkv_t[ep][:, 4096:8192])
            kpar = (sp * KVH + ep) % 2
            P.wait("G", bkv_free[kpar])
            tok_bkv = P.dma("G", sKb[kpar], bkv_r[0:1, kpar * 256:(kpar + 1) * 256],
                            bkv_p[0:1, ep * 256:(ep + 1) * 256])
            for t in SP_TILES[sp]:
                pg = it % 2
                kb = ps[pg]
                P.waits("T", [tok_wA, tok_wB, tok_bkv, tok_init] + bank_free[pg])
                if sp == 0:
                    P.wait("T", tok_xnT)
                    lhs, cw, toff = xnT_my, TOK, t * 128
                else:
                    P.wait("T", tok_oq)
                    lhs, cw, toff = buf_B, 512, (t - 8 - (sp - 1) * 4) * 128
                P.wait("T", tok_const)
                for dc in range(DC):
                    wsrc = wA if dc < 16 else wB
                    wcol = (dc % 16) * 256
                    fin_kv = (dc == 31) and not dbg_bias
                    tok_kv = P.op("T", "matmul", kb[:, 0:256],
                                  lhs[:, dc * cw + toff:dc * cw + toff + 128],
                                  wsrc[:, wcol:wcol + 256], start=(dc == 0),
                                  stop=fin_kv, inc=(sT if fin_kv else None), **MM)
                if dbg_bias:
                    tok_kv = P.op("T", "matmul", kb[:, 0:256], ones_r[0:1, :],
                                  bkv_r[0:1, kpar * 256:(kpar + 1) * 256],
                                  start=False, stop=True, inc=sT, **MM)
                if not dbg_epi:
                    bank_free[pg] = []
                    it += 1
                    tok_nm = tok_kv
                    continue
                eA = estage[:, pg * 512:pg * 512 + 128]
                eB = estage[:, pg * 512 + 128:pg * 512 + 256]
                eC = estage[:, pg * 512 + 256:pg * 512 + 384]
                kv = qbf[:, pg * 512:pg * 512 + 256]
                P.waits("V", [tok_kv] + est_free[pg] + kvst_free[pg])
                tm = P.op("V", "tensor_scalar_min", eA, kb[:, 0:128], 0.0, inc=sV)
                tvc = P.op("V", "tensor_copy", kv[:, 128:256], kb[:, 128:256],
                           inc=sV)
                P.wait("S", tm)
                te = P.op("S", "activation", eB, eA, AF.Exp, inc=sS)
                tu = P.op("V", "tensor_scalar_add", eC, kb[:, 0:128], 1.0, inc=sV)
                P.op("V", "drain")
                P.wait("V", te)
                tkb = P.op("V", "tensor_tensor", kv[:, 0:128], eC, eB, ALU.max,
                           inc=sV)
                est_free[pg] = [tkb, te]
                bank_free[pg] = [tu, tvc]
                if not dbg_m:
                    kvst_free[pg] = [tkb]
                    it += 1
                    tok_nm = tok_kv
                    continue
                P.waits("T", [tkb, tvc])
                mbank = ps[2 + ep // 4]
                mcol = (ep % 4) * 128
                stM = (sp == 0 and t == 0 and ep % 4 == 0)
                stN = (sp == 0 and t == 0 and ep == 0)
                spmm = (sp == last_sp and t == SP_TILES[sp][-1])
                P.op("T", "matmul", mbank[:, mcol:mcol + 128], kv[:, 0:128],
                     kv[:, 128:256], start=stM, stop=spmm, **MM)
                tok_nm = P.op("T", "matmul", ps[4][:, 4 * ep:4 * ep + 2],
                              kv[:, 0:128], ones_c[:], start=stN, stop=spmm,
                              inc=sT, **MM)
                kvst_free[pg] = [tok_nm]
                it += 1
            ring_mark(sA)
            ring_mark(sB)
            bkv_free[kpar] = tnow()
    tok_M_last = tok_nm
    P.wait("V", tok_M_last)
    P.op("V", "tensor_copy", M_bf[:, 0:512], ps[2][:, 0:512])
    P.op("V", "tensor_copy", M_bf[:, 512:1024], ps[3][:, 0:512])
    for e_ in range(KVH):
        P.op("V", "tensor_scalar_add", nrm[:, 8 + e_:9 + e_],
             ps[4][:, 4 * e_:4 * e_ + 1], EPS_ATTN)
    P.op("V", "drain")
    tok_M = P.op("V", "reciprocal", nrm[:, 0:8], nrm[:, 8:16], inc=sV)
    for i in (2, 3, 4):
        bank_free[i] = [tok_M]

    if upto <= 1:
        return debug_finish(taps=[
            (xnT_my[:, 0:1024], 0),
            (xnT_my[:, 5 * 1024:6 * 1024], 128),
            (M_bf[:, 0:1024], 256),
            (nrm[:, 0:16], 384),
            (qbf[:, 0:1024], 512),
            (stats[:, 0:16], 768),
        ], dram_taps=[
            (xn_scr[0:128, 0:1024], 640),
            (xn_scr[1920:2048, 0:1024], 896),
        ])

    attnT = buf_B
    est2_free = [[snow()], [snow()]]
    qbf_free = [kvst_free[0] + [snow()], kvst_free[1] + [snow()]]
    tok_attnT = None
    for h in range(HEADS):
        s, wdst, tok_w = ring_load(wq_t[h])
        b0, b1 = (h % 2) * 2, (h % 2) * 2 + 1
        P.waits("T", [tok_w, tok_xnT] + bank_free[b0] + bank_free[b1])
        tok_q = None
        for dc in range(DC):
            st, fin = dc == 0, dc == 31
            P.op("T", "matmul", ps[b0][:], wdst[:, dc * 128:(dc + 1) * 128],
                 xnT_my[:, dc * TOK:dc * TOK + 512], start=st, stop=fin, **MM)
            tk = P.op("T", "matmul", ps[b1][:], wdst[:, dc * 128:(dc + 1) * 128],
                      xnT_my[:, dc * TOK + 512:(dc + 1) * TOK], start=st,
                      stop=fin, inc=(sT if fin else None), **MM)
            if fin:
                tok_q = tk
        ring_mark(s)
        for half in range(2):
            qb = ps[b0 + half]
            pg = (h * 2 + half) % 2
            eA = estage[:, pg * 1024:pg * 1024 + 512]
            eB = estage[:, pg * 1024 + 512:(pg + 1) * 1024]
            qslot = qbf[:, pg * 512:(pg + 1) * 512]
            P.waits("V", [tok_q, tok_const] + est2_free[pg])
            tm = P.op("V", "tensor_scalar", eA, qb[:], bq_sb[:, h:h + 1], 0.0,
                      ALU.add, ALU.min, inc=sV)
            P.wait("S", tm)
            te = P.op("S", "activation", eB, eA, AF.Exp, inc=sS)
            P.wait("V", te)
            tu = P.op("V", "tensor_scalar_add", eA, qb[:], bq1_sb[:, h:h + 1],
                      inc=sV)
            P.op("V", "drain")
            P.waits("V", qbf_free[pg])
            tqb = P.op("V", "tensor_tensor", qslot, eA, eB, ALU.max, inc=sV)
            est2_free[pg] = [tqb, te]
            bank_free[b0 + half] = [tu]
            ab = 4 + (h % 2) * 2 + half
            kvh = h // 4
            P.waits("T", [tqb, tok_M] + bank_free[ab])
            tok_a = P.op("T", "matmul", ps[ab][:],
                         M_bf[:, kvh * 128:(kvh + 1) * 128], qslot,
                         start=True, stop=True, inc=sT, **MM)
            qbf_free[pg] = [tok_a]
            P.wait("V", tok_a)
            tok_attnT = P.op("V", "tensor_scalar_mul",
                             attnT[:, h * TOK + half * 512:
                                   h * TOK + (half + 1) * 512],
                             ps[ab][:], nrm[:, kvh:kvh + 1], inc=sV)
            bank_free[ab] = [tok_attnT]

    if upto <= 2:
        return debug_finish()

    xT_free = [[snow(), vnow(), tnow()], [snow(), vnow(), tnow()]]
    out_free = [[], []]
    for oc in range(OC):
        s, wdst, tok_w = ring_load(wo_t[oc])
        pgx = oc % 2
        P.waits("Y", xT_free[pgx])
        tok_xT = P.dma("Y", sFs[pgx], fstage[:, pgx * 1024:(pgx + 1) * 1024],
                       xT[oc * 128:(oc + 1) * 128, :])
        b0, b1 = (oc % 2) * 2, (oc % 2) * 2 + 1
        P.waits("T", [tok_w, tok_attnT] + bank_free[b0] + bank_free[b1])
        tok_wo = None
        for fc in range(DC):
            st, fin = fc == 0, fc == 31
            P.op("T", "matmul", ps[b0][:], wdst[:, fc * 128:(fc + 1) * 128],
                 attnT[:, fc * TOK:fc * TOK + 512], start=st, stop=fin, **MM)
            tk = P.op("T", "matmul", ps[b1][:], wdst[:, fc * 128:(fc + 1) * 128],
                      attnT[:, fc * TOK + 512:(fc + 1) * TOK], start=st,
                      stop=fin, inc=(sT if fin else None), **MM)
            if fin:
                tok_wo = tk
        ring_mark(s)
        P.waits("V", [tok_wo, tok_xT] + out_free[pgx])
        tv = None
        for half in range(2):
            tv = P.op("V", "scalar_tensor_tensor",
                      fstage[:, 2048 + pgx * 1024 + half * 512:
                             2048 + pgx * 1024 + (half + 1) * 512],
                      ps[b0 + half][:], bo_sb[:, oc:oc + 1],
                      fstage[:, pgx * 1024 + half * 512:
                             pgx * 1024 + (half + 1) * 512],
                      ALU.add, ALU.add, inc=sV)
        xT_free[pgx] = [tv]
        bank_free[b0] = [tv]
        bank_free[b1] = [tv]
        P.wait("Y", tv)
        wt = P.dma("Y", sAc[pgx], acc_scr[0][oc * 128:(oc + 1) * 128, :],
                   fstage[:, 2048 + pgx * 1024:2048 + (pgx + 1) * 1024])
        out_free[pgx] = [wt]
    tok_accs_done = [(sAc[0], sAc[0].n), (sAc[1], sAc[1].n)]

    if upto <= 3:
        return debug_finish()

    for (asb, xout) in ((a1_sb, xa1_bf), (a3_sb, xa3_bf)):
        P.waits("T", [tok_init] + bank_free[4] + bank_free[5])
        tok_xa = None
        for dc in range(DC):
            st, fin = dc == 0, dc == 31
            P.op("T", "matmul", ps[4][0:R, :], asb[:, dc * R:(dc + 1) * R],
                 xnT_my[:, dc * TOK:dc * TOK + 512], start=st, stop=fin, **MM)
            tk = P.op("T", "matmul", ps[5][0:R, :], asb[:, dc * R:(dc + 1) * R],
                      xnT_my[:, dc * TOK + 512:(dc + 1) * TOK], start=st,
                      stop=fin, inc=(sT if fin else None), **MM)
            if fin:
                tok_xa = tk
        P.wait("V", tok_xa)
        P.op("V", "tensor_scalar_mul", xout[:, 0:512], ps[4][0:R, :], 2.0)
        tva = P.op("V", "tensor_scalar_mul", xout[:, 512:1024], ps[5][0:R, :],
                   2.0, inc=sV)
        bank_free[4] = [tva]
        bank_free[5] = [tva]
    tok_xa13 = tva

    if upto <= 4:
        return debug_finish()

    g_sb = buf_B
    b13_free = [None, None]
    b2_free = [None, None]
    a2r_free = [None, None]
    sig_free = [[vnow()], [vnow()]]
    acc_rd_free = [xT_free[0], xT_free[1]]
    tok_g_banks = {}
    tok_xa2s = None
    for p, (h0, h1) in enumerate(FFN_SPLITS):
        n = h1 - h0
        pga = p % 2
        P.wait("G", a2r_free[pga])
        tok_a2 = P.dma("G", sA2[pga], a2r[:, pga * 22 * R:pga * 22 * R + n * R],
                       a2_t[:, h0 * R:h1 * R])
        for hc in range(h0, h1):
            s1, w1d, tok_w1 = ring_load(w13_t[hc, 0])
            s3, w3d, tok_w3 = ring_load(w13_t[hc, 1])
            pgb = hc % 2
            P.wait("G", b13_free[pgb])
            tok_b13 = P.dma("G", sB13[pgb], b13r[:, pgb * 256:(pgb + 1) * 256],
                            b13_t[hc])
            bb = [pgb * 4, pgb * 4 + 1, pgb * 4 + 2, pgb * 4 + 3]
            P.waits("T", [tok_w1, tok_w3, tok_b13, tok_xa13]
                    + bank_free[bb[0]] + bank_free[bb[1]]
                    + bank_free[bb[2]] + bank_free[bb[3]])
            for dc in range(DC):
                st = dc == 0
                P.op("T", "matmul", ps[bb[0]][:], w1d[:, dc * 128:(dc + 1) * 128],
                     xnT_my[:, dc * TOK:dc * TOK + 512], start=st, stop=False, **MM)
                P.op("T", "matmul", ps[bb[1]][:], w1d[:, dc * 128:(dc + 1) * 128],
                     xnT_my[:, dc * TOK + 512:(dc + 1) * TOK], start=st,
                     stop=False, **MM)
                P.op("T", "matmul", ps[bb[2]][:], w3d[:, dc * 128:(dc + 1) * 128],
                     xnT_my[:, dc * TOK:dc * TOK + 512], start=st, stop=False, **MM)
                P.op("T", "matmul", ps[bb[3]][:], w3d[:, dc * 128:(dc + 1) * 128],
                     xnT_my[:, dc * TOK + 512:(dc + 1) * TOK], start=st,
                     stop=False, **MM)
            P.op("T", "matmul", ps[bb[0]][:], b13r[0:R, pgb * 256:pgb * 256 + 128],
                 xa1_bf[:, 0:512], start=False, stop=True, **MM)
            P.op("T", "matmul", ps[bb[1]][:], b13r[0:R, pgb * 256:pgb * 256 + 128],
                 xa1_bf[:, 512:1024], start=False, stop=True, **MM)
            P.op("T", "matmul", ps[bb[2]][:],
                 b13r[0:R, pgb * 256 + 128:pgb * 256 + 256],
                 xa3_bf[:, 0:512], start=False, stop=True, **MM)
            tok_h = P.op("T", "matmul", ps[bb[3]][:],
                         b13r[0:R, pgb * 256 + 128:pgb * 256 + 256],
                         xa3_bf[:, 512:1024], start=False, stop=True, inc=sT, **MM)
            b13_free[pgb] = tok_h
            ring_mark(s1)
            ring_mark(s3)
            P.waits("S", [tok_h] + sig_free[pgb])
            P.op("S", "activation", estage[:, pgb * 1024:pgb * 1024 + 512],
                 ps[bb[0]][:], AF.Silu)
            tok_sig = P.op("S", "activation",
                           estage[:, pgb * 1024 + 512:(pgb + 1) * 1024],
                           ps[bb[1]][:], AF.Silu, inc=sS)
            P.wait("V", tok_sig)
            P.op("V", "tensor_mul", g_sb[:, (hc - h0) * TOK:(hc - h0) * TOK + 512],
                 estage[:, pgb * 1024:pgb * 1024 + 512], ps[bb[2]][:])
            tok_g = P.op("V", "tensor_mul",
                         g_sb[:, (hc - h0) * TOK + 512:(hc - h0 + 1) * TOK],
                         estage[:, pgb * 1024 + 512:(pgb + 1) * 1024],
                         ps[bb[3]][:], inc=sV)
            sig_free[pgb] = [tok_g]
            bank_free[bb[0]] = [tok_sig]
            bank_free[bb[1]] = [tok_sig]
            bank_free[bb[2]] = [tok_g]
            bank_free[bb[3]] = [tok_g]
            tok_g_banks[hc] = tok_g
        P.waits("T", [tok_a2] + bank_free[0] + bank_free[1])
        tok_xa2 = None
        for hc in range(h0, h1):
            P.wait("T", tok_g_banks[hc])
            st, fin = hc == h0, hc == h1 - 1
            acol = pga * 22 * R + (hc - h0) * R
            P.op("T", "matmul", ps[0][0:R, :], a2r[:, acol:acol + R],
                 g_sb[:, (hc - h0) * TOK:(hc - h0) * TOK + 512],
                 start=st, stop=fin, **MM)
            tk = P.op("T", "matmul", ps[1][0:R, :], a2r[:, acol:acol + R],
                      g_sb[:, (hc - h0) * TOK + 512:(hc - h0 + 1) * TOK],
                      start=st, stop=fin, inc=(sT if fin else None), **MM)
            if fin:
                tok_xa2 = tk
        a2r_free[pga] = tok_xa2
        P.wait("V", tok_xa2)
        if p == 0:
            P.op("V", "tensor_copy", xa2_acc[:, 0:512], ps[0][0:R, :])
            tx2 = P.op("V", "tensor_copy", xa2_acc[:, 512:1024], ps[1][0:R, :],
                       inc=sV)
        else:
            P.op("V", "tensor_add", xa2_acc[:, 0:512], xa2_acc[:, 0:512],
                 ps[0][0:R, :])
            tx2 = P.op("V", "tensor_add", xa2_acc[:, 512:1024],
                       xa2_acc[:, 512:1024], ps[1][0:R, :], inc=sV)
        bank_free[0] = [tx2]
        bank_free[1] = [tx2]
        if p == 3:
            P.op("V", "drain")
            P.op("V", "tensor_scalar_mul", xa2s_bf[:, 0:512], xa2_acc[:, 0:512],
                 2.0)
            tok_xa2s = P.op("V", "tensor_scalar_mul", xa2s_bf[:, 512:1024],
                            xa2_acc[:, 512:1024], 2.0, inc=sV)
        asrc = acc_scr[0] if p % 2 == 0 else acc_scr[1]
        adst = acc_scr[1] if p % 2 == 0 else acc_scr[0]
        P.waits("Y", tok_accs_done)
        acc_wr0 = acc_wr1 = None
        for oc in range(OC):
            s, wdst, tok_w = ring_load(w2_t[oc][:, h0 * 128:h1 * 128])
            pgx = oc % 2
            P.waits("Y", acc_rd_free[pgx])
            tok_ar = P.dma("Y", sFs[pgx], fstage[:, pgx * 1024:(pgx + 1) * 1024],
                           asrc[oc * 128:(oc + 1) * 128, :])
            b0, b1 = 2 + (oc % 2) * 2, 3 + (oc % 2) * 2
            P.waits("T", [tok_w] + bank_free[b0] + bank_free[b1])
            tok_w2 = None
            for hc in range(h0, h1):
                st = hc == h0
                fin = (hc == h1 - 1) and p != 3
                P.op("T", "matmul", ps[b0][:],
                     wdst[:, (hc - h0) * 128:(hc - h0 + 1) * 128],
                     g_sb[:, (hc - h0) * TOK:(hc - h0) * TOK + 512],
                     start=st, stop=fin, **MM)
                tk = P.op("T", "matmul", ps[b1][:],
                          wdst[:, (hc - h0) * 128:(hc - h0 + 1) * 128],
                          g_sb[:, (hc - h0) * TOK + 512:(hc - h0 + 1) * TOK],
                          start=st, stop=fin, inc=(sT if fin else None), **MM)
                if fin:
                    tok_w2 = tk
            if p == 3:
                pgc = oc % 2
                P.wait("G", b2_free[pgc])
                tok_b2 = P.dma("G", sB2[pgc], b2r[:, pgc * 128:(pgc + 1) * 128],
                               b2_t[oc])
                P.waits("T", [tok_b2, tok_xa2s])
                P.op("T", "matmul", ps[b0][:], b2r[0:R, pgc * 128:(pgc + 1) * 128],
                     xa2s_bf[:, 0:512], start=False, stop=True, **MM)
                tok_w2 = P.op("T", "matmul", ps[b1][:],
                              b2r[0:R, pgc * 128:(pgc + 1) * 128],
                              xa2s_bf[:, 512:1024], start=False, stop=True,
                              inc=sT, **MM)
                b2_free[pgc] = tok_w2
            ring_mark(s)
            P.waits("V", [tok_w2, tok_ar] + out_free[pgx])
            tv = None
            for half in range(2):
                tv = P.op("V", "tensor_add",
                          fstage[:, 2048 + pgx * 1024 + half * 512:
                                 2048 + pgx * 1024 + (half + 1) * 512],
                          ps[b0 + half][:],
                          fstage[:, pgx * 1024 + half * 512:
                                 pgx * 1024 + (half + 1) * 512], inc=sV)
            acc_rd_free[pgx] = [tv]
            bank_free[b0] = [tv]
            bank_free[b1] = [tv]
            P.wait("Y", tv)
            dst = out if p == 3 else adst
            wt = P.dma("Y", sAc[pgx], dst[oc * 128:(oc + 1) * 128, :],
                       fstage[:, 2048 + pgx * 1024:2048 + (pgx + 1) * 1024])
            out_free[pgx] = [wt]
        tok_accs_done = [(sAc[0], sAc[0].n), (sAc[1], sAc[1].n)]

    P.waits("Y", tok_accs_done)
    emit(nc, P)
    return nc


def emit(nc, P):
    with nc.Block() as blk:
        @blk.tensor
        def _(E):
            for f in P.ops["T"]:
                f(E)

        @blk.vector
        def _(E):
            for f in P.ops["V"]:
                f(E)

        @blk.scalar
        def _(E):
            for f in P.ops["S"]:
                f(E)

        @blk.gpsimd
        def _(E):
            for f in P.ops["G"]:
                f(E)

        @blk.sync
        def _(E):
            for f in P.ops["Y"]:
                f(E)


def _prep_tapes(Wq, bq, Wk, bk, Wv, bv, Wo, bo, w1, w3, w2,
                A1, B1, A3, B3, A2, B2):
    C = np.ascontiguousarray
    f = np.float32

    def img(W):
        o, d = W.shape
        ocs, dcs = o // 128, d // 128
        return C(W.reshape(ocs, 128, dcs, 128).transpose(0, 3, 2, 1)
                 .reshape(ocs, 128, dcs * 128)).astype(f)

    wq_t = img(Wq)
    wo_t = img(Wo)
    w13_t = C(np.stack([img(w1), img(w3)], axis=1))
    w2_t = img(w2)
    wk4 = Wk.reshape(KVH, 128, DC, 128).transpose(0, 3, 2, 1)
    wv4 = Wv.reshape(KVH, 128, DC, 128).transpose(0, 3, 2, 1)
    wkv_t = C(np.concatenate([wk4, wv4], axis=3).reshape(KVH, 128, DC * 256)).astype(f)
    b13_t = C(np.concatenate([B1.reshape(NHC, 128, R).transpose(0, 2, 1),
                              B3.reshape(NHC, 128, R).transpose(0, 2, 1)],
                             axis=2)).astype(f)
    b2_t = C(B2.reshape(OC, 128, R).transpose(0, 2, 1)).astype(f)
    a1_t = C(A1.reshape(R, DC, 128).transpose(2, 1, 0).reshape(128, DC * R)).astype(f)
    a3_t = C(A3.reshape(R, DC, 128).transpose(2, 1, 0).reshape(128, DC * R)).astype(f)
    a2_t = C(A2.reshape(R, NHC, 128).transpose(2, 1, 0).reshape(128, NHC * R)).astype(f)
    bq_p = C(bq.reshape(HEADS, 128).T).astype(f)
    bo_p = C(bo.reshape(OC, 128).T).astype(f)
    bkv_p = C(np.stack([bk.reshape(KVH, 128), bv.reshape(KVH, 128)], axis=1)
              .reshape(1, KVH * 256)).astype(f)
    return dict(wq_t=wq_t, wo_t=wo_t, w13_t=w13_t, w2_t=w2_t, wkv_t=wkv_t,
                b13_t=b13_t, b2_t=b2_t, a1_t=a1_t, a3_t=a3_t, a2_t=a2_t,
                bq_p=bq_p, bo_p=bo_p, bkv_p=bkv_p,
                id_p=np.eye(128, dtype=np.float32))


def kernel(L0_attn_hidden_states, L0_attn_Wq, L0_attn_bq, L0_attn_Wk, L0_attn_bk,
           L0_attn_Wv, L0_attn_bv, L0_attn_Wo, L0_attn_bo, L0_ffn_w1, L0_ffn_w3,
           L0_ffn_w2, L0_ffn_lora_A1, L0_ffn_lora_B1, L0_ffn_lora_A3,
           L0_ffn_lora_B3, L0_ffn_lora_A2, L0_ffn_lora_B2):
    x = np.ascontiguousarray(np.asarray(L0_attn_hidden_states, dtype=np.float32))
    a = lambda v: np.asarray(v, np.float32)
    tapes = _prep_tapes(a(L0_attn_Wq), a(L0_attn_bq), a(L0_attn_Wk), a(L0_attn_bk),
                        a(L0_attn_Wv), a(L0_attn_bv), a(L0_attn_Wo), a(L0_attn_bo),
                        a(L0_ffn_w1), a(L0_ffn_w3), a(L0_ffn_w2),
                        a(L0_ffn_lora_A1), a(L0_ffn_lora_B1), a(L0_ffn_lora_A3),
                        a(L0_ffn_lora_B3), a(L0_ffn_lora_A2), a(L0_ffn_lora_B2))

    nc = build_nc()
    in_maps = []
    for c in range(8):
        b, h = c // 2, c % 2
        xm = np.ascontiguousarray(x[b, h * TOK:(h + 1) * TOK, :])
        xo = np.ascontiguousarray(x[b, (1 - h) * TOK:(2 - h) * TOK, :])
        xt = np.ascontiguousarray(xm.T)
        in_maps.append(dict(x_my=xm, x_ot=xo, xT=xt, **tapes))

    res = run_bass_kernel_spmd(nc, in_maps, core_ids=list(range(8)))
    outp = np.empty((B, L, D), np.float32)
    for c in range(8):
        b, h = c // 2, c % 2
        outp[b, h * TOK:(h + 1) * TOK, :] = res.results[c]["out"].T
    return outp


# version 05
# speedup vs baseline: 1.0525x; 1.0107x over previous
import numpy as np
import concourse.bass as bass
import concourse.mybir as mybir
from concourse.bass_utils import run_bass_kernel_spmd

F32 = mybir.dt.float32
BF16 = mybir.dt.bfloat16
AF = mybir.ActivationFunctionType
ALU = mybir.AluOpType
AX = mybir.AxisListType

B, L, D = 4, 2048, 4096
TOK = 1024
DC = 32
HID, NHC = 11008, 86
OC = 32
HEADS, KVH = 32, 8
R = 16
FFN_SPLITS = [(0, 22), (22, 44), (44, 65), (65, 86)]
EPS_LN = 1e-5
EPS_ATTN = 1e-10


class SemTok:
    def __init__(self, nc, name):
        self.h = nc.alloc_semaphore(name)
        self.n = 0


class Prog:
    ENGS = ("T", "V", "S", "G", "Y")

    def __init__(self):
        self.ops = {e: [] for e in self.ENGS}
        self.last_wait = {e: {} for e in self.ENGS}

    def wait(self, eng, tok):
        if tok is None:
            return
        sem, val = tok
        lw = self.last_wait[eng]
        if lw.get(id(sem), -1) >= val:
            return
        lw[id(sem)] = val
        self.ops[eng].append(lambda E, s=sem.h, v=val: E.wait_ge(s, v))

    def waits(self, eng, toks):
        for t in toks:
            self.wait(eng, t)

    def op(self, eng, name, *args, inc=None, **kw):
        tok = None
        if inc is not None:
            inc.n += 1
            tok = (inc, inc.n)

        def f(E, name=name, args=args, kw=kw, tok=tok):
            ins = getattr(E, name)(*args, **kw)
            if tok is not None:
                ins.then_inc(tok[0].h, 1)
        self.ops[eng].append(f)
        return tok

    def dma(self, eng, sem, out, in_, **kw):
        sem.n += 16
        tok = (sem, sem.n)

        def f(E, out=out, in_=in_, kw=kw, tok=tok):
            E.dma_start(out=out, in_=in_, **kw).then_inc(tok[0].h, 16)
        self.ops[eng].append(f)
        return tok


DBG = {}

def build_nc(upto=99):
    dbg_sp = DBG.get("sp", 3)
    dbg_eps = DBG.get("eps", KVH)
    dbg_bias = DBG.get("bias", True)
    dbg_epi = DBG.get("epi", True)
    dbg_m = DBG.get("m", True)
    nc = bass.Bass("TRN2", target_bir_lowering=False, debug=False)

    x_my = nc.dram_tensor("x_my", [TOK, D], F32, kind="ExternalInput").ap()
    x_ot = nc.dram_tensor("x_ot", [TOK, D], F32, kind="ExternalInput").ap()
    xT = nc.dram_tensor("xT", [D, TOK], F32, kind="ExternalInput").ap()
    wkv_t = nc.dram_tensor("wkv_t", [KVH, 128, DC * 256], F32, kind="ExternalInput").ap()
    wq_t = nc.dram_tensor("wq_t", [HEADS, 128, DC * 128], F32, kind="ExternalInput").ap()
    wo_t = nc.dram_tensor("wo_t", [OC, 128, DC * 128], F32, kind="ExternalInput").ap()
    w13_t = nc.dram_tensor("w13_t", [NHC, 2, 128, DC * 128], F32, kind="ExternalInput").ap()
    w2_t = nc.dram_tensor("w2_t", [OC, 128, NHC * 128], F32, kind="ExternalInput").ap()
    b13_t = nc.dram_tensor("b13_t", [NHC, R, 256], F32, kind="ExternalInput").ap()
    b2_t = nc.dram_tensor("b2_t", [OC, R, 128], F32, kind="ExternalInput").ap()
    a1_t = nc.dram_tensor("a1_t", [128, DC * R], F32, kind="ExternalInput").ap()
    a3_t = nc.dram_tensor("a3_t", [128, DC * R], F32, kind="ExternalInput").ap()
    a2_t = nc.dram_tensor("a2_t", [128, NHC * R], F32, kind="ExternalInput").ap()
    bq_p = nc.dram_tensor("bq_p", [128, HEADS], F32, kind="ExternalInput").ap()
    bo_p = nc.dram_tensor("bo_p", [128, OC], F32, kind="ExternalInput").ap()
    bkv_p = nc.dram_tensor("bkv_p", [1, KVH * 256], F32, kind="ExternalInput").ap()
    id_p = nc.dram_tensor("id_p", [128, 128], F32, kind="ExternalInput").ap()
    out = nc.dram_tensor("out", [D, TOK], F32, kind="ExternalOutput").ap()

    xn_scr = nc.dram_tensor("xn_scr", [L, D], BF16, kind="Internal").ap()
    acc_scr = [nc.dram_tensor(f"acc{i}", [D, TOK], F32, kind="Internal").ap()
               for i in range(2)]

    xnT_my = nc.alloc_sbuf_tensor("xnT_my", [128, DC * TOK], BF16)
    buf_B = nc.alloc_sbuf_tensor("buf_B", [128, DC * TOK], BF16)
    wring = nc.alloc_sbuf_tensor("wring", [128, 16384], BF16)
    fstage = nc.alloc_sbuf_tensor("fstage", [128, 4096], F32)
    estage = nc.alloc_sbuf_tensor("estage", [128, 2048], F32)
    qbf = nc.alloc_sbuf_tensor("qbf", [128, 1024], BF16)
    M_bf = nc.alloc_sbuf_tensor("M_bf", [128, KVH * 128], BF16)
    bkv_r = nc.alloc_sbuf_tensor("bkv_r", [1, 512], BF16)
    b13r = nc.alloc_sbuf_tensor("b13r", [R, 512], BF16)
    b2r = nc.alloc_sbuf_tensor("b2r", [R, 256], BF16)
    a1_sb = nc.alloc_sbuf_tensor("a1_sb", [128, DC * R], BF16)
    a3_sb = nc.alloc_sbuf_tensor("a3_sb", [128, DC * R], BF16)
    a2r = nc.alloc_sbuf_tensor("a2r", [128, 2 * 22 * R], BF16)
    xa1_bf = nc.alloc_sbuf_tensor("xa1_bf", [R, TOK], BF16)
    xa2s_bf = xa1_bf
    xa3_bf = nc.alloc_sbuf_tensor("xa3_bf", [R, TOK], BF16)
    xa2_acc = nc.alloc_sbuf_tensor("xa2_acc", [R, TOK], BF16)
    bq_sb = nc.alloc_sbuf_tensor("bq_sb", [128, HEADS], F32)
    bq1_sb = nc.alloc_sbuf_tensor("bq1_sb", [128, HEADS], F32)
    bo_sb = nc.alloc_sbuf_tensor("bo_sb", [128, OC], F32)
    stats = nc.alloc_sbuf_tensor("stats", [128, 16], F32)
    nrm = nc.alloc_sbuf_tensor("nrm", [128, 16], F32)
    id_sb = nc.alloc_sbuf_tensor("id_sb", [128, 128], F32)
    ones_r = nc.alloc_sbuf_tensor("ones_r", [1, 128], BF16)
    ones_c = nc.alloc_sbuf_tensor("ones_c", [128, 2], BF16)

    assert nc.sbuf_bytes_remaining >= 0, nc.sbuf_bytes_remaining

    ps = [nc.alloc_psum_tensor(f"ps{i}", [128, 512], F32) for i in range(8)]

    P = Prog()
    sT = SemTok(nc, "sT")
    sV = SemTok(nc, "sV")
    sS = SemTok(nc, "sS")
    sIni = SemTok(nc, "sIni")
    sWs = [SemTok(nc, f"sW{i}") for i in range(4)]
    sFs = [SemTok(nc, f"sF{i}") for i in range(2)]
    sTr = SemTok(nc, "sTr")
    sScr = SemTok(nc, "sScr")
    sAc = [SemTok(nc, f"sAc{i}") for i in range(2)]
    sB13 = [SemTok(nc, f"sB13{i}") for i in range(2)]
    sB2 = [SemTok(nc, f"sB2{i}") for i in range(2)]
    sKb = [SemTok(nc, f"sKb{i}") for i in range(2)]
    sA2 = [SemTok(nc, f"sA2{i}") for i in range(2)]

    MM = dict(skip_group_check=True)
    bank_free = [[] for _ in range(8)]

    def tnow():
        return (sT, sT.n)

    def vnow():
        return (sV, sV.n)

    def snow():
        return (sS, sS.n)

    for dst, src in ((bq_sb[:], bq_p), (bo_sb[:], bo_p), (id_sb[:], id_p),
                     (a1_sb[:], a1_t), (a3_sb[:], a3_t)):
        P.dma("G", sIni, dst, src)
    tok_init = (sIni, sIni.n)
    P.op("V", "memset", ones_r[:], 1.0)
    P.op("V", "memset", ones_c[:], 1.0)
    P.wait("V", tok_init)
    tok_const = P.op("V", "tensor_scalar_add", bq1_sb[:], bq_sb[:], 1.0, inc=sV)

    xslot_free = [None, None]
    dump_free = None
    xn_gate = None
    tp_free = [[], []]
    tok_xnT = None
    xnT_v = xnT_my[:].rearrange("p (dc t) -> p dc t", t=TOK)
    for t in range(16):
        src = x_my if t < 8 else x_ot
        rows = (t % 8) * 128
        htoks = [None, None]
        for half in range(2):
            P.wait("Y", xslot_free[half])
            htoks[half] = P.dma("Y", sFs[half],
                                fstage[:, half * 2048:(half + 1) * 2048],
                                src[rows:rows + 128, half * 2048:(half + 1) * 2048])
        for half in range(2):
            xh = fstage[:, half * 2048:(half + 1) * 2048]
            P.wait("V", htoks[half])
            P.op("V", "reduce_sum", stats[:, half:half + 1], xh, axis=AX.X)
            P.waits("S", [htoks[half], dump_free, xn_gate])
            tq = P.op("S", "activation", estage[:], xh, AF.Square, inc=sS)
            P.wait("V", tq)
            dump_free = P.op("V", "reduce_sum", stats[:, 2 + half:3 + half],
                             estage[:], axis=AX.X, inc=sV)
        P.op("V", "drain")
        P.op("V", "tensor_add", stats[:, 4:5], stats[:, 0:1], stats[:, 1:2])
        P.op("V", "tensor_add", stats[:, 5:6], stats[:, 2:3], stats[:, 3:4])
        P.op("V", "drain")
        P.op("V", "tensor_scalar_mul", stats[:, 6:7], stats[:, 4:5], 1.0 / D)
        P.op("V", "drain")
        P.op("V", "tensor_mul", stats[:, 7:8], stats[:, 6:7], stats[:, 6:7])
        P.op("V", "drain")
        P.op("V", "scalar_tensor_tensor", stats[:, 8:9], stats[:, 5:6], 1.0 / D,
             stats[:, 7:8], ALU.mult, ALU.subtract)
        P.op("V", "drain")
        P.op("V", "tensor_scalar_add", stats[:, 9:10], stats[:, 8:9], EPS_LN)
        P.op("V", "drain")
        tr = P.op("V", "reciprocal", stats[:, 10:11], stats[:, 9:10], inc=sV)
        P.wait("S", tr)
        trs = P.op("S", "activation", stats[:, 11:12], stats[:, 10:11], AF.Sqrt,
                   inc=sS)
        P.wait("V", trs)
        tmb = P.op("V", "tensor_scalar", stats[:, 12:13], stats[:, 6:7],
                   stats[:, 11:12], -1.0, ALU.mult, ALU.mult, inc=sV)
        P.op("S", "drain")
        P.wait("S", tmb)
        for half in range(2):
            xh = fstage[:, half * 2048:(half + 1) * 2048]
            P.wait("S", xn_gate)
            tn = P.op("S", "activation", estage[:], xh, AF.Identity,
                      bias=stats[:, 12:13], scale=stats[:, 11:12], inc=sS)
            xslot_free[half] = tn
            if t < 8:
                for grp in range(4):
                    bk = ps[grp % 2]
                    P.waits("T", [tn, tok_init, tok_const] + tp_free[grp % 2])
                    for j in range(4):
                        cc = grp * 4 + j
                        ttok = P.op("T", "transpose", bk[:, j * 128:(j + 1) * 128],
                                    estage[:, cc * 128:(cc + 1) * 128], id_sb[:],
                                    inc=(sT if j == 3 else None))
                    dc0 = half * 16 + grp * 4
                    P.wait("V", ttok)
                    vtok = P.op("V", "tensor_copy",
                                xnT_v[:, dc0:dc0 + 4, t * 128:(t + 1) * 128],
                                bk[:, 0:512].rearrange("p (c x) -> p c x", x=128),
                                inc=sV)
                    tp_free[grp % 2] = [vtok]
                xn_gate = ttok
                tok_xnT = vtok
            else:
                P.wait("G", tn)
                xn_gate = P.dma("G", sScr,
                                xn_scr[t * 128:(t + 1) * 128,
                                       half * 2048:(half + 1) * 2048], estage[:])

    def debug_finish(taps=(), dram_taps=()):
        P.wait("V", (sS, sS.n))
        P.wait("V", (sT, sT.n))
        P.op("V", "memset", fstage[:, 2048:3072], 0.0, inc=sV)
        P.wait("Y", (sV, sV.n))
        for oc_ in range(OC):
            P.dma("Y", sAc[0], out[oc_ * 128:(oc_ + 1) * 128, :],
                  fstage[:, 2048:3072])
        P.wait("Y", (sAc[0], sAc[0].n))
        for ap, row in taps:
            ncols = ap.shape[-1]
            pdim = ap.shape[0]
            P.wait("V", (sAc[0], sAc[0].n))
            tv_ = P.op("V", "tensor_copy", fstage[0:pdim, 2048:2048 + ncols], ap,
                       inc=sV)
            P.wait("Y", tv_)
            P.dma("Y", sAc[0], out[row:row + pdim, 0:ncols],
                  fstage[0:pdim, 2048:2048 + ncols])
            P.wait("Y", (sAc[0], sAc[0].n))
        for ap, row in dram_taps:
            ncols = ap.shape[-1]
            P.wait("G", (sAc[0], sAc[0].n))
            tg_ = P.dma("G", sIni, fstage[:, 2048:2048 + ncols], ap)
            P.wait("Y", tg_)
            P.dma("Y", sAc[0], out[row:row + 128, 0:ncols],
                  fstage[:, 2048:2048 + ncols])
            P.wait("Y", (sAc[0], sAc[0].n))
        emit(nc, P)
        return nc

    if upto <= 0:
        return debug_finish()


    ring_free = [None, None, None, None]
    ring_i = [0]

    def ring_load(src_ap):
        s = ring_i[0] % 4
        ring_i[0] += 1
        P.wait("G", ring_free[s])
        ncols = src_ap.shape[-1]
        dst = wring[:, s * 4096:s * 4096 + ncols]
        tok = P.dma("G", sWs[s], dst, src_ap)
        return s, dst, tok

    def ring_mark(s):
        ring_free[s] = tnow()

    SP_TILES = [list(range(8)), [8, 9, 10, 11], [12, 13, 14, 15]]
    xnst_free = (sS, sS.n)
    tp_free = [[], []]
    bkv_free = [None, None]
    est_free = [[snow(), tnow()], [snow(), tnow()]]
    kvst_free = [[], []]
    it = 0
    tok_oq = None
    pend_M = None

    def emit_M(kv_, deps, bankno, mcol, ncol, stM, stN, spmm, pg_):
        P.waits("T", deps)
        P.op("T", "matmul", ps[bankno][:, mcol:mcol + 128], kv_[:, 0:128],
             kv_[:, 128:256], start=stM, stop=spmm, **MM)
        tok = P.op("T", "matmul", ps[4][:, ncol:ncol + 2], kv_[:, 0:128],
                   ones_c[:], start=stN, stop=spmm, inc=sT, **MM)
        kvst_free[pg_] = [tok]
        return tok

    last_sp = dbg_sp - 1
    for sp in range(dbg_sp):
        if sp > 0:
            P.wait("G", (sScr, sScr.n))
            rows0 = TOK + (sp - 1) * 512
            bufB_v = buf_B[:].rearrange("p (dc t) -> p dc t", t=512)
            for tl in range(4):
                for half in range(2):
                    P.wait("G", xnst_free)
                    ld = P.dma("G", sTr, fstage[:, 0:2048],
                               xn_scr[rows0 + tl * 128:rows0 + (tl + 1) * 128,
                                      half * 2048:(half + 1) * 2048])
                    for grp in range(4):
                        bk = ps[5 + grp % 2]
                        P.waits("T", [ld] + tp_free[grp % 2])
                        for j in range(4):
                            cc = grp * 4 + j
                            ttok = P.op("T", "transpose",
                                        bk[:, j * 128:(j + 1) * 128],
                                        fstage[:, cc * 128:(cc + 1) * 128],
                                        id_sb[:], inc=(sT if j == 3 else None))
                        dc0 = half * 16 + grp * 4
                        P.wait("V", ttok)
                        vtok = P.op("V", "tensor_copy",
                                    bufB_v[:, dc0:dc0 + 4,
                                           tl * 128:(tl + 1) * 128],
                                    bk[:, 0:512].rearrange("p (c x) -> p c x",
                                                           x=128), inc=sV)
                        tp_free[grp % 2] = [vtok]
                    xnst_free = ttok
            tok_oq = vtok
        for ep in range(dbg_eps):
            sA, wA, tok_wA = ring_load(wkv_t[ep][:, 0:4096])
            sB, wB, tok_wB = ring_load(wkv_t[ep][:, 4096:8192])
            kpar = (sp * KVH + ep) % 2
            P.wait("G", bkv_free[kpar])
            tok_bkv = P.dma("G", sKb[kpar], bkv_r[0:1, kpar * 256:(kpar + 1) * 256],
                            bkv_p[0:1, ep * 256:(ep + 1) * 256])
            for t in SP_TILES[sp]:
                pg = it % 2
                kb = ps[pg]
                P.waits("T", [tok_wA, tok_wB, tok_bkv, tok_init] + bank_free[pg])
                if sp == 0:
                    P.wait("T", tok_xnT)
                    lhs, cw, toff = xnT_my, TOK, t * 128
                else:
                    P.wait("T", tok_oq)
                    lhs, cw, toff = buf_B, 512, (t - 8 - (sp - 1) * 4) * 128
                P.wait("T", tok_const)
                for dc in range(DC):
                    wsrc = wA if dc < 16 else wB
                    wcol = (dc % 16) * 256
                    fin_kv = (dc == 31) and not dbg_bias
                    tok_kv = P.op("T", "matmul", kb[:, 0:256],
                                  lhs[:, dc * cw + toff:dc * cw + toff + 128],
                                  wsrc[:, wcol:wcol + 256], start=(dc == 0),
                                  stop=fin_kv, inc=(sT if fin_kv else None), **MM)
                if dbg_bias:
                    tok_kv = P.op("T", "matmul", kb[:, 0:256], ones_r[0:1, :],
                                  bkv_r[0:1, kpar * 256:(kpar + 1) * 256],
                                  start=False, stop=True, inc=sT, **MM)
                if not dbg_epi:
                    bank_free[pg] = []
                    it += 1
                    tok_nm = tok_kv
                    continue
                eA = estage[:, pg * 512:pg * 512 + 128]
                eB = estage[:, pg * 512 + 128:pg * 512 + 256]
                eC = estage[:, pg * 512 + 256:pg * 512 + 384]
                kv = qbf[:, pg * 512:pg * 512 + 256]
                P.waits("V", [tok_kv] + est_free[pg] + kvst_free[pg])
                tm = P.op("V", "tensor_scalar_min", eA, kb[:, 0:128], 0.0, inc=sV)
                tvc = P.op("V", "tensor_copy", kv[:, 128:256], kb[:, 128:256],
                           inc=sV)
                P.wait("S", tm)
                te = P.op("S", "activation", eB, eA, AF.Exp, inc=sS)
                tu = P.op("V", "tensor_scalar_add", eC, kb[:, 0:128], 1.0, inc=sV)
                P.op("V", "drain")
                P.wait("V", te)
                tkb = P.op("V", "tensor_tensor", kv[:, 0:128], eC, eB, ALU.max,
                           inc=sV)
                est_free[pg] = [tkb, te]
                bank_free[pg] = [tu, tvc]
                stM = (sp == 0 and t == 0 and ep % 4 == 0)
                stN = (sp == 0 and t == 0 and ep == 0)
                spmm = (sp == last_sp and t == SP_TILES[sp][-1])
                if pend_M is not None:
                    tok_nm = emit_M(*pend_M)
                pend_M = (kv, [tkb, tvc], 2 + ep // 4, (ep % 4) * 128,
                          4 * ep, stM, stN, spmm, pg)
                it += 1
            ring_mark(sA)
            ring_mark(sB)
            bkv_free[kpar] = tnow()
    tok_nm = emit_M(*pend_M)
    pend_M = None
    tok_M_last = tok_nm
    P.wait("V", tok_M_last)
    P.op("V", "tensor_copy", M_bf[:, 0:512], ps[2][:, 0:512])
    P.op("V", "tensor_copy", M_bf[:, 512:1024], ps[3][:, 0:512])
    for e_ in range(KVH):
        P.op("V", "tensor_scalar_add", nrm[:, 8 + e_:9 + e_],
             ps[4][:, 4 * e_:4 * e_ + 1], EPS_ATTN)
    P.op("V", "drain")
    tok_M = P.op("V", "reciprocal", nrm[:, 0:8], nrm[:, 8:16], inc=sV)
    for i in (2, 3, 4):
        bank_free[i] = [tok_M]

    if upto <= 1:
        return debug_finish(taps=[
            (xnT_my[:, 0:1024], 0),
            (xnT_my[:, 5 * 1024:6 * 1024], 128),
            (M_bf[:, 0:1024], 256),
            (nrm[:, 0:16], 384),
            (qbf[:, 0:1024], 512),
            (stats[:, 0:16], 768),
        ], dram_taps=[
            (xn_scr[0:128, 0:1024], 640),
            (xn_scr[1920:2048, 0:1024], 896),
        ])

    attnT = buf_B
    est2_free = [[snow()], [snow()]]
    qbf_free = [kvst_free[0] + [snow()], kvst_free[1] + [snow()]]
    tok_attnT = None
    for h in range(HEADS):
        s, wdst, tok_w = ring_load(wq_t[h])
        b0, b1 = (h % 2) * 2, (h % 2) * 2 + 1
        P.waits("T", [tok_w, tok_xnT] + bank_free[b0] + bank_free[b1])
        tok_q = None
        for dc in range(DC):
            st, fin = dc == 0, dc == 31
            P.op("T", "matmul", ps[b0][:], wdst[:, dc * 128:(dc + 1) * 128],
                 xnT_my[:, dc * TOK:dc * TOK + 512], start=st, stop=fin, **MM)
            tk = P.op("T", "matmul", ps[b1][:], wdst[:, dc * 128:(dc + 1) * 128],
                      xnT_my[:, dc * TOK + 512:(dc + 1) * TOK], start=st,
                      stop=fin, inc=(sT if fin else None), **MM)
            if fin:
                tok_q = tk
        ring_mark(s)
        for half in range(2):
            qb = ps[b0 + half]
            pg = (h * 2 + half) % 2
            eA = estage[:, pg * 1024:pg * 1024 + 512]
            eB = estage[:, pg * 1024 + 512:(pg + 1) * 1024]
            qslot = qbf[:, pg * 512:(pg + 1) * 512]
            P.waits("V", [tok_q, tok_const] + est2_free[pg])
            tm = P.op("V", "tensor_scalar", eA, qb[:], bq_sb[:, h:h + 1], 0.0,
                      ALU.add, ALU.min, inc=sV)
            P.wait("S", tm)
            te = P.op("S", "activation", eB, eA, AF.Exp, inc=sS)
            P.wait("V", te)
            tu = P.op("V", "tensor_scalar_add", eA, qb[:], bq1_sb[:, h:h + 1],
                      inc=sV)
            P.op("V", "drain")
            P.waits("V", qbf_free[pg])
            tqb = P.op("V", "tensor_tensor", qslot, eA, eB, ALU.max, inc=sV)
            est2_free[pg] = [tqb, te]
            bank_free[b0 + half] = [tu]
            ab = 4 + (h % 2) * 2 + half
            kvh = h // 4
            P.waits("T", [tqb, tok_M] + bank_free[ab])
            tok_a = P.op("T", "matmul", ps[ab][:],
                         M_bf[:, kvh * 128:(kvh + 1) * 128], qslot,
                         start=True, stop=True, inc=sT, **MM)
            qbf_free[pg] = [tok_a]
            P.wait("V", tok_a)
            tok_attnT = P.op("V", "tensor_scalar_mul",
                             attnT[:, h * TOK + half * 512:
                                   h * TOK + (half + 1) * 512],
                             ps[ab][:], nrm[:, kvh:kvh + 1], inc=sV)
            bank_free[ab] = [tok_attnT]

    if upto <= 2:
        return debug_finish()

    xT_free = [[snow(), vnow(), tnow()], [snow(), vnow(), tnow()]]
    out_free = [[], []]
    for oc in range(OC):
        s, wdst, tok_w = ring_load(wo_t[oc])
        pgx = oc % 2
        P.waits("Y", xT_free[pgx])
        tok_xT = P.dma("Y", sFs[pgx], fstage[:, pgx * 1024:(pgx + 1) * 1024],
                       xT[oc * 128:(oc + 1) * 128, :])
        b0, b1 = (oc % 2) * 2, (oc % 2) * 2 + 1
        P.waits("T", [tok_w, tok_attnT] + bank_free[b0] + bank_free[b1])
        tok_wo = None
        for fc in range(DC):
            st, fin = fc == 0, fc == 31
            P.op("T", "matmul", ps[b0][:], wdst[:, fc * 128:(fc + 1) * 128],
                 attnT[:, fc * TOK:fc * TOK + 512], start=st, stop=fin, **MM)
            tk = P.op("T", "matmul", ps[b1][:], wdst[:, fc * 128:(fc + 1) * 128],
                      attnT[:, fc * TOK + 512:(fc + 1) * TOK], start=st,
                      stop=fin, inc=(sT if fin else None), **MM)
            if fin:
                tok_wo = tk
        ring_mark(s)
        P.waits("V", [tok_wo, tok_xT] + out_free[pgx])
        tv = None
        for half in range(2):
            tv = P.op("V", "scalar_tensor_tensor",
                      fstage[:, 2048 + pgx * 1024 + half * 512:
                             2048 + pgx * 1024 + (half + 1) * 512],
                      ps[b0 + half][:], bo_sb[:, oc:oc + 1],
                      fstage[:, pgx * 1024 + half * 512:
                             pgx * 1024 + (half + 1) * 512],
                      ALU.add, ALU.add, inc=sV)
        xT_free[pgx] = [tv]
        bank_free[b0] = [tv]
        bank_free[b1] = [tv]
        P.wait("Y", tv)
        wt = P.dma("Y", sAc[pgx], acc_scr[0][oc * 128:(oc + 1) * 128, :],
                   fstage[:, 2048 + pgx * 1024:2048 + (pgx + 1) * 1024])
        out_free[pgx] = [wt]
    tok_accs_done = [(sAc[0], sAc[0].n), (sAc[1], sAc[1].n)]

    if upto <= 3:
        return debug_finish()

    for (asb, xout) in ((a1_sb, xa1_bf), (a3_sb, xa3_bf)):
        P.waits("T", [tok_init] + bank_free[4] + bank_free[5])
        tok_xa = None
        for dc in range(DC):
            st, fin = dc == 0, dc == 31
            P.op("T", "matmul", ps[4][0:R, :], asb[:, dc * R:(dc + 1) * R],
                 xnT_my[:, dc * TOK:dc * TOK + 512], start=st, stop=fin, **MM)
            tk = P.op("T", "matmul", ps[5][0:R, :], asb[:, dc * R:(dc + 1) * R],
                      xnT_my[:, dc * TOK + 512:(dc + 1) * TOK], start=st,
                      stop=fin, inc=(sT if fin else None), **MM)
            if fin:
                tok_xa = tk
        P.wait("V", tok_xa)
        P.op("V", "tensor_scalar_mul", xout[:, 0:512], ps[4][0:R, :], 2.0)
        tva = P.op("V", "tensor_scalar_mul", xout[:, 512:1024], ps[5][0:R, :],
                   2.0, inc=sV)
        bank_free[4] = [tva]
        bank_free[5] = [tva]
    tok_xa13 = tva

    if upto <= 4:
        return debug_finish()

    g_sb = buf_B
    b13_free = [None, None]
    b2_free = [None, None]
    a2r_free = [None, None]
    sig_free = [[vnow()], [vnow()]]
    acc_rd_free = [xT_free[0], xT_free[1]]
    tok_g_banks = {}
    tok_xa2s = None
    for p, (h0, h1) in enumerate(FFN_SPLITS):
        n = h1 - h0
        pga = p % 2
        P.wait("G", a2r_free[pga])
        tok_a2 = P.dma("G", sA2[pga], a2r[:, pga * 22 * R:pga * 22 * R + n * R],
                       a2_t[:, h0 * R:h1 * R])
        for hc in range(h0, h1):
            s1, w1d, tok_w1 = ring_load(w13_t[hc, 0])
            s3, w3d, tok_w3 = ring_load(w13_t[hc, 1])
            pgb = hc % 2
            P.wait("G", b13_free[pgb])
            tok_b13 = P.dma("G", sB13[pgb], b13r[:, pgb * 256:(pgb + 1) * 256],
                            b13_t[hc])
            bb = [pgb * 4, pgb * 4 + 1, pgb * 4 + 2, pgb * 4 + 3]
            P.waits("T", [tok_w1, tok_w3, tok_b13, tok_xa13]
                    + bank_free[bb[0]] + bank_free[bb[1]]
                    + bank_free[bb[2]] + bank_free[bb[3]])
            for dc in range(DC):
                st = dc == 0
                P.op("T", "matmul", ps[bb[0]][:], w1d[:, dc * 128:(dc + 1) * 128],
                     xnT_my[:, dc * TOK:dc * TOK + 512], start=st, stop=False, **MM)
                P.op("T", "matmul", ps[bb[1]][:], w1d[:, dc * 128:(dc + 1) * 128],
                     xnT_my[:, dc * TOK + 512:(dc + 1) * TOK], start=st,
                     stop=False, **MM)
                P.op("T", "matmul", ps[bb[2]][:], w3d[:, dc * 128:(dc + 1) * 128],
                     xnT_my[:, dc * TOK:dc * TOK + 512], start=st, stop=False, **MM)
                P.op("T", "matmul", ps[bb[3]][:], w3d[:, dc * 128:(dc + 1) * 128],
                     xnT_my[:, dc * TOK + 512:(dc + 1) * TOK], start=st,
                     stop=False, **MM)
            P.op("T", "matmul", ps[bb[0]][:], b13r[0:R, pgb * 256:pgb * 256 + 128],
                 xa1_bf[:, 0:512], start=False, stop=True, **MM)
            P.op("T", "matmul", ps[bb[1]][:], b13r[0:R, pgb * 256:pgb * 256 + 128],
                 xa1_bf[:, 512:1024], start=False, stop=True, **MM)
            P.op("T", "matmul", ps[bb[2]][:],
                 b13r[0:R, pgb * 256 + 128:pgb * 256 + 256],
                 xa3_bf[:, 0:512], start=False, stop=True, **MM)
            tok_h = P.op("T", "matmul", ps[bb[3]][:],
                         b13r[0:R, pgb * 256 + 128:pgb * 256 + 256],
                         xa3_bf[:, 512:1024], start=False, stop=True, inc=sT, **MM)
            b13_free[pgb] = tok_h
            ring_mark(s1)
            ring_mark(s3)
            P.waits("S", [tok_h] + sig_free[pgb])
            P.op("S", "activation", estage[:, pgb * 1024:pgb * 1024 + 512],
                 ps[bb[0]][:], AF.Silu)
            tok_sig = P.op("S", "activation",
                           estage[:, pgb * 1024 + 512:(pgb + 1) * 1024],
                           ps[bb[1]][:], AF.Silu, inc=sS)
            P.wait("V", tok_sig)
            P.op("V", "tensor_mul", g_sb[:, (hc - h0) * TOK:(hc - h0) * TOK + 512],
                 estage[:, pgb * 1024:pgb * 1024 + 512], ps[bb[2]][:])
            tok_g = P.op("V", "tensor_mul",
                         g_sb[:, (hc - h0) * TOK + 512:(hc - h0 + 1) * TOK],
                         estage[:, pgb * 1024 + 512:(pgb + 1) * 1024],
                         ps[bb[3]][:], inc=sV)
            sig_free[pgb] = [tok_g]
            bank_free[bb[0]] = [tok_sig]
            bank_free[bb[1]] = [tok_sig]
            bank_free[bb[2]] = [tok_g]
            bank_free[bb[3]] = [tok_g]
            tok_g_banks[hc] = tok_g
        P.waits("T", [tok_a2] + bank_free[0] + bank_free[1])
        tok_xa2 = None
        for hc in range(h0, h1):
            P.wait("T", tok_g_banks[hc])
            st, fin = hc == h0, hc == h1 - 1
            acol = pga * 22 * R + (hc - h0) * R
            P.op("T", "matmul", ps[0][0:R, :], a2r[:, acol:acol + R],
                 g_sb[:, (hc - h0) * TOK:(hc - h0) * TOK + 512],
                 start=st, stop=fin, **MM)
            tk = P.op("T", "matmul", ps[1][0:R, :], a2r[:, acol:acol + R],
                      g_sb[:, (hc - h0) * TOK + 512:(hc - h0 + 1) * TOK],
                      start=st, stop=fin, inc=(sT if fin else None), **MM)
            if fin:
                tok_xa2 = tk
        a2r_free[pga] = tok_xa2
        P.wait("V", tok_xa2)
        if p == 0:
            P.op("V", "tensor_copy", xa2_acc[:, 0:512], ps[0][0:R, :])
            tx2 = P.op("V", "tensor_copy", xa2_acc[:, 512:1024], ps[1][0:R, :],
                       inc=sV)
        else:
            P.op("V", "tensor_add", xa2_acc[:, 0:512], xa2_acc[:, 0:512],
                 ps[0][0:R, :])
            tx2 = P.op("V", "tensor_add", xa2_acc[:, 512:1024],
                       xa2_acc[:, 512:1024], ps[1][0:R, :], inc=sV)
        bank_free[0] = [tx2]
        bank_free[1] = [tx2]
        if p == 3:
            P.op("V", "drain")
            P.op("V", "tensor_scalar_mul", xa2s_bf[:, 0:512], xa2_acc[:, 0:512],
                 2.0)
            tok_xa2s = P.op("V", "tensor_scalar_mul", xa2s_bf[:, 512:1024],
                            xa2_acc[:, 512:1024], 2.0, inc=sV)
        asrc = acc_scr[0] if p % 2 == 0 else acc_scr[1]
        adst = acc_scr[1] if p % 2 == 0 else acc_scr[0]
        P.waits("Y", tok_accs_done)
        acc_wr0 = acc_wr1 = None
        for oc in range(OC):
            s, wdst, tok_w = ring_load(w2_t[oc][:, h0 * 128:h1 * 128])
            pgx = oc % 2
            P.waits("Y", acc_rd_free[pgx])
            tok_ar = P.dma("Y", sFs[pgx], fstage[:, pgx * 1024:(pgx + 1) * 1024],
                           asrc[oc * 128:(oc + 1) * 128, :])
            b0, b1 = 2 + (oc % 2) * 2, 3 + (oc % 2) * 2
            P.waits("T", [tok_w] + bank_free[b0] + bank_free[b1])
            tok_w2 = None
            for hc in range(h0, h1):
                st = hc == h0
                fin = (hc == h1 - 1) and p != 3
                P.op("T", "matmul", ps[b0][:],
                     wdst[:, (hc - h0) * 128:(hc - h0 + 1) * 128],
                     g_sb[:, (hc - h0) * TOK:(hc - h0) * TOK + 512],
                     start=st, stop=fin, **MM)
                tk = P.op("T", "matmul", ps[b1][:],
                          wdst[:, (hc - h0) * 128:(hc - h0 + 1) * 128],
                          g_sb[:, (hc - h0) * TOK + 512:(hc - h0 + 1) * TOK],
                          start=st, stop=fin, inc=(sT if fin else None), **MM)
                if fin:
                    tok_w2 = tk
            if p == 3:
                pgc = oc % 2
                P.wait("G", b2_free[pgc])
                tok_b2 = P.dma("G", sB2[pgc], b2r[:, pgc * 128:(pgc + 1) * 128],
                               b2_t[oc])
                P.waits("T", [tok_b2, tok_xa2s])
                P.op("T", "matmul", ps[b0][:], b2r[0:R, pgc * 128:(pgc + 1) * 128],
                     xa2s_bf[:, 0:512], start=False, stop=True, **MM)
                tok_w2 = P.op("T", "matmul", ps[b1][:],
                              b2r[0:R, pgc * 128:(pgc + 1) * 128],
                              xa2s_bf[:, 512:1024], start=False, stop=True,
                              inc=sT, **MM)
                b2_free[pgc] = tok_w2
            ring_mark(s)
            P.waits("V", [tok_w2, tok_ar] + out_free[pgx])
            tv = None
            for half in range(2):
                tv = P.op("V", "tensor_add",
                          fstage[:, 2048 + pgx * 1024 + half * 512:
                                 2048 + pgx * 1024 + (half + 1) * 512],
                          ps[b0 + half][:],
                          fstage[:, pgx * 1024 + half * 512:
                                 pgx * 1024 + (half + 1) * 512], inc=sV)
            acc_rd_free[pgx] = [tv]
            bank_free[b0] = [tv]
            bank_free[b1] = [tv]
            P.wait("Y", tv)
            dst = out if p == 3 else adst
            wt = P.dma("Y", sAc[pgx], dst[oc * 128:(oc + 1) * 128, :],
                       fstage[:, 2048 + pgx * 1024:2048 + (pgx + 1) * 1024])
            out_free[pgx] = [wt]
        tok_accs_done = [(sAc[0], sAc[0].n), (sAc[1], sAc[1].n)]

    P.waits("Y", tok_accs_done)
    emit(nc, P)
    return nc


def emit(nc, P):
    with nc.Block() as blk:
        @blk.tensor
        def _(E):
            for f in P.ops["T"]:
                f(E)

        @blk.vector
        def _(E):
            for f in P.ops["V"]:
                f(E)

        @blk.scalar
        def _(E):
            for f in P.ops["S"]:
                f(E)

        @blk.gpsimd
        def _(E):
            for f in P.ops["G"]:
                f(E)

        @blk.sync
        def _(E):
            for f in P.ops["Y"]:
                f(E)


def _prep_tapes(Wq, bq, Wk, bk, Wv, bv, Wo, bo, w1, w3, w2,
                A1, B1, A3, B3, A2, B2):
    C = np.ascontiguousarray
    f = np.float32

    def img(W):
        o, d = W.shape
        ocs, dcs = o // 128, d // 128
        return C(W.reshape(ocs, 128, dcs, 128).transpose(0, 3, 2, 1)
                 .reshape(ocs, 128, dcs * 128)).astype(f)

    wq_t = img(Wq)
    wo_t = img(Wo)
    w13_t = C(np.stack([img(w1), img(w3)], axis=1))
    w2_t = img(w2)
    wk4 = Wk.reshape(KVH, 128, DC, 128).transpose(0, 3, 2, 1)
    wv4 = Wv.reshape(KVH, 128, DC, 128).transpose(0, 3, 2, 1)
    wkv_t = C(np.concatenate([wk4, wv4], axis=3).reshape(KVH, 128, DC * 256)).astype(f)
    b13_t = C(np.concatenate([B1.reshape(NHC, 128, R).transpose(0, 2, 1),
                              B3.reshape(NHC, 128, R).transpose(0, 2, 1)],
                             axis=2)).astype(f)
    b2_t = C(B2.reshape(OC, 128, R).transpose(0, 2, 1)).astype(f)
    a1_t = C(A1.reshape(R, DC, 128).transpose(2, 1, 0).reshape(128, DC * R)).astype(f)
    a3_t = C(A3.reshape(R, DC, 128).transpose(2, 1, 0).reshape(128, DC * R)).astype(f)
    a2_t = C(A2.reshape(R, NHC, 128).transpose(2, 1, 0).reshape(128, NHC * R)).astype(f)
    bq_p = C(bq.reshape(HEADS, 128).T).astype(f)
    bo_p = C(bo.reshape(OC, 128).T).astype(f)
    bkv_p = C(np.stack([bk.reshape(KVH, 128), bv.reshape(KVH, 128)], axis=1)
              .reshape(1, KVH * 256)).astype(f)
    return dict(wq_t=wq_t, wo_t=wo_t, w13_t=w13_t, w2_t=w2_t, wkv_t=wkv_t,
                b13_t=b13_t, b2_t=b2_t, a1_t=a1_t, a3_t=a3_t, a2_t=a2_t,
                bq_p=bq_p, bo_p=bo_p, bkv_p=bkv_p,
                id_p=np.eye(128, dtype=np.float32))


def kernel(L0_attn_hidden_states, L0_attn_Wq, L0_attn_bq, L0_attn_Wk, L0_attn_bk,
           L0_attn_Wv, L0_attn_bv, L0_attn_Wo, L0_attn_bo, L0_ffn_w1, L0_ffn_w3,
           L0_ffn_w2, L0_ffn_lora_A1, L0_ffn_lora_B1, L0_ffn_lora_A3,
           L0_ffn_lora_B3, L0_ffn_lora_A2, L0_ffn_lora_B2):
    x = np.ascontiguousarray(np.asarray(L0_attn_hidden_states, dtype=np.float32))
    a = lambda v: np.asarray(v, np.float32)
    tapes = _prep_tapes(a(L0_attn_Wq), a(L0_attn_bq), a(L0_attn_Wk), a(L0_attn_bk),
                        a(L0_attn_Wv), a(L0_attn_bv), a(L0_attn_Wo), a(L0_attn_bo),
                        a(L0_ffn_w1), a(L0_ffn_w3), a(L0_ffn_w2),
                        a(L0_ffn_lora_A1), a(L0_ffn_lora_B1), a(L0_ffn_lora_A3),
                        a(L0_ffn_lora_B3), a(L0_ffn_lora_A2), a(L0_ffn_lora_B2))

    nc = build_nc()
    in_maps = []
    for c in range(8):
        b, h = c // 2, c % 2
        xm = np.ascontiguousarray(x[b, h * TOK:(h + 1) * TOK, :])
        xo = np.ascontiguousarray(x[b, (1 - h) * TOK:(2 - h) * TOK, :])
        xt = np.ascontiguousarray(xm.T)
        in_maps.append(dict(x_my=xm, x_ot=xo, xT=xt, **tapes))

    res = run_bass_kernel_spmd(nc, in_maps, core_ids=list(range(8)))
    outp = np.empty((B, L, D), np.float32)
    for c in range(8):
        b, h = c // 2, c % 2
        outp[b, h * TOK:(h + 1) * TOK, :] = res.results[c]["out"].T
    return outp


# version 06
# speedup vs baseline: 1.2305x; 1.1692x over previous
import numpy as np
import concourse.bass as bass
import concourse.mybir as mybir
from concourse.bass_utils import run_bass_kernel_spmd

F32 = mybir.dt.float32
BF16 = mybir.dt.bfloat16
AF = mybir.ActivationFunctionType
ALU = mybir.AluOpType
AX = mybir.AxisListType

B, L, D = 4, 2048, 4096
TOK = 1024
DC = 32
HID, NHC = 11008, 86
OC = 32
HEADS, KVH = 32, 8
R = 16
FFN_SPLITS = [(0, 22), (22, 44), (44, 65), (65, 86)]
EPS_LN = 1e-5
EPS_ATTN = 1e-10


class SemTok:
    def __init__(self, nc, name):
        self.h = nc.alloc_semaphore(name)
        self.n = 0


class Prog:
    ENGS = ("T", "V", "S", "G", "Y")

    def __init__(self):
        self.ops = {e: [] for e in self.ENGS}
        self.last_wait = {e: {} for e in self.ENGS}

    def wait(self, eng, tok):
        if tok is None:
            return
        sem, val = tok
        lw = self.last_wait[eng]
        if lw.get(id(sem), -1) >= val:
            return
        lw[id(sem)] = val
        self.ops[eng].append(lambda E, s=sem.h, v=val: E.wait_ge(s, v))

    def waits(self, eng, toks):
        for t in toks:
            self.wait(eng, t)

    def op(self, eng, name, *args, inc=None, **kw):
        tok = None
        if inc is not None:
            inc.n += 1
            tok = (inc, inc.n)

        def f(E, name=name, args=args, kw=kw, tok=tok):
            ins = getattr(E, name)(*args, **kw)
            if tok is not None:
                ins.then_inc(tok[0].h, 1)
        self.ops[eng].append(f)
        return tok

    def dma(self, eng, sem, out, in_, **kw):
        sem.n += 16
        tok = (sem, sem.n)

        def f(E, out=out, in_=in_, kw=kw, tok=tok):
            E.dma_start(out=out, in_=in_, **kw).then_inc(tok[0].h, 16)
        self.ops[eng].append(f)
        return tok


DBG = {}

def build_nc(upto=99):
    dbg_sp = DBG.get("sp", 3)
    dbg_eps = DBG.get("eps", KVH)
    dbg_bias = DBG.get("bias", True)
    dbg_epi = DBG.get("epi", True)
    dbg_m = DBG.get("m", True)
    nc = bass.Bass("TRN2", target_bir_lowering=False, debug=False)

    x_my = nc.dram_tensor("x_my", [TOK, D], F32, kind="ExternalInput").ap()
    xT = nc.dram_tensor("xT", [D, TOK], F32, kind="ExternalInput").ap()
    wkv_t = nc.dram_tensor("wkv_t", [KVH, 128, DC * 256], F32, kind="ExternalInput").ap()
    wq_t = nc.dram_tensor("wq_t", [HEADS, 128, DC * 128], F32, kind="ExternalInput").ap()
    wo_t = nc.dram_tensor("wo_t", [OC, 128, DC * 128], F32, kind="ExternalInput").ap()
    w13_t = nc.dram_tensor("w13_t", [NHC, 2, 128, DC * 128], F32, kind="ExternalInput").ap()
    w2_t = nc.dram_tensor("w2_t", [OC, 128, NHC * 128], F32, kind="ExternalInput").ap()
    b13_t = nc.dram_tensor("b13_t", [NHC, R, 256], F32, kind="ExternalInput").ap()
    b2_t = nc.dram_tensor("b2_t", [OC, R, 128], F32, kind="ExternalInput").ap()
    a1_t = nc.dram_tensor("a1_t", [128, DC * R], F32, kind="ExternalInput").ap()
    a3_t = nc.dram_tensor("a3_t", [128, DC * R], F32, kind="ExternalInput").ap()
    a2_t = nc.dram_tensor("a2_t", [128, NHC * R], F32, kind="ExternalInput").ap()
    bq_p = nc.dram_tensor("bq_p", [128, HEADS], F32, kind="ExternalInput").ap()
    bo_p = nc.dram_tensor("bo_p", [128, OC], F32, kind="ExternalInput").ap()
    bkv_p = nc.dram_tensor("bkv_p", [1, KVH * 256], F32, kind="ExternalInput").ap()
    id_p = nc.dram_tensor("id_p", [128, 128], F32, kind="ExternalInput").ap()
    out = nc.dram_tensor("out", [D, TOK], F32, kind="ExternalOutput").ap()

    cc_in = nc.dram_tensor("cc_in", [128, 1040], F32, kind="Internal").ap()
    cc_out = nc.dram_tensor("cc_out", [128, 1040], F32, kind="Internal").ap()
    acc_scr = [nc.dram_tensor(f"acc{i}", [D, TOK], F32, kind="Internal").ap()
               for i in range(2)]

    xnT_my = nc.alloc_sbuf_tensor("xnT_my", [128, DC * TOK], BF16)
    buf_B = nc.alloc_sbuf_tensor("buf_B", [128, DC * TOK], BF16)
    wring = nc.alloc_sbuf_tensor("wring", [128, 16384], BF16)
    fstage = nc.alloc_sbuf_tensor("fstage", [128, 4096], F32)
    estage = nc.alloc_sbuf_tensor("estage", [128, 2048], F32)
    qbf = nc.alloc_sbuf_tensor("qbf", [128, 1024], BF16)
    M_bf = nc.alloc_sbuf_tensor("M_bf", [128, KVH * 128], BF16)
    bkv_r = nc.alloc_sbuf_tensor("bkv_r", [1, 512], BF16)
    b13r = nc.alloc_sbuf_tensor("b13r", [R, 512], BF16)
    b2r = nc.alloc_sbuf_tensor("b2r", [R, 256], BF16)
    a1_sb = nc.alloc_sbuf_tensor("a1_sb", [128, DC * R], BF16)
    a3_sb = nc.alloc_sbuf_tensor("a3_sb", [128, DC * R], BF16)
    a2r = nc.alloc_sbuf_tensor("a2r", [128, 2 * 22 * R], BF16)
    xa1_bf = nc.alloc_sbuf_tensor("xa1_bf", [R, TOK], BF16)
    xa2s_bf = xa1_bf
    xa3_bf = nc.alloc_sbuf_tensor("xa3_bf", [R, TOK], BF16)
    xa2_acc = nc.alloc_sbuf_tensor("xa2_acc", [R, TOK], BF16)
    bq_sb = nc.alloc_sbuf_tensor("bq_sb", [128, HEADS], F32)
    bq1_sb = nc.alloc_sbuf_tensor("bq1_sb", [128, HEADS], F32)
    bo_sb = nc.alloc_sbuf_tensor("bo_sb", [128, OC], F32)
    stats = nc.alloc_sbuf_tensor("stats", [128, 16], F32)
    nrm = nc.alloc_sbuf_tensor("nrm", [128, 16], F32)
    id_sb = nc.alloc_sbuf_tensor("id_sb", [128, 128], F32)
    ones_r = nc.alloc_sbuf_tensor("ones_r", [1, 128], BF16)
    ones_c = nc.alloc_sbuf_tensor("ones_c", [128, 2], BF16)

    assert nc.sbuf_bytes_remaining >= 0, nc.sbuf_bytes_remaining

    ps = [nc.alloc_psum_tensor(f"ps{i}", [128, 512], F32) for i in range(8)]

    P = Prog()
    sT = SemTok(nc, "sT")
    sV = SemTok(nc, "sV")
    sS = SemTok(nc, "sS")
    sIni = SemTok(nc, "sIni")
    sWs = [SemTok(nc, f"sW{i}") for i in range(4)]
    sFs = [SemTok(nc, f"sF{i}") for i in range(2)]
    sTr = SemTok(nc, "sTr")
    sScr = SemTok(nc, "sScr")
    sAc = [SemTok(nc, f"sAc{i}") for i in range(2)]
    sB13 = [SemTok(nc, f"sB13{i}") for i in range(2)]
    sB2 = [SemTok(nc, f"sB2{i}") for i in range(2)]
    sKb = [SemTok(nc, f"sKb{i}") for i in range(2)]
    sA2 = [SemTok(nc, f"sA2{i}") for i in range(2)]

    MM = dict(skip_group_check=True)
    bank_free = [[] for _ in range(8)]

    def tnow():
        return (sT, sT.n)

    def vnow():
        return (sV, sV.n)

    def snow():
        return (sS, sS.n)

    for dst, src in ((bq_sb[:], bq_p), (bo_sb[:], bo_p), (id_sb[:], id_p),
                     (a1_sb[:], a1_t), (a3_sb[:], a3_t)):
        P.dma("G", sIni, dst, src)
    tok_init = (sIni, sIni.n)
    P.op("V", "memset", ones_r[:], 1.0)
    P.op("V", "memset", ones_c[:], 1.0)
    P.wait("V", tok_init)
    tok_const = P.op("V", "tensor_scalar_add", bq1_sb[:], bq_sb[:], 1.0, inc=sV)

    xslot_free = [None, None]
    dump_free = None
    xn_gate = None
    tp_free = [[], []]
    tok_xnT = None
    xnT_v = xnT_my[:].rearrange("p (dc t) -> p dc t", t=TOK)
    for t in range(8):
        src = x_my
        rows = (t % 8) * 128
        htoks = [None, None]
        for half in range(2):
            P.wait("Y", xslot_free[half])
            htoks[half] = P.dma("Y", sFs[half],
                                fstage[:, half * 2048:(half + 1) * 2048],
                                src[rows:rows + 128, half * 2048:(half + 1) * 2048])
        for half in range(2):
            xh = fstage[:, half * 2048:(half + 1) * 2048]
            P.wait("V", htoks[half])
            P.op("V", "reduce_sum", stats[:, half:half + 1], xh, axis=AX.X)
            P.waits("S", [htoks[half], dump_free, xn_gate])
            tq = P.op("S", "activation", estage[:], xh, AF.Square, inc=sS)
            P.wait("V", tq)
            dump_free = P.op("V", "reduce_sum", stats[:, 2 + half:3 + half],
                             estage[:], axis=AX.X, inc=sV)
        P.op("V", "drain")
        P.op("V", "tensor_add", stats[:, 4:5], stats[:, 0:1], stats[:, 1:2])
        P.op("V", "tensor_add", stats[:, 5:6], stats[:, 2:3], stats[:, 3:4])
        P.op("V", "drain")
        P.op("V", "tensor_scalar_mul", stats[:, 6:7], stats[:, 4:5], 1.0 / D)
        P.op("V", "drain")
        P.op("V", "tensor_mul", stats[:, 7:8], stats[:, 6:7], stats[:, 6:7])
        P.op("V", "drain")
        P.op("V", "scalar_tensor_tensor", stats[:, 8:9], stats[:, 5:6], 1.0 / D,
             stats[:, 7:8], ALU.mult, ALU.subtract)
        P.op("V", "drain")
        P.op("V", "tensor_scalar_add", stats[:, 9:10], stats[:, 8:9], EPS_LN)
        P.op("V", "drain")
        tr = P.op("V", "reciprocal", stats[:, 10:11], stats[:, 9:10], inc=sV)
        P.wait("S", tr)
        trs = P.op("S", "activation", stats[:, 11:12], stats[:, 10:11], AF.Sqrt,
                   inc=sS)
        P.wait("V", trs)
        tmb = P.op("V", "tensor_scalar", stats[:, 12:13], stats[:, 6:7],
                   stats[:, 11:12], -1.0, ALU.mult, ALU.mult, inc=sV)
        P.op("S", "drain")
        P.wait("S", tmb)
        for half in range(2):
            xh = fstage[:, half * 2048:(half + 1) * 2048]
            P.wait("S", xn_gate)
            tn = P.op("S", "activation", estage[:], xh, AF.Identity,
                      bias=stats[:, 12:13], scale=stats[:, 11:12], inc=sS)
            xslot_free[half] = tn
            if t < 8:
                for grp in range(4):
                    bk = ps[grp % 2]
                    P.waits("T", [tn, tok_init, tok_const] + tp_free[grp % 2])
                    for j in range(4):
                        cc = grp * 4 + j
                        ttok = P.op("T", "transpose", bk[:, j * 128:(j + 1) * 128],
                                    estage[:, cc * 128:(cc + 1) * 128], id_sb[:],
                                    inc=(sT if j == 3 else None))
                    dc0 = half * 16 + grp * 4
                    P.wait("V", ttok)
                    vtok = P.op("V", "tensor_copy",
                                xnT_v[:, dc0:dc0 + 4, t * 128:(t + 1) * 128],
                                bk[:, 0:512].rearrange("p (c x) -> p c x", x=128),
                                inc=sV)
                    tp_free[grp % 2] = [vtok]
                xn_gate = ttok
                tok_xnT = vtok

    def debug_finish(taps=(), dram_taps=()):
        P.wait("V", (sS, sS.n))
        P.wait("V", (sT, sT.n))
        P.op("V", "memset", fstage[:, 2048:3072], 0.0, inc=sV)
        P.wait("Y", (sV, sV.n))
        for oc_ in range(OC):
            P.dma("Y", sAc[0], out[oc_ * 128:(oc_ + 1) * 128, :],
                  fstage[:, 2048:3072])
        P.wait("Y", (sAc[0], sAc[0].n))
        for ap, row in taps:
            ncols = ap.shape[-1]
            pdim = ap.shape[0]
            P.wait("V", (sAc[0], sAc[0].n))
            tv_ = P.op("V", "tensor_copy", fstage[0:pdim, 2048:2048 + ncols], ap,
                       inc=sV)
            P.wait("Y", tv_)
            P.dma("Y", sAc[0], out[row:row + pdim, 0:ncols],
                  fstage[0:pdim, 2048:2048 + ncols])
            P.wait("Y", (sAc[0], sAc[0].n))
        for ap, row in dram_taps:
            ncols = ap.shape[-1]
            P.wait("G", (sAc[0], sAc[0].n))
            tg_ = P.dma("G", sIni, fstage[:, 2048:2048 + ncols], ap)
            P.wait("Y", tg_)
            P.dma("Y", sAc[0], out[row:row + 128, 0:ncols],
                  fstage[:, 2048:2048 + ncols])
            P.wait("Y", (sAc[0], sAc[0].n))
        emit(nc, P)
        return nc

    if upto <= 0:
        return debug_finish()


    ring_free = [None, None, None, None]
    ring_i = [0]

    def ring_load(src_ap):
        s = ring_i[0] % 4
        ring_i[0] += 1
        P.wait("G", ring_free[s])
        ncols = src_ap.shape[-1]
        dst = wring[:, s * 4096:s * 4096 + ncols]
        tok = P.dma("G", sWs[s], dst, src_ap)
        return s, dst, tok

    def ring_mark(s):
        ring_free[s] = tnow()

    SP_TILES = [list(range(8))]
    bkv_free = [None, None]
    est_free = [[snow(), tnow()], [snow(), tnow()]]
    kvst_free = [[], []]
    it = 0
    tok_oq = None
    pend_M = None

    def emit_M(kv_, deps, bankno, mcol, ncol, stM, stN, spmm, pg_):
        P.waits("T", deps)
        P.op("T", "matmul", ps[bankno][:, mcol:mcol + 128], kv_[:, 0:128],
             kv_[:, 128:256], start=stM, stop=spmm, **MM)
        tok = P.op("T", "matmul", ps[4][:, ncol:ncol + 2], kv_[:, 0:128],
                   ones_c[:], start=stN, stop=spmm, inc=sT, **MM)
        kvst_free[pg_] = [tok]
        return tok

    last_sp = 0
    for sp in range(1):
        for ep in range(dbg_eps):
            sA, wA, tok_wA = ring_load(wkv_t[ep][:, 0:4096])
            sB, wB, tok_wB = ring_load(wkv_t[ep][:, 4096:8192])
            kpar = (sp * KVH + ep) % 2
            P.wait("G", bkv_free[kpar])
            tok_bkv = P.dma("G", sKb[kpar], bkv_r[0:1, kpar * 256:(kpar + 1) * 256],
                            bkv_p[0:1, ep * 256:(ep + 1) * 256])
            for t in SP_TILES[sp]:
                pg = it % 2
                kb = ps[pg]
                P.waits("T", [tok_wA, tok_wB, tok_bkv, tok_init] + bank_free[pg])
                P.wait("T", tok_xnT)
                lhs, cw, toff = xnT_my, TOK, t * 128
                P.wait("T", tok_const)
                for dc in range(DC):
                    wsrc = wA if dc < 16 else wB
                    wcol = (dc % 16) * 256
                    fin_kv = (dc == 31) and not dbg_bias
                    tok_kv = P.op("T", "matmul", kb[:, 0:256],
                                  lhs[:, dc * cw + toff:dc * cw + toff + 128],
                                  wsrc[:, wcol:wcol + 256], start=(dc == 0),
                                  stop=fin_kv, inc=(sT if fin_kv else None), **MM)
                if dbg_bias:
                    tok_kv = P.op("T", "matmul", kb[:, 0:256], ones_r[0:1, :],
                                  bkv_r[0:1, kpar * 256:(kpar + 1) * 256],
                                  start=False, stop=True, inc=sT, **MM)
                if not dbg_epi:
                    bank_free[pg] = []
                    it += 1
                    tok_nm = tok_kv
                    continue
                eA = estage[:, pg * 512:pg * 512 + 128]
                eB = estage[:, pg * 512 + 128:pg * 512 + 256]
                eC = estage[:, pg * 512 + 256:pg * 512 + 384]
                kv = qbf[:, pg * 512:pg * 512 + 256]
                P.waits("V", [tok_kv] + est_free[pg] + kvst_free[pg])
                tm = P.op("V", "tensor_scalar_min", eA, kb[:, 0:128], 0.0, inc=sV)
                tvc = P.op("V", "tensor_copy", kv[:, 128:256], kb[:, 128:256],
                           inc=sV)
                P.wait("S", tm)
                te = P.op("S", "activation", eB, eA, AF.Exp, inc=sS)
                tu = P.op("V", "tensor_scalar_add", eC, kb[:, 0:128], 1.0, inc=sV)
                P.op("V", "drain")
                P.wait("V", te)
                tkb = P.op("V", "tensor_tensor", kv[:, 0:128], eC, eB, ALU.max,
                           inc=sV)
                est_free[pg] = [tkb, te]
                bank_free[pg] = [tu, tvc]
                stM = (sp == 0 and t == 0 and ep % 4 == 0)
                stN = (sp == 0 and t == 0 and ep == 0)
                spmm = (sp == last_sp and t == SP_TILES[sp][-1])
                if pend_M is not None:
                    tok_nm = emit_M(*pend_M)
                pend_M = (kv, [tkb, tvc], 2 + ep // 4, (ep % 4) * 128,
                          4 * ep, stM, stN, spmm, pg)
                it += 1
            ring_mark(sA)
            ring_mark(sB)
            bkv_free[kpar] = tnow()
    tok_nm = emit_M(*pend_M)
    pend_M = None
    tok_M_last = tok_nm
    P.wait("V", tok_M_last)
    P.op("V", "tensor_copy", fstage[:, 0:512], ps[2][:, 0:512])
    P.op("V", "tensor_copy", fstage[:, 512:1024], ps[3][:, 0:512])
    for e_ in range(KVH):
        P.op("V", "tensor_copy", fstage[:, 1024 + e_:1025 + e_],
             ps[4][:, 4 * e_:4 * e_ + 1])
    tok_mev = P.op("V", "drain", inc=sV)
    for i in (2, 3, 4):
        bank_free[i] = [tok_mev]
    P.wait("Y", tok_mev)
    tok_ccin = P.dma("Y", sScr, cc_in[:, 0:1032], fstage[:, 0:1032])
    P.wait("G", tok_ccin)
    tok_cc = P.op("G", "collective_compute", "AllReduce", ALU.add,
                  replica_groups=[[0, 1], [2, 3], [4, 5], [6, 7]],
                  ins=[cc_in], outs=[cc_out], inc=sIni)
    P.wait("Y", tok_cc)
    tok_ccout = P.dma("Y", sScr, fstage[:, 2048:3080], cc_out[:, 0:1032])
    P.wait("V", tok_ccout)
    P.op("V", "tensor_copy", M_bf[:, 0:512], fstage[:, 2048:2560])
    P.op("V", "tensor_copy", M_bf[:, 512:1024], fstage[:, 2560:3072])
    for e_ in range(KVH):
        P.op("V", "tensor_scalar_add", nrm[:, 8 + e_:9 + e_],
             fstage[:, 3072 + e_:3073 + e_], EPS_ATTN)
    P.op("V", "drain")
    tok_M = P.op("V", "reciprocal", nrm[:, 0:8], nrm[:, 8:16], inc=sV)

    if upto <= 1:
        return debug_finish(taps=[
            (xnT_my[:, 0:1024], 0),
            (xnT_my[:, 5 * 1024:6 * 1024], 128),
            (M_bf[:, 0:1024], 256),
            (nrm[:, 0:16], 384),
            (qbf[:, 0:1024], 512),
            (stats[:, 0:16], 768),
        ], dram_taps=[
            (xn_scr[0:128, 0:1024], 640),
            (xn_scr[1920:2048, 0:1024], 896),
        ])

    attnT = buf_B
    est2_free = [[snow()], [snow()]]
    qbf_free = [kvst_free[0] + [snow()], kvst_free[1] + [snow()]]
    tok_attnT = None
    for h in range(HEADS):
        s, wdst, tok_w = ring_load(wq_t[h])
        b0, b1 = (h % 2) * 2, (h % 2) * 2 + 1
        P.waits("T", [tok_w, tok_xnT] + bank_free[b0] + bank_free[b1])
        tok_q = None
        for dc in range(DC):
            st, fin = dc == 0, dc == 31
            P.op("T", "matmul", ps[b0][:], wdst[:, dc * 128:(dc + 1) * 128],
                 xnT_my[:, dc * TOK:dc * TOK + 512], start=st, stop=fin, **MM)
            tk = P.op("T", "matmul", ps[b1][:], wdst[:, dc * 128:(dc + 1) * 128],
                      xnT_my[:, dc * TOK + 512:(dc + 1) * TOK], start=st,
                      stop=fin, inc=(sT if fin else None), **MM)
            if fin:
                tok_q = tk
        ring_mark(s)
        for half in range(2):
            qb = ps[b0 + half]
            pg = (h * 2 + half) % 2
            eA = estage[:, pg * 1024:pg * 1024 + 512]
            eB = estage[:, pg * 1024 + 512:(pg + 1) * 1024]
            qslot = qbf[:, pg * 512:(pg + 1) * 512]
            P.waits("V", [tok_q, tok_const] + est2_free[pg])
            tm = P.op("V", "tensor_scalar", eA, qb[:], bq_sb[:, h:h + 1], 0.0,
                      ALU.add, ALU.min, inc=sV)
            P.wait("S", tm)
            te = P.op("S", "activation", eB, eA, AF.Exp, inc=sS)
            P.wait("V", te)
            tu = P.op("V", "tensor_scalar_add", eA, qb[:], bq1_sb[:, h:h + 1],
                      inc=sV)
            P.op("V", "drain")
            P.waits("V", qbf_free[pg])
            tqb = P.op("V", "tensor_tensor", qslot, eA, eB, ALU.max, inc=sV)
            est2_free[pg] = [tqb, te]
            bank_free[b0 + half] = [tu]
            ab = 4 + (h % 2) * 2 + half
            kvh = h // 4
            P.waits("T", [tqb, tok_M] + bank_free[ab])
            tok_a = P.op("T", "matmul", ps[ab][:],
                         M_bf[:, kvh * 128:(kvh + 1) * 128], qslot,
                         start=True, stop=True, inc=sT, **MM)
            qbf_free[pg] = [tok_a]
            P.wait("V", tok_a)
            tok_attnT = P.op("V", "tensor_scalar_mul",
                             attnT[:, h * TOK + half * 512:
                                   h * TOK + (half + 1) * 512],
                             ps[ab][:], nrm[:, kvh:kvh + 1], inc=sV)
            bank_free[ab] = [tok_attnT]

    if upto <= 2:
        return debug_finish()

    xT_free = [[snow(), vnow(), tnow(), (sScr, sScr.n)],
               [snow(), vnow(), tnow(), (sScr, sScr.n)]]
    out_free = [[], []]
    for oc in range(OC):
        s, wdst, tok_w = ring_load(wo_t[oc])
        pgx = oc % 2
        P.waits("Y", xT_free[pgx])
        tok_xT = P.dma("Y", sFs[pgx], fstage[:, pgx * 1024:(pgx + 1) * 1024],
                       xT[oc * 128:(oc + 1) * 128, :])
        b0, b1 = (oc % 2) * 2, (oc % 2) * 2 + 1
        P.waits("T", [tok_w, tok_attnT] + bank_free[b0] + bank_free[b1])
        tok_wo = None
        for fc in range(DC):
            st, fin = fc == 0, fc == 31
            P.op("T", "matmul", ps[b0][:], wdst[:, fc * 128:(fc + 1) * 128],
                 attnT[:, fc * TOK:fc * TOK + 512], start=st, stop=fin, **MM)
            tk = P.op("T", "matmul", ps[b1][:], wdst[:, fc * 128:(fc + 1) * 128],
                      attnT[:, fc * TOK + 512:(fc + 1) * TOK], start=st,
                      stop=fin, inc=(sT if fin else None), **MM)
            if fin:
                tok_wo = tk
        ring_mark(s)
        P.waits("V", [tok_wo, tok_xT] + out_free[pgx])
        tv = None
        for half in range(2):
            tv = P.op("V", "scalar_tensor_tensor",
                      fstage[:, 2048 + pgx * 1024 + half * 512:
                             2048 + pgx * 1024 + (half + 1) * 512],
                      ps[b0 + half][:], bo_sb[:, oc:oc + 1],
                      fstage[:, pgx * 1024 + half * 512:
                             pgx * 1024 + (half + 1) * 512],
                      ALU.add, ALU.add, inc=sV)
        xT_free[pgx] = [tv]
        bank_free[b0] = [tv]
        bank_free[b1] = [tv]
        P.wait("Y", tv)
        wt = P.dma("Y", sAc[pgx], acc_scr[0][oc * 128:(oc + 1) * 128, :],
                   fstage[:, 2048 + pgx * 1024:2048 + (pgx + 1) * 1024])
        out_free[pgx] = [wt]
    tok_accs_done = [(sAc[0], sAc[0].n), (sAc[1], sAc[1].n)]

    if upto <= 3:
        return debug_finish()

    for (asb, xout) in ((a1_sb, xa1_bf), (a3_sb, xa3_bf)):
        P.waits("T", [tok_init] + bank_free[4] + bank_free[5])
        tok_xa = None
        for dc in range(DC):
            st, fin = dc == 0, dc == 31
            P.op("T", "matmul", ps[4][0:R, :], asb[:, dc * R:(dc + 1) * R],
                 xnT_my[:, dc * TOK:dc * TOK + 512], start=st, stop=fin, **MM)
            tk = P.op("T", "matmul", ps[5][0:R, :], asb[:, dc * R:(dc + 1) * R],
                      xnT_my[:, dc * TOK + 512:(dc + 1) * TOK], start=st,
                      stop=fin, inc=(sT if fin else None), **MM)
            if fin:
                tok_xa = tk
        P.wait("V", tok_xa)
        P.op("V", "tensor_scalar_mul", xout[:, 0:512], ps[4][0:R, :], 2.0)
        tva = P.op("V", "tensor_scalar_mul", xout[:, 512:1024], ps[5][0:R, :],
                   2.0, inc=sV)
        bank_free[4] = [tva]
        bank_free[5] = [tva]
    tok_xa13 = tva

    if upto <= 4:
        return debug_finish()

    g_sb = buf_B
    b13_free = [None, None]
    b2_free = [None, None]
    a2r_free = [None, None]
    sig_free = [[vnow()], [vnow()]]
    acc_rd_free = [xT_free[0], xT_free[1]]
    tok_g_banks = {}
    tok_xa2s = None
    for p, (h0, h1) in enumerate(FFN_SPLITS):
        n = h1 - h0
        pga = p % 2
        P.wait("G", a2r_free[pga])
        tok_a2 = P.dma("G", sA2[pga], a2r[:, pga * 22 * R:pga * 22 * R + n * R],
                       a2_t[:, h0 * R:h1 * R])
        for hc in range(h0, h1):
            s1, w1d, tok_w1 = ring_load(w13_t[hc, 0])
            s3, w3d, tok_w3 = ring_load(w13_t[hc, 1])
            pgb = hc % 2
            P.wait("G", b13_free[pgb])
            tok_b13 = P.dma("G", sB13[pgb], b13r[:, pgb * 256:(pgb + 1) * 256],
                            b13_t[hc])
            bb = [pgb * 4, pgb * 4 + 1, pgb * 4 + 2, pgb * 4 + 3]
            P.waits("T", [tok_w1, tok_w3, tok_b13, tok_xa13]
                    + bank_free[bb[0]] + bank_free[bb[1]]
                    + bank_free[bb[2]] + bank_free[bb[3]])
            for dc in range(DC):
                st = dc == 0
                P.op("T", "matmul", ps[bb[0]][:], w1d[:, dc * 128:(dc + 1) * 128],
                     xnT_my[:, dc * TOK:dc * TOK + 512], start=st, stop=False, **MM)
                P.op("T", "matmul", ps[bb[1]][:], w1d[:, dc * 128:(dc + 1) * 128],
                     xnT_my[:, dc * TOK + 512:(dc + 1) * TOK], start=st,
                     stop=False, **MM)
                P.op("T", "matmul", ps[bb[2]][:], w3d[:, dc * 128:(dc + 1) * 128],
                     xnT_my[:, dc * TOK:dc * TOK + 512], start=st, stop=False, **MM)
                P.op("T", "matmul", ps[bb[3]][:], w3d[:, dc * 128:(dc + 1) * 128],
                     xnT_my[:, dc * TOK + 512:(dc + 1) * TOK], start=st,
                     stop=False, **MM)
            P.op("T", "matmul", ps[bb[0]][:], b13r[0:R, pgb * 256:pgb * 256 + 128],
                 xa1_bf[:, 0:512], start=False, stop=True, **MM)
            P.op("T", "matmul", ps[bb[1]][:], b13r[0:R, pgb * 256:pgb * 256 + 128],
                 xa1_bf[:, 512:1024], start=False, stop=True, **MM)
            P.op("T", "matmul", ps[bb[2]][:],
                 b13r[0:R, pgb * 256 + 128:pgb * 256 + 256],
                 xa3_bf[:, 0:512], start=False, stop=True, **MM)
            tok_h = P.op("T", "matmul", ps[bb[3]][:],
                         b13r[0:R, pgb * 256 + 128:pgb * 256 + 256],
                         xa3_bf[:, 512:1024], start=False, stop=True, inc=sT, **MM)
            b13_free[pgb] = tok_h
            ring_mark(s1)
            ring_mark(s3)
            P.waits("S", [tok_h] + sig_free[pgb])
            P.op("S", "activation", estage[:, pgb * 1024:pgb * 1024 + 512],
                 ps[bb[0]][:], AF.Silu)
            tok_sig = P.op("S", "activation",
                           estage[:, pgb * 1024 + 512:(pgb + 1) * 1024],
                           ps[bb[1]][:], AF.Silu, inc=sS)
            P.wait("V", tok_sig)
            P.op("V", "tensor_mul", g_sb[:, (hc - h0) * TOK:(hc - h0) * TOK + 512],
                 estage[:, pgb * 1024:pgb * 1024 + 512], ps[bb[2]][:])
            tok_g = P.op("V", "tensor_mul",
                         g_sb[:, (hc - h0) * TOK + 512:(hc - h0 + 1) * TOK],
                         estage[:, pgb * 1024 + 512:(pgb + 1) * 1024],
                         ps[bb[3]][:], inc=sV)
            sig_free[pgb] = [tok_g]
            bank_free[bb[0]] = [tok_sig]
            bank_free[bb[1]] = [tok_sig]
            bank_free[bb[2]] = [tok_g]
            bank_free[bb[3]] = [tok_g]
            tok_g_banks[hc] = tok_g
        P.waits("T", [tok_a2] + bank_free[0] + bank_free[1])
        tok_xa2 = None
        for hc in range(h0, h1):
            P.wait("T", tok_g_banks[hc])
            st, fin = hc == h0, hc == h1 - 1
            acol = pga * 22 * R + (hc - h0) * R
            P.op("T", "matmul", ps[0][0:R, :], a2r[:, acol:acol + R],
                 g_sb[:, (hc - h0) * TOK:(hc - h0) * TOK + 512],
                 start=st, stop=fin, **MM)
            tk = P.op("T", "matmul", ps[1][0:R, :], a2r[:, acol:acol + R],
                      g_sb[:, (hc - h0) * TOK + 512:(hc - h0 + 1) * TOK],
                      start=st, stop=fin, inc=(sT if fin else None), **MM)
            if fin:
                tok_xa2 = tk
        a2r_free[pga] = tok_xa2
        P.wait("V", tok_xa2)
        if p == 0:
            P.op("V", "tensor_copy", xa2_acc[:, 0:512], ps[0][0:R, :])
            tx2 = P.op("V", "tensor_copy", xa2_acc[:, 512:1024], ps[1][0:R, :],
                       inc=sV)
        else:
            P.op("V", "tensor_add", xa2_acc[:, 0:512], xa2_acc[:, 0:512],
                 ps[0][0:R, :])
            tx2 = P.op("V", "tensor_add", xa2_acc[:, 512:1024],
                       xa2_acc[:, 512:1024], ps[1][0:R, :], inc=sV)
        bank_free[0] = [tx2]
        bank_free[1] = [tx2]
        if p == 3:
            P.op("V", "drain")
            P.op("V", "tensor_scalar_mul", xa2s_bf[:, 0:512], xa2_acc[:, 0:512],
                 2.0)
            tok_xa2s = P.op("V", "tensor_scalar_mul", xa2s_bf[:, 512:1024],
                            xa2_acc[:, 512:1024], 2.0, inc=sV)
        asrc = acc_scr[0] if p % 2 == 0 else acc_scr[1]
        adst = acc_scr[1] if p % 2 == 0 else acc_scr[0]
        P.waits("Y", tok_accs_done)
        acc_wr0 = acc_wr1 = None
        for oc in range(OC):
            s, wdst, tok_w = ring_load(w2_t[oc][:, h0 * 128:h1 * 128])
            pgx = oc % 2
            P.waits("Y", acc_rd_free[pgx])
            tok_ar = P.dma("Y", sFs[pgx], fstage[:, pgx * 1024:(pgx + 1) * 1024],
                           asrc[oc * 128:(oc + 1) * 128, :])
            b0, b1 = 2 + (oc % 2) * 2, 3 + (oc % 2) * 2
            P.waits("T", [tok_w] + bank_free[b0] + bank_free[b1])
            tok_w2 = None
            for hc in range(h0, h1):
                st = hc == h0
                fin = (hc == h1 - 1) and p != 3
                P.op("T", "matmul", ps[b0][:],
                     wdst[:, (hc - h0) * 128:(hc - h0 + 1) * 128],
                     g_sb[:, (hc - h0) * TOK:(hc - h0) * TOK + 512],
                     start=st, stop=fin, **MM)
                tk = P.op("T", "matmul", ps[b1][:],
                          wdst[:, (hc - h0) * 128:(hc - h0 + 1) * 128],
                          g_sb[:, (hc - h0) * TOK + 512:(hc - h0 + 1) * TOK],
                          start=st, stop=fin, inc=(sT if fin else None), **MM)
                if fin:
                    tok_w2 = tk
            if p == 3:
                pgc = oc % 2
                P.wait("G", b2_free[pgc])
                tok_b2 = P.dma("G", sB2[pgc], b2r[:, pgc * 128:(pgc + 1) * 128],
                               b2_t[oc])
                P.waits("T", [tok_b2, tok_xa2s])
                P.op("T", "matmul", ps[b0][:], b2r[0:R, pgc * 128:(pgc + 1) * 128],
                     xa2s_bf[:, 0:512], start=False, stop=True, **MM)
                tok_w2 = P.op("T", "matmul", ps[b1][:],
                              b2r[0:R, pgc * 128:(pgc + 1) * 128],
                              xa2s_bf[:, 512:1024], start=False, stop=True,
                              inc=sT, **MM)
                b2_free[pgc] = tok_w2
            ring_mark(s)
            P.waits("V", [tok_w2, tok_ar] + out_free[pgx])
            tv = None
            for half in range(2):
                tv = P.op("V", "tensor_add",
                          fstage[:, 2048 + pgx * 1024 + half * 512:
                                 2048 + pgx * 1024 + (half + 1) * 512],
                          ps[b0 + half][:],
                          fstage[:, pgx * 1024 + half * 512:
                                 pgx * 1024 + (half + 1) * 512], inc=sV)
            acc_rd_free[pgx] = [tv]
            bank_free[b0] = [tv]
            bank_free[b1] = [tv]
            P.wait("Y", tv)
            dst = out if p == 3 else adst
            wt = P.dma("Y", sAc[pgx], dst[oc * 128:(oc + 1) * 128, :],
                       fstage[:, 2048 + pgx * 1024:2048 + (pgx + 1) * 1024])
            out_free[pgx] = [wt]
        tok_accs_done = [(sAc[0], sAc[0].n), (sAc[1], sAc[1].n)]

    P.waits("Y", tok_accs_done)
    emit(nc, P)
    return nc


def emit(nc, P):
    with nc.Block() as blk:
        @blk.tensor
        def _(E):
            for f in P.ops["T"]:
                f(E)

        @blk.vector
        def _(E):
            for f in P.ops["V"]:
                f(E)

        @blk.scalar
        def _(E):
            for f in P.ops["S"]:
                f(E)

        @blk.gpsimd
        def _(E):
            for f in P.ops["G"]:
                f(E)

        @blk.sync
        def _(E):
            for f in P.ops["Y"]:
                f(E)


def _prep_tapes(Wq, bq, Wk, bk, Wv, bv, Wo, bo, w1, w3, w2,
                A1, B1, A3, B3, A2, B2):
    C = np.ascontiguousarray
    f = np.float32

    def img(W):
        o, d = W.shape
        ocs, dcs = o // 128, d // 128
        return C(W.reshape(ocs, 128, dcs, 128).transpose(0, 3, 2, 1)
                 .reshape(ocs, 128, dcs * 128)).astype(f)

    wq_t = img(Wq)
    wo_t = img(Wo)
    w13_t = C(np.stack([img(w1), img(w3)], axis=1))
    w2_t = img(w2)
    wk4 = Wk.reshape(KVH, 128, DC, 128).transpose(0, 3, 2, 1)
    wv4 = Wv.reshape(KVH, 128, DC, 128).transpose(0, 3, 2, 1)
    wkv_t = C(np.concatenate([wk4, wv4], axis=3).reshape(KVH, 128, DC * 256)).astype(f)
    b13_t = C(np.concatenate([B1.reshape(NHC, 128, R).transpose(0, 2, 1),
                              B3.reshape(NHC, 128, R).transpose(0, 2, 1)],
                             axis=2)).astype(f)
    b2_t = C(B2.reshape(OC, 128, R).transpose(0, 2, 1)).astype(f)
    a1_t = C(A1.reshape(R, DC, 128).transpose(2, 1, 0).reshape(128, DC * R)).astype(f)
    a3_t = C(A3.reshape(R, DC, 128).transpose(2, 1, 0).reshape(128, DC * R)).astype(f)
    a2_t = C(A2.reshape(R, NHC, 128).transpose(2, 1, 0).reshape(128, NHC * R)).astype(f)
    bq_p = C(bq.reshape(HEADS, 128).T).astype(f)
    bo_p = C(bo.reshape(OC, 128).T).astype(f)
    bkv_p = C(np.stack([bk.reshape(KVH, 128), bv.reshape(KVH, 128)], axis=1)
              .reshape(1, KVH * 256)).astype(f)
    return dict(wq_t=wq_t, wo_t=wo_t, w13_t=w13_t, w2_t=w2_t, wkv_t=wkv_t,
                b13_t=b13_t, b2_t=b2_t, a1_t=a1_t, a3_t=a3_t, a2_t=a2_t,
                bq_p=bq_p, bo_p=bo_p, bkv_p=bkv_p,
                id_p=np.eye(128, dtype=np.float32))


def kernel(L0_attn_hidden_states, L0_attn_Wq, L0_attn_bq, L0_attn_Wk, L0_attn_bk,
           L0_attn_Wv, L0_attn_bv, L0_attn_Wo, L0_attn_bo, L0_ffn_w1, L0_ffn_w3,
           L0_ffn_w2, L0_ffn_lora_A1, L0_ffn_lora_B1, L0_ffn_lora_A3,
           L0_ffn_lora_B3, L0_ffn_lora_A2, L0_ffn_lora_B2):
    x = np.ascontiguousarray(np.asarray(L0_attn_hidden_states, dtype=np.float32))
    a = lambda v: np.asarray(v, np.float32)
    tapes = _prep_tapes(a(L0_attn_Wq), a(L0_attn_bq), a(L0_attn_Wk), a(L0_attn_bk),
                        a(L0_attn_Wv), a(L0_attn_bv), a(L0_attn_Wo), a(L0_attn_bo),
                        a(L0_ffn_w1), a(L0_ffn_w3), a(L0_ffn_w2),
                        a(L0_ffn_lora_A1), a(L0_ffn_lora_B1), a(L0_ffn_lora_A3),
                        a(L0_ffn_lora_B3), a(L0_ffn_lora_A2), a(L0_ffn_lora_B2))

    nc = build_nc()
    in_maps = []
    for c in range(8):
        b, h = c // 2, c % 2
        xm = np.ascontiguousarray(x[b, h * TOK:(h + 1) * TOK, :])
        xt = np.ascontiguousarray(xm.T)
        in_maps.append(dict(x_my=xm, xT=xt, **tapes))

    res = run_bass_kernel_spmd(nc, in_maps, core_ids=list(range(8)))
    outp = np.empty((B, L, D), np.float32)
    for c in range(8):
        b, h = c // 2, c % 2
        outp[b, h * TOK:(h + 1) * TOK, :] = res.results[c]["out"].T
    return outp


# version 07
# speedup vs baseline: 1.2984x; 1.0552x over previous
import numpy as np
import concourse.bass as bass
import concourse.mybir as mybir
from concourse.bass_utils import run_bass_kernel_spmd

F32 = mybir.dt.float32
BF16 = mybir.dt.bfloat16
AF = mybir.ActivationFunctionType
ALU = mybir.AluOpType
AX = mybir.AxisListType

B, L, D = 4, 2048, 4096
TOK = 1024
DC = 32
HID, NHC = 11008, 86
OC = 32
HEADS, KVH = 32, 8
R = 16
FFN_SPLITS = [(0, 22), (22, 44), (44, 65), (65, 86)]
EPS_LN = 1e-5
EPS_ATTN = 1e-10


class SemTok:
    def __init__(self, nc, name):
        self.h = nc.alloc_semaphore(name)
        self.n = 0


class Prog:
    ENGS = ("T", "V", "S", "G", "Y")

    def __init__(self):
        self.ops = {e: [] for e in self.ENGS}
        self.last_wait = {e: {} for e in self.ENGS}

    def wait(self, eng, tok):
        if tok is None:
            return
        sem, val = tok
        lw = self.last_wait[eng]
        if lw.get(id(sem), -1) >= val:
            return
        lw[id(sem)] = val
        self.ops[eng].append(lambda E, s=sem.h, v=val: E.wait_ge(s, v))

    def waits(self, eng, toks):
        for t in toks:
            self.wait(eng, t)

    def op(self, eng, name, *args, inc=None, **kw):
        tok = None
        if inc is not None:
            inc.n += 1
            tok = (inc, inc.n)

        def f(E, name=name, args=args, kw=kw, tok=tok):
            ins = getattr(E, name)(*args, **kw)
            if tok is not None:
                ins.then_inc(tok[0].h, 1)
        self.ops[eng].append(f)
        return tok

    def dma(self, eng, sem, out, in_, **kw):
        sem.n += 16
        tok = (sem, sem.n)

        def f(E, out=out, in_=in_, kw=kw, tok=tok):
            E.dma_start(out=out, in_=in_, **kw).then_inc(tok[0].h, 16)
        self.ops[eng].append(f)
        return tok


DBG = {}

def build_nc(upto=99):
    dbg_sp = DBG.get("sp", 3)
    dbg_eps = DBG.get("eps", KVH)
    dbg_bias = DBG.get("bias", True)
    dbg_epi = DBG.get("epi", True)
    dbg_m = DBG.get("m", True)
    nc = bass.Bass("TRN2", target_bir_lowering=False, debug=False)

    x_my = nc.dram_tensor("x_my", [TOK, D], F32, kind="ExternalInput").ap()
    xT = nc.dram_tensor("xT", [D, TOK], F32, kind="ExternalInput").ap()
    wkv_t = nc.dram_tensor("wkv_t", [KVH, 128, DC * 256], F32, kind="ExternalInput").ap()
    wq_t = nc.dram_tensor("wq_t", [HEADS, 128, DC * 128], F32, kind="ExternalInput").ap()
    wo_t = nc.dram_tensor("wo_t", [OC, 128, DC * 128], F32, kind="ExternalInput").ap()
    w13_t = nc.dram_tensor("w13_t", [NHC, 2, 128, DC * 128], F32, kind="ExternalInput").ap()
    w2_t = nc.dram_tensor("w2_t", [OC, 128, NHC * 128], F32, kind="ExternalInput").ap()
    b13_t = nc.dram_tensor("b13_t", [NHC, R, 256], F32, kind="ExternalInput").ap()
    b2_t = nc.dram_tensor("b2_t", [OC, R, 128], F32, kind="ExternalInput").ap()
    a1_t = nc.dram_tensor("a1_t", [128, DC * R], F32, kind="ExternalInput").ap()
    a3_t = nc.dram_tensor("a3_t", [128, DC * R], F32, kind="ExternalInput").ap()
    a2_t = nc.dram_tensor("a2_t", [128, NHC * R], F32, kind="ExternalInput").ap()
    bq_p = nc.dram_tensor("bq_p", [128, HEADS], F32, kind="ExternalInput").ap()
    bo_p = nc.dram_tensor("bo_p", [128, OC], F32, kind="ExternalInput").ap()
    bkv_p = nc.dram_tensor("bkv_p", [1, KVH * 256], F32, kind="ExternalInput").ap()
    id_p = nc.dram_tensor("id_p", [128, 128], F32, kind="ExternalInput").ap()
    out = nc.dram_tensor("out", [D, TOK], F32, kind="ExternalOutput").ap()

    cc_in = nc.dram_tensor("cc_in", [128, 1040], F32, kind="Internal").ap()
    cc_out = nc.dram_tensor("cc_out", [128, 1040], F32, kind="Internal").ap()
    acc_scr = [nc.dram_tensor(f"acc{i}", [D, TOK], F32, kind="Internal").ap()
               for i in range(2)]

    xnT_my = nc.alloc_sbuf_tensor("xnT_my", [128, DC * TOK], BF16)
    buf_B = nc.alloc_sbuf_tensor("buf_B", [128, DC * TOK], BF16)
    wring = nc.alloc_sbuf_tensor("wring", [128, 16384], BF16)
    fstage = nc.alloc_sbuf_tensor("fstage", [128, 4096], F32)
    estage = nc.alloc_sbuf_tensor("estage", [128, 2048], F32)
    qbf = nc.alloc_sbuf_tensor("qbf", [128, 1024], BF16)
    M_bf = nc.alloc_sbuf_tensor("M_bf", [128, KVH * 128], BF16)
    bkv_r = nc.alloc_sbuf_tensor("bkv_r", [1, 512], BF16)
    b13r = nc.alloc_sbuf_tensor("b13r", [R, 512], BF16)
    b2r = nc.alloc_sbuf_tensor("b2r", [R, 256], BF16)
    a1_sb = nc.alloc_sbuf_tensor("a1_sb", [128, DC * R], BF16)
    a3_sb = nc.alloc_sbuf_tensor("a3_sb", [128, DC * R], BF16)
    a2r = nc.alloc_sbuf_tensor("a2r", [128, 2 * 22 * R], BF16)
    xa1_bf = nc.alloc_sbuf_tensor("xa1_bf", [R, TOK], BF16)
    xa2s_bf = xa1_bf
    xa3_bf = nc.alloc_sbuf_tensor("xa3_bf", [R, TOK], BF16)
    xa2_acc = nc.alloc_sbuf_tensor("xa2_acc", [R, TOK], BF16)
    bq_sb = nc.alloc_sbuf_tensor("bq_sb", [128, HEADS], F32)
    bq1_sb = nc.alloc_sbuf_tensor("bq1_sb", [128, HEADS], F32)
    bo_sb = nc.alloc_sbuf_tensor("bo_sb", [128, OC], F32)
    stats = nc.alloc_sbuf_tensor("stats", [128, 16], F32)
    nrm = nc.alloc_sbuf_tensor("nrm", [128, 16], F32)
    id_sb = nc.alloc_sbuf_tensor("id_sb", [128, 128], F32)
    ones_r = nc.alloc_sbuf_tensor("ones_r", [1, 128], BF16)
    ones_c = nc.alloc_sbuf_tensor("ones_c", [128, 2], BF16)

    assert nc.sbuf_bytes_remaining >= 0, nc.sbuf_bytes_remaining

    ps = [nc.alloc_psum_tensor(f"ps{i}", [128, 512], F32) for i in range(8)]

    P = Prog()
    sT = SemTok(nc, "sT")
    sV = SemTok(nc, "sV")
    sS = SemTok(nc, "sS")
    sIni = SemTok(nc, "sIni")
    sWs = [SemTok(nc, f"sW{i}") for i in range(4)]
    sFs = [SemTok(nc, f"sF{i}") for i in range(2)]
    sTr = SemTok(nc, "sTr")
    sScr = SemTok(nc, "sScr")
    sAc = [SemTok(nc, f"sAc{i}") for i in range(2)]
    sB13 = [SemTok(nc, f"sB13{i}") for i in range(2)]
    sB2 = [SemTok(nc, f"sB2{i}") for i in range(2)]
    sKb = [SemTok(nc, f"sKb{i}") for i in range(2)]
    sA2 = [SemTok(nc, f"sA2{i}") for i in range(2)]

    MM = dict(skip_group_check=True)
    bank_free = [[] for _ in range(8)]

    def tnow():
        return (sT, sT.n)

    def vnow():
        return (sV, sV.n)

    def snow():
        return (sS, sS.n)

    for dst, src in ((bq_sb[:], bq_p), (bo_sb[:], bo_p), (id_sb[:], id_p),
                     (a1_sb[:], a1_t), (a3_sb[:], a3_t)):
        P.dma("G", sIni, dst, src)
    tok_init = (sIni, sIni.n)
    P.op("V", "memset", ones_r[:], 1.0)
    P.op("V", "memset", ones_c[:], 1.0)
    P.wait("V", tok_init)
    tok_const = P.op("V", "tensor_scalar_add", bq1_sb[:], bq_sb[:], 1.0, inc=sV)

    xslot_free = [None, None]
    dump_free = None
    xn_gate = None
    tp_free = [[], []]
    tok_xnT = None
    xnT_v = xnT_my[:].rearrange("p (dc t) -> p dc t", t=TOK)
    for t in range(8):
        src = x_my
        rows = (t % 8) * 128
        htoks = [None, None]
        for half in range(2):
            P.wait("Y", xslot_free[half])
            htoks[half] = P.dma("Y", sFs[half],
                                fstage[:, half * 2048:(half + 1) * 2048],
                                src[rows:rows + 128, half * 2048:(half + 1) * 2048])
        for half in range(2):
            xh = fstage[:, half * 2048:(half + 1) * 2048]
            P.wait("V", htoks[half])
            P.op("V", "reduce_sum", stats[:, half:half + 1], xh, axis=AX.X)
            P.waits("S", [htoks[half], dump_free, xn_gate])
            tq = P.op("S", "activation", estage[:], xh, AF.Square, inc=sS)
            P.wait("V", tq)
            dump_free = P.op("V", "reduce_sum", stats[:, 2 + half:3 + half],
                             estage[:], axis=AX.X, inc=sV)
        P.op("V", "drain")
        P.op("V", "tensor_add", stats[:, 4:5], stats[:, 0:1], stats[:, 1:2])
        P.op("V", "tensor_add", stats[:, 5:6], stats[:, 2:3], stats[:, 3:4])
        P.op("V", "drain")
        P.op("V", "tensor_scalar_mul", stats[:, 6:7], stats[:, 4:5], 1.0 / D)
        P.op("V", "drain")
        P.op("V", "tensor_mul", stats[:, 7:8], stats[:, 6:7], stats[:, 6:7])
        P.op("V", "drain")
        P.op("V", "scalar_tensor_tensor", stats[:, 8:9], stats[:, 5:6], 1.0 / D,
             stats[:, 7:8], ALU.mult, ALU.subtract)
        P.op("V", "drain")
        P.op("V", "tensor_scalar_add", stats[:, 9:10], stats[:, 8:9], EPS_LN)
        P.op("V", "drain")
        tr = P.op("V", "reciprocal", stats[:, 10:11], stats[:, 9:10], inc=sV)
        P.wait("S", tr)
        trs = P.op("S", "activation", stats[:, 11:12], stats[:, 10:11], AF.Sqrt,
                   inc=sS)
        P.wait("V", trs)
        tmb = P.op("V", "tensor_scalar", stats[:, 12:13], stats[:, 6:7],
                   stats[:, 11:12], -1.0, ALU.mult, ALU.mult, inc=sV)
        P.op("S", "drain")
        P.wait("S", tmb)
        for half in range(2):
            xh = fstage[:, half * 2048:(half + 1) * 2048]
            P.wait("S", xn_gate)
            tn = P.op("S", "activation", estage[:], xh, AF.Identity,
                      bias=stats[:, 12:13], scale=stats[:, 11:12], inc=sS)
            xslot_free[half] = tn
            if t < 8:
                for grp in range(4):
                    bk = ps[grp % 2]
                    P.waits("T", [tn, tok_init, tok_const] + tp_free[grp % 2])
                    for j in range(4):
                        cc = grp * 4 + j
                        ttok = P.op("T", "transpose", bk[:, j * 128:(j + 1) * 128],
                                    estage[:, cc * 128:(cc + 1) * 128], id_sb[:],
                                    inc=(sT if j == 3 else None))
                    dc0 = half * 16 + grp * 4
                    P.wait("V", ttok)
                    vtok = P.op("V", "tensor_copy",
                                xnT_v[:, dc0:dc0 + 4, t * 128:(t + 1) * 128],
                                bk[:, 0:512].rearrange("p (c x) -> p c x", x=128),
                                inc=sV)
                    tp_free[grp % 2] = [vtok]
                xn_gate = ttok
                tok_xnT = vtok

    def debug_finish(taps=(), dram_taps=()):
        P.wait("V", (sS, sS.n))
        P.wait("V", (sT, sT.n))
        P.op("V", "memset", fstage[:, 2048:3072], 0.0, inc=sV)
        P.wait("Y", (sV, sV.n))
        for oc_ in range(OC):
            P.dma("Y", sAc[0], out[oc_ * 128:(oc_ + 1) * 128, :],
                  fstage[:, 2048:3072])
        P.wait("Y", (sAc[0], sAc[0].n))
        for ap, row in taps:
            ncols = ap.shape[-1]
            pdim = ap.shape[0]
            P.wait("V", (sAc[0], sAc[0].n))
            tv_ = P.op("V", "tensor_copy", fstage[0:pdim, 2048:2048 + ncols], ap,
                       inc=sV)
            P.wait("Y", tv_)
            P.dma("Y", sAc[0], out[row:row + pdim, 0:ncols],
                  fstage[0:pdim, 2048:2048 + ncols])
            P.wait("Y", (sAc[0], sAc[0].n))
        for ap, row in dram_taps:
            ncols = ap.shape[-1]
            P.wait("G", (sAc[0], sAc[0].n))
            tg_ = P.dma("G", sIni, fstage[:, 2048:2048 + ncols], ap)
            P.wait("Y", tg_)
            P.dma("Y", sAc[0], out[row:row + 128, 0:ncols],
                  fstage[:, 2048:2048 + ncols])
            P.wait("Y", (sAc[0], sAc[0].n))
        emit(nc, P)
        return nc

    if upto <= 0:
        return debug_finish()


    ring_free = [None, None, None, None]
    ring_i = [0]

    def ring_load(src_ap):
        s = ring_i[0] % 4
        ring_i[0] += 1
        P.wait("G", ring_free[s])
        ncols = src_ap.shape[-1]
        dst = wring[:, s * 4096:s * 4096 + ncols]
        tok = P.dma("G", sWs[s], dst, src_ap)
        return s, dst, tok

    def ring_load2(src_ap):
        if ring_i[0] % 2:
            ring_i[0] += 1
        s = ring_i[0] % 4
        ring_i[0] += 2
        P.waits("G", [ring_free[s], ring_free[s + 1]])
        dst = wring[:, s * 4096:(s + 2) * 4096]
        din = dst if len(src_ap.shape) == 2 else dst.rearrange(
            "p (i c) -> p i c", c=src_ap.shape[-1])
        tok = P.dma("G", sWs[s], din, src_ap)
        return s, dst, tok

    def ring_mark(s):
        ring_free[s] = tnow()

    SP_TILES = [list(range(8))]
    bkv_free = [None, None]
    est_free = [[snow(), tnow()], [snow(), tnow()]]
    kvst_free = [[], []]
    it = 0
    tok_oq = None
    pend_M = None

    def emit_M(kv_, deps, bankno, mcol, ncol, stM, stN, spmm, pg_):
        P.waits("T", deps)
        P.op("T", "matmul", ps[bankno][:, mcol:mcol + 128], kv_[:, 0:128],
             kv_[:, 128:256], start=stM, stop=spmm, **MM)
        tok = P.op("T", "matmul", ps[4][:, ncol:ncol + 2], kv_[:, 0:128],
                   ones_c[:], start=stN, stop=spmm, inc=sT, **MM)
        kvst_free[pg_] = [tok]
        return tok

    last_sp = 0
    for sp in range(1):
        for ep in range(dbg_eps):
            sA, wAB, tok_wA = ring_load2(wkv_t[ep])
            sB = sA + 1
            wA, wB = wAB[:, 0:4096], wAB[:, 4096:8192]
            tok_wB = tok_wA
            kpar = (sp * KVH + ep) % 2
            P.wait("G", bkv_free[kpar])
            tok_bkv = P.dma("G", sKb[kpar], bkv_r[0:1, kpar * 256:(kpar + 1) * 256],
                            bkv_p[0:1, ep * 256:(ep + 1) * 256])
            for t in SP_TILES[sp]:
                pg = it % 2
                kb = ps[pg]
                P.waits("T", [tok_wA, tok_wB, tok_bkv, tok_init] + bank_free[pg])
                P.wait("T", tok_xnT)
                lhs, cw, toff = xnT_my, TOK, t * 128
                P.wait("T", tok_const)
                for dc in range(DC):
                    wsrc = wA if dc < 16 else wB
                    wcol = (dc % 16) * 256
                    fin_kv = (dc == 31) and not dbg_bias
                    tok_kv = P.op("T", "matmul", kb[:, 0:256],
                                  lhs[:, dc * cw + toff:dc * cw + toff + 128],
                                  wsrc[:, wcol:wcol + 256], start=(dc == 0),
                                  stop=fin_kv, inc=(sT if fin_kv else None), **MM)
                if dbg_bias:
                    tok_kv = P.op("T", "matmul", kb[:, 0:256], ones_r[0:1, :],
                                  bkv_r[0:1, kpar * 256:(kpar + 1) * 256],
                                  start=False, stop=True, inc=sT, **MM)
                if not dbg_epi:
                    bank_free[pg] = []
                    it += 1
                    tok_nm = tok_kv
                    continue
                eA = estage[:, pg * 512:pg * 512 + 128]
                eB = estage[:, pg * 512 + 128:pg * 512 + 256]
                eC = estage[:, pg * 512 + 256:pg * 512 + 384]
                kv = qbf[:, pg * 512:pg * 512 + 256]
                P.waits("V", [tok_kv] + est_free[pg] + kvst_free[pg])
                tm = P.op("V", "tensor_scalar_min", eA, kb[:, 0:128], 0.0, inc=sV)
                tvc = P.op("V", "tensor_copy", kv[:, 128:256], kb[:, 128:256],
                           inc=sV)
                P.wait("S", tm)
                te = P.op("S", "activation", eB, eA, AF.Exp, inc=sS)
                tu = P.op("V", "tensor_scalar_add", eC, kb[:, 0:128], 1.0, inc=sV)
                P.op("V", "drain")
                P.wait("V", te)
                tkb = P.op("V", "tensor_tensor", kv[:, 0:128], eC, eB, ALU.max,
                           inc=sV)
                est_free[pg] = [tkb, te]
                bank_free[pg] = [tu, tvc]
                stM = (sp == 0 and t == 0 and ep % 4 == 0)
                stN = (sp == 0 and t == 0 and ep == 0)
                spmm = (sp == last_sp and t == SP_TILES[sp][-1])
                if pend_M is not None:
                    tok_nm = emit_M(*pend_M)
                pend_M = (kv, [tkb, tvc], 2 + ep // 4, (ep % 4) * 128,
                          4 * ep, stM, stN, spmm, pg)
                it += 1
            ring_mark(sA)
            ring_mark(sB)
            bkv_free[kpar] = tnow()
    tok_nm = emit_M(*pend_M)
    pend_M = None
    tok_M_last = tok_nm
    P.wait("V", tok_M_last)
    P.op("V", "tensor_copy", fstage[:, 0:512], ps[2][:, 0:512])
    P.op("V", "tensor_copy", fstage[:, 512:1024], ps[3][:, 0:512])
    for e_ in range(KVH):
        P.op("V", "tensor_copy", fstage[:, 1024 + e_:1025 + e_],
             ps[4][:, 4 * e_:4 * e_ + 1])
    tok_mev = P.op("V", "drain", inc=sV)
    for i in (2, 3, 4):
        bank_free[i] = [tok_mev]
    P.wait("Y", tok_mev)
    tok_ccin = P.dma("Y", sScr, cc_in[:, 0:1032], fstage[:, 0:1032])
    P.wait("G", tok_ccin)
    tok_cc = P.op("G", "collective_compute", "AllReduce", ALU.add,
                  replica_groups=[[0, 1], [2, 3], [4, 5], [6, 7]],
                  ins=[cc_in], outs=[cc_out], inc=sIni)
    P.wait("Y", tok_cc)
    tok_ccout = P.dma("Y", sScr, fstage[:, 2048:3080], cc_out[:, 0:1032])
    P.wait("V", tok_ccout)
    P.op("V", "tensor_copy", M_bf[:, 0:512], fstage[:, 2048:2560])
    P.op("V", "tensor_copy", M_bf[:, 512:1024], fstage[:, 2560:3072])
    for e_ in range(KVH):
        P.op("V", "tensor_scalar_add", nrm[:, 8 + e_:9 + e_],
             fstage[:, 3072 + e_:3073 + e_], EPS_ATTN)
    P.op("V", "drain")
    tok_M = P.op("V", "reciprocal", nrm[:, 0:8], nrm[:, 8:16], inc=sV)

    if upto <= 1:
        return debug_finish(taps=[
            (xnT_my[:, 0:1024], 0),
            (xnT_my[:, 5 * 1024:6 * 1024], 128),
            (M_bf[:, 0:1024], 256),
            (nrm[:, 0:16], 384),
            (qbf[:, 0:1024], 512),
            (stats[:, 0:16], 768),
        ], dram_taps=[
            (xn_scr[0:128, 0:1024], 640),
            (xn_scr[1920:2048, 0:1024], 896),
        ])

    attnT = buf_B
    est2_free = [[snow()], [snow()]]
    qbf_free = [kvst_free[0] + [snow()], kvst_free[1] + [snow()]]
    tok_attnT = None
    pend_attn = []

    def emit_attn(qslot, tqb, h_, half_, pg_):
        ab = 4 + (h_ % 2) * 2 + half_
        kvh = h_ // 4
        P.waits("T", [tqb, tok_M] + bank_free[ab])
        tok_a = P.op("T", "matmul", ps[ab][:],
                     M_bf[:, kvh * 128:(kvh + 1) * 128], qslot,
                     start=True, stop=True, inc=sT, **MM)
        qbf_free[pg_] = [tok_a]
        P.wait("V", tok_a)
        tokA = P.op("V", "tensor_scalar_mul",
                    attnT[:, h_ * TOK + half_ * 512:h_ * TOK + (half_ + 1) * 512],
                    ps[ab][:], nrm[:, kvh:kvh + 1], inc=sV)
        bank_free[ab] = [tokA]
        return tokA

    for h in range(HEADS):
        s, wdst, tok_w = ring_load(wq_t[h])
        b0, b1 = (h % 2) * 2, (h % 2) * 2 + 1
        P.waits("T", [tok_w, tok_xnT] + bank_free[b0] + bank_free[b1])
        tok_q = None
        for dc in range(DC):
            st, fin = dc == 0, dc == 31
            P.op("T", "matmul", ps[b0][:], wdst[:, dc * 128:(dc + 1) * 128],
                 xnT_my[:, dc * TOK:dc * TOK + 512], start=st, stop=fin, **MM)
            tk = P.op("T", "matmul", ps[b1][:], wdst[:, dc * 128:(dc + 1) * 128],
                      xnT_my[:, dc * TOK + 512:(dc + 1) * TOK], start=st,
                      stop=fin, inc=(sT if fin else None), **MM)
            if fin:
                tok_q = tk
        ring_mark(s)
        for args in pend_attn:
            tok_attnT = emit_attn(*args)
        pend_attn = []
        for half in range(2):
            qb = ps[b0 + half]
            pg = (h * 2 + half) % 2
            eA = estage[:, pg * 1024:pg * 1024 + 512]
            eB = estage[:, pg * 1024 + 512:(pg + 1) * 1024]
            qslot = qbf[:, pg * 512:(pg + 1) * 512]
            P.waits("V", [tok_q, tok_const] + est2_free[pg])
            tm = P.op("V", "tensor_scalar", eA, qb[:], bq_sb[:, h:h + 1], 0.0,
                      ALU.add, ALU.min, inc=sV)
            P.wait("S", tm)
            te = P.op("S", "activation", eB, eA, AF.Exp, inc=sS)
            P.wait("V", te)
            tu = P.op("V", "tensor_scalar_add", eA, qb[:], bq1_sb[:, h:h + 1],
                      inc=sV)
            P.op("V", "drain")
            P.waits("V", qbf_free[pg])
            tqb = P.op("V", "tensor_tensor", qslot, eA, eB, ALU.max, inc=sV)
            est2_free[pg] = [tqb, te]
            bank_free[b0 + half] = [tu]
            pend_attn.append((qslot, tqb, h, half, pg))
    for args in pend_attn:
        tok_attnT = emit_attn(*args)
    pend_attn = []

    if upto <= 2:
        return debug_finish()

    xT_free = [[snow(), vnow(), tnow(), (sScr, sScr.n)],
               [snow(), vnow(), tnow(), (sScr, sScr.n)]]
    out_free = [[], []]
    for oc in range(OC):
        s, wdst, tok_w = ring_load(wo_t[oc])
        pgx = oc % 2
        P.waits("Y", xT_free[pgx])
        tok_xT = P.dma("Y", sFs[pgx], fstage[:, pgx * 1024:(pgx + 1) * 1024],
                       xT[oc * 128:(oc + 1) * 128, :])
        b0, b1 = (oc % 2) * 2, (oc % 2) * 2 + 1
        P.waits("T", [tok_w, tok_attnT] + bank_free[b0] + bank_free[b1])
        tok_wo = None
        for fc in range(DC):
            st, fin = fc == 0, fc == 31
            P.op("T", "matmul", ps[b0][:], wdst[:, fc * 128:(fc + 1) * 128],
                 attnT[:, fc * TOK:fc * TOK + 512], start=st, stop=fin, **MM)
            tk = P.op("T", "matmul", ps[b1][:], wdst[:, fc * 128:(fc + 1) * 128],
                      attnT[:, fc * TOK + 512:(fc + 1) * TOK], start=st,
                      stop=fin, inc=(sT if fin else None), **MM)
            if fin:
                tok_wo = tk
        ring_mark(s)
        P.waits("V", [tok_wo, tok_xT] + out_free[pgx])
        tv = None
        for half in range(2):
            tv = P.op("V", "scalar_tensor_tensor",
                      fstage[:, 2048 + pgx * 1024 + half * 512:
                             2048 + pgx * 1024 + (half + 1) * 512],
                      ps[b0 + half][:], bo_sb[:, oc:oc + 1],
                      fstage[:, pgx * 1024 + half * 512:
                             pgx * 1024 + (half + 1) * 512],
                      ALU.add, ALU.add, inc=sV)
        xT_free[pgx] = [tv]
        bank_free[b0] = [tv]
        bank_free[b1] = [tv]
        P.wait("Y", tv)
        wt = P.dma("Y", sAc[pgx], acc_scr[0][oc * 128:(oc + 1) * 128, :],
                   fstage[:, 2048 + pgx * 1024:2048 + (pgx + 1) * 1024])
        out_free[pgx] = [wt]
    tok_accs_done = [(sAc[0], sAc[0].n), (sAc[1], sAc[1].n)]

    if upto <= 3:
        return debug_finish()

    for (asb, xout) in ((a1_sb, xa1_bf), (a3_sb, xa3_bf)):
        P.waits("T", [tok_init] + bank_free[4] + bank_free[5])
        tok_xa = None
        for dc in range(DC):
            st, fin = dc == 0, dc == 31
            P.op("T", "matmul", ps[4][0:R, :], asb[:, dc * R:(dc + 1) * R],
                 xnT_my[:, dc * TOK:dc * TOK + 512], start=st, stop=fin, **MM)
            tk = P.op("T", "matmul", ps[5][0:R, :], asb[:, dc * R:(dc + 1) * R],
                      xnT_my[:, dc * TOK + 512:(dc + 1) * TOK], start=st,
                      stop=fin, inc=(sT if fin else None), **MM)
            if fin:
                tok_xa = tk
        P.wait("V", tok_xa)
        P.op("V", "tensor_scalar_mul", xout[:, 0:512], ps[4][0:R, :], 2.0)
        tva = P.op("V", "tensor_scalar_mul", xout[:, 512:1024], ps[5][0:R, :],
                   2.0, inc=sV)
        bank_free[4] = [tva]
        bank_free[5] = [tva]
    tok_xa13 = tva

    if upto <= 4:
        return debug_finish()

    g_sb = buf_B
    b13_free = [None, None]
    b2_free = [None, None]
    a2r_free = [None, None]
    sig_free = [[vnow()], [vnow()]]
    acc_rd_free = [xT_free[0], xT_free[1]]
    tok_g_banks = {}
    tok_xa2s = None
    for p, (h0, h1) in enumerate(FFN_SPLITS):
        n = h1 - h0
        pga = p % 2
        P.wait("G", a2r_free[pga])
        tok_a2 = P.dma("G", sA2[pga], a2r[:, pga * 22 * R:pga * 22 * R + n * R],
                       a2_t[:, h0 * R:h1 * R])
        for hc in range(h0, h1):
            s1, w13d, tok_w1 = ring_load2(w13_t[hc].rearrange("i p c -> p i c"))
            s3 = s1 + 1
            w1d, w3d = w13d[:, 0:4096], w13d[:, 4096:8192]
            tok_w3 = tok_w1
            pgb = hc % 2
            P.wait("G", b13_free[pgb])
            tok_b13 = P.dma("G", sB13[pgb], b13r[:, pgb * 256:(pgb + 1) * 256],
                            b13_t[hc])
            bb = [pgb * 4, pgb * 4 + 1, pgb * 4 + 2, pgb * 4 + 3]
            P.waits("T", [tok_w1, tok_w3, tok_b13, tok_xa13]
                    + bank_free[bb[0]] + bank_free[bb[1]]
                    + bank_free[bb[2]] + bank_free[bb[3]])
            for dc in range(DC):
                st = dc == 0
                P.op("T", "matmul", ps[bb[0]][:], w1d[:, dc * 128:(dc + 1) * 128],
                     xnT_my[:, dc * TOK:dc * TOK + 512], start=st, stop=False, **MM)
                P.op("T", "matmul", ps[bb[1]][:], w1d[:, dc * 128:(dc + 1) * 128],
                     xnT_my[:, dc * TOK + 512:(dc + 1) * TOK], start=st,
                     stop=False, **MM)
                P.op("T", "matmul", ps[bb[2]][:], w3d[:, dc * 128:(dc + 1) * 128],
                     xnT_my[:, dc * TOK:dc * TOK + 512], start=st, stop=False, **MM)
                P.op("T", "matmul", ps[bb[3]][:], w3d[:, dc * 128:(dc + 1) * 128],
                     xnT_my[:, dc * TOK + 512:(dc + 1) * TOK], start=st,
                     stop=False, **MM)
            P.op("T", "matmul", ps[bb[0]][:], b13r[0:R, pgb * 256:pgb * 256 + 128],
                 xa1_bf[:, 0:512], start=False, stop=True, **MM)
            P.op("T", "matmul", ps[bb[1]][:], b13r[0:R, pgb * 256:pgb * 256 + 128],
                 xa1_bf[:, 512:1024], start=False, stop=True, **MM)
            P.op("T", "matmul", ps[bb[2]][:],
                 b13r[0:R, pgb * 256 + 128:pgb * 256 + 256],
                 xa3_bf[:, 0:512], start=False, stop=True, **MM)
            tok_h = P.op("T", "matmul", ps[bb[3]][:],
                         b13r[0:R, pgb * 256 + 128:pgb * 256 + 256],
                         xa3_bf[:, 512:1024], start=False, stop=True, inc=sT, **MM)
            b13_free[pgb] = tok_h
            ring_mark(s1)
            ring_mark(s3)
            P.waits("S", [tok_h] + sig_free[pgb])
            P.op("S", "activation", estage[:, pgb * 1024:pgb * 1024 + 512],
                 ps[bb[0]][:], AF.Silu)
            tok_sig = P.op("S", "activation",
                           estage[:, pgb * 1024 + 512:(pgb + 1) * 1024],
                           ps[bb[1]][:], AF.Silu, inc=sS)
            P.wait("V", tok_sig)
            P.op("V", "tensor_mul", g_sb[:, (hc - h0) * TOK:(hc - h0) * TOK + 512],
                 estage[:, pgb * 1024:pgb * 1024 + 512], ps[bb[2]][:])
            tok_g = P.op("V", "tensor_mul",
                         g_sb[:, (hc - h0) * TOK + 512:(hc - h0 + 1) * TOK],
                         estage[:, pgb * 1024 + 512:(pgb + 1) * 1024],
                         ps[bb[3]][:], inc=sV)
            sig_free[pgb] = [tok_g]
            bank_free[bb[0]] = [tok_sig]
            bank_free[bb[1]] = [tok_sig]
            bank_free[bb[2]] = [tok_g]
            bank_free[bb[3]] = [tok_g]
            tok_g_banks[hc] = tok_g
        P.waits("T", [tok_a2] + bank_free[0] + bank_free[1])
        tok_xa2 = None
        for hc in range(h0, h1):
            P.wait("T", tok_g_banks[hc])
            st, fin = hc == h0, hc == h1 - 1
            acol = pga * 22 * R + (hc - h0) * R
            P.op("T", "matmul", ps[0][0:R, :], a2r[:, acol:acol + R],
                 g_sb[:, (hc - h0) * TOK:(hc - h0) * TOK + 512],
                 start=st, stop=fin, **MM)
            tk = P.op("T", "matmul", ps[1][0:R, :], a2r[:, acol:acol + R],
                      g_sb[:, (hc - h0) * TOK + 512:(hc - h0 + 1) * TOK],
                      start=st, stop=fin, inc=(sT if fin else None), **MM)
            if fin:
                tok_xa2 = tk
        a2r_free[pga] = tok_xa2
        P.wait("V", tok_xa2)
        if p == 0:
            P.op("V", "tensor_copy", xa2_acc[:, 0:512], ps[0][0:R, :])
            tx2 = P.op("V", "tensor_copy", xa2_acc[:, 512:1024], ps[1][0:R, :],
                       inc=sV)
        else:
            P.op("V", "tensor_add", xa2_acc[:, 0:512], xa2_acc[:, 0:512],
                 ps[0][0:R, :])
            tx2 = P.op("V", "tensor_add", xa2_acc[:, 512:1024],
                       xa2_acc[:, 512:1024], ps[1][0:R, :], inc=sV)
        bank_free[0] = [tx2]
        bank_free[1] = [tx2]
        if p == 3:
            P.op("V", "drain")
            P.op("V", "tensor_scalar_mul", xa2s_bf[:, 0:512], xa2_acc[:, 0:512],
                 2.0)
            tok_xa2s = P.op("V", "tensor_scalar_mul", xa2s_bf[:, 512:1024],
                            xa2_acc[:, 512:1024], 2.0, inc=sV)
        asrc = acc_scr[0] if p % 2 == 0 else acc_scr[1]
        adst = acc_scr[1] if p % 2 == 0 else acc_scr[0]
        P.waits("Y", tok_accs_done)
        acc_wr0 = acc_wr1 = None
        for oc in range(OC):
            s, wdst, tok_w = ring_load(w2_t[oc][:, h0 * 128:h1 * 128])
            pgx = oc % 2
            P.waits("Y", acc_rd_free[pgx])
            tok_ar = P.dma("Y", sFs[pgx], fstage[:, pgx * 1024:(pgx + 1) * 1024],
                           asrc[oc * 128:(oc + 1) * 128, :])
            b0, b1 = 2 + (oc % 2) * 2, 3 + (oc % 2) * 2
            P.waits("T", [tok_w] + bank_free[b0] + bank_free[b1])
            tok_w2 = None
            for hc in range(h0, h1):
                st = hc == h0
                fin = (hc == h1 - 1) and p != 3
                P.op("T", "matmul", ps[b0][:],
                     wdst[:, (hc - h0) * 128:(hc - h0 + 1) * 128],
                     g_sb[:, (hc - h0) * TOK:(hc - h0) * TOK + 512],
                     start=st, stop=fin, **MM)
                tk = P.op("T", "matmul", ps[b1][:],
                          wdst[:, (hc - h0) * 128:(hc - h0 + 1) * 128],
                          g_sb[:, (hc - h0) * TOK + 512:(hc - h0 + 1) * TOK],
                          start=st, stop=fin, inc=(sT if fin else None), **MM)
                if fin:
                    tok_w2 = tk
            if p == 3:
                pgc = oc % 2
                P.wait("G", b2_free[pgc])
                tok_b2 = P.dma("G", sB2[pgc], b2r[:, pgc * 128:(pgc + 1) * 128],
                               b2_t[oc])
                P.waits("T", [tok_b2, tok_xa2s])
                P.op("T", "matmul", ps[b0][:], b2r[0:R, pgc * 128:(pgc + 1) * 128],
                     xa2s_bf[:, 0:512], start=False, stop=True, **MM)
                tok_w2 = P.op("T", "matmul", ps[b1][:],
                              b2r[0:R, pgc * 128:(pgc + 1) * 128],
                              xa2s_bf[:, 512:1024], start=False, stop=True,
                              inc=sT, **MM)
                b2_free[pgc] = tok_w2
            ring_mark(s)
            P.waits("V", [tok_w2, tok_ar] + out_free[pgx])
            tv = None
            for half in range(2):
                tv = P.op("V", "tensor_add",
                          fstage[:, 2048 + pgx * 1024 + half * 512:
                                 2048 + pgx * 1024 + (half + 1) * 512],
                          ps[b0 + half][:],
                          fstage[:, pgx * 1024 + half * 512:
                                 pgx * 1024 + (half + 1) * 512], inc=sV)
            acc_rd_free[pgx] = [tv]
            bank_free[b0] = [tv]
            bank_free[b1] = [tv]
            P.wait("Y", tv)
            dst = out if p == 3 else adst
            wt = P.dma("Y", sAc[pgx], dst[oc * 128:(oc + 1) * 128, :],
                       fstage[:, 2048 + pgx * 1024:2048 + (pgx + 1) * 1024])
            out_free[pgx] = [wt]
        tok_accs_done = [(sAc[0], sAc[0].n), (sAc[1], sAc[1].n)]

    P.waits("Y", tok_accs_done)
    emit(nc, P)
    return nc


def emit(nc, P):
    with nc.Block() as blk:
        @blk.tensor
        def _(E):
            for f in P.ops["T"]:
                f(E)

        @blk.vector
        def _(E):
            for f in P.ops["V"]:
                f(E)

        @blk.scalar
        def _(E):
            for f in P.ops["S"]:
                f(E)

        @blk.gpsimd
        def _(E):
            for f in P.ops["G"]:
                f(E)

        @blk.sync
        def _(E):
            for f in P.ops["Y"]:
                f(E)


def _prep_tapes(Wq, bq, Wk, bk, Wv, bv, Wo, bo, w1, w3, w2,
                A1, B1, A3, B3, A2, B2):
    C = np.ascontiguousarray
    f = np.float32

    def img(W):
        o, d = W.shape
        ocs, dcs = o // 128, d // 128
        return C(W.reshape(ocs, 128, dcs, 128).transpose(0, 3, 2, 1)
                 .reshape(ocs, 128, dcs * 128)).astype(f)

    wq_t = img(Wq)
    wo_t = img(Wo)
    w13_t = C(np.stack([img(w1), img(w3)], axis=1))
    w2_t = img(w2)
    wk4 = Wk.reshape(KVH, 128, DC, 128).transpose(0, 3, 2, 1)
    wv4 = Wv.reshape(KVH, 128, DC, 128).transpose(0, 3, 2, 1)
    wkv_t = C(np.concatenate([wk4, wv4], axis=3).reshape(KVH, 128, DC * 256)).astype(f)
    b13_t = C(np.concatenate([B1.reshape(NHC, 128, R).transpose(0, 2, 1),
                              B3.reshape(NHC, 128, R).transpose(0, 2, 1)],
                             axis=2)).astype(f)
    b2_t = C(B2.reshape(OC, 128, R).transpose(0, 2, 1)).astype(f)
    a1_t = C(A1.reshape(R, DC, 128).transpose(2, 1, 0).reshape(128, DC * R)).astype(f)
    a3_t = C(A3.reshape(R, DC, 128).transpose(2, 1, 0).reshape(128, DC * R)).astype(f)
    a2_t = C(A2.reshape(R, NHC, 128).transpose(2, 1, 0).reshape(128, NHC * R)).astype(f)
    bq_p = C(bq.reshape(HEADS, 128).T).astype(f)
    bo_p = C(bo.reshape(OC, 128).T).astype(f)
    bkv_p = C(np.stack([bk.reshape(KVH, 128), bv.reshape(KVH, 128)], axis=1)
              .reshape(1, KVH * 256)).astype(f)
    return dict(wq_t=wq_t, wo_t=wo_t, w13_t=w13_t, w2_t=w2_t, wkv_t=wkv_t,
                b13_t=b13_t, b2_t=b2_t, a1_t=a1_t, a3_t=a3_t, a2_t=a2_t,
                bq_p=bq_p, bo_p=bo_p, bkv_p=bkv_p,
                id_p=np.eye(128, dtype=np.float32))


def kernel(L0_attn_hidden_states, L0_attn_Wq, L0_attn_bq, L0_attn_Wk, L0_attn_bk,
           L0_attn_Wv, L0_attn_bv, L0_attn_Wo, L0_attn_bo, L0_ffn_w1, L0_ffn_w3,
           L0_ffn_w2, L0_ffn_lora_A1, L0_ffn_lora_B1, L0_ffn_lora_A3,
           L0_ffn_lora_B3, L0_ffn_lora_A2, L0_ffn_lora_B2):
    x = np.ascontiguousarray(np.asarray(L0_attn_hidden_states, dtype=np.float32))
    a = lambda v: np.asarray(v, np.float32)
    tapes = _prep_tapes(a(L0_attn_Wq), a(L0_attn_bq), a(L0_attn_Wk), a(L0_attn_bk),
                        a(L0_attn_Wv), a(L0_attn_bv), a(L0_attn_Wo), a(L0_attn_bo),
                        a(L0_ffn_w1), a(L0_ffn_w3), a(L0_ffn_w2),
                        a(L0_ffn_lora_A1), a(L0_ffn_lora_B1), a(L0_ffn_lora_A3),
                        a(L0_ffn_lora_B3), a(L0_ffn_lora_A2), a(L0_ffn_lora_B2))

    nc = build_nc()
    in_maps = []
    for c in range(8):
        b, h = c // 2, c % 2
        xm = np.ascontiguousarray(x[b, h * TOK:(h + 1) * TOK, :])
        xt = np.ascontiguousarray(xm.T)
        in_maps.append(dict(x_my=xm, xT=xt, **tapes))

    res = run_bass_kernel_spmd(nc, in_maps, core_ids=list(range(8)))
    outp = np.empty((B, L, D), np.float32)
    for c in range(8):
        b, h = c // 2, c % 2
        outp[b, h * TOK:(h + 1) * TOK, :] = res.results[c]["out"].T
    return outp
